# version 12
# baseline (speedup 1.0000x reference)
import math
from collections import defaultdict
from contextlib import ExitStack

import numpy as np
import concourse.bass as bass
import concourse.mybir as mybir
from concourse.bass_utils import run_bass_kernel_spmd

F32 = mybir.dt.float32
BF16 = mybir.dt.bfloat16
U8 = mybir.dt.uint8
U32 = mybir.dt.uint32
I32 = mybir.dt.int32
AF = mybir.ActivationFunctionType
ALU = mybir.AluOpType
AX = mybir.AxisListType

ENGINES = ("sync", "scalar", "vector", "tensor", "gpsimd")

B_PER_CORE = 2
T = 2048
D = 1024
NTT = T // 128
MAGIC = 12582912.0

RIEMANN = (14.134725142, 21.022039639, 25.01085758, 30.424876126, 32.935061588,
           37.586178159, 40.918719012, 43.327073281, 48.005150881, 49.773832478,
           52.970321478, 56.446247697, 59.347044003, 60.831778525, 65.112544048,
           67.079810529, 69.546401711, 72.067157674, 75.704690699, 77.144840069,
           79.33737502, 82.910380854, 84.735492981, 87.425274613, 88.809111208,
           92.491899271, 94.651344041, 95.870634228, 98.831194218, 101.317851006,
           103.72553804, 105.446623052)

_LAYERS = [0, 1, 2, 3]
_DUMPS = []
_L3FFN_B = None


class Ctx:
    def __init__(self, nc):
        self.nc = nc
        self.sem = {}
        self.n = defaultdict(int)
        self.stack = ExitStack()
        self.dram = {}

    def get_sem(self, name):
        if name not in self.sem:
            self.sem[name] = self.stack.enter_context(self.nc.semaphore(name))
        return self.sem[name]

    def bump(self, name, n):
        self.get_sem(name)
        self.n[name] += n
        return self.n[name]

    def barrier(self):
        snap = {k: v for k, v in self.n.items() if v}
        with self.nc.Block() as B:
            for eng in ENGINES:
                def f(e, snap=snap):
                    for s, v in snap.items():
                        e.wait_ge(self.sem[s], v)
                getattr(B, eng)(f)




import inspect as _inspect

def _phname():
    f = _inspect.currentframe().f_back
    return f.f_code.co_name + ":" + str(f.f_lineno)

_PH_SEQ = [0]


class Ph:

    def __init__(self, c: Ctx, name=None):
        self.c = c
        self.q = defaultdict(list)
        self.stack = ExitStack()
        _PH_SEQ[0] += 1
        self.tag = f"p{_PH_SEQ[0]}_"
        self.name = name or self.tag

    def sb(self, name, shape, dt=F32):
        return self.stack.enter_context(self.c.nc.sbuf_tensor(self.tag + name, shape, dt))

    def ps(self, name, shape, dt=F32):
        return self.stack.enter_context(self.c.nc.psum_tensor(self.tag + name, shape, dt))

    def emit(self, eng, fn):
        self.q[eng].append(fn)

    def wait(self, eng, tok):
        if tok is None:
            return
        sem, val = tok
        self.emit(eng, lambda e, s=sem, v=val: e.wait_ge(self.c.sem[s], v))

    def fence_all(self, eng):
        snap = {k: v for k, v in self.c.n.items() if v}
        self.emit(eng, lambda e, snap=snap: [e.wait_ge(self.c.sem[s], v)
                                             for s, v in snap.items()])

    def do(self, eng, fn, inc=None, n=1):
        if inc is not None:
            v = self.c.bump(inc, n)
            self.emit(eng, lambda e, fn=fn, s=inc, n=n: fn(e).then_inc(self.c.sem[s], n))
            return (inc, v)
        self.emit(eng, fn)
        return None

    def dma(self, q, out, in_, **kw):
        return self.do(q, lambda e, o=out, i=in_, kw=kw: e.dma_start(out=o, in_=i, **kw),
                       inc="d" + q[0], n=16)

    def close(self):
        with self.c.nc.named_scope(self.name):
            with self.c.nc.Block() as B:
                for eng in ENGINES:
                    if self.q[eng]:
                        getattr(B, eng)(lambda e, fns=list(self.q[eng]): [f(e) for f in fns])
        self.stack.close()

    def __enter__(self):
        return self

    def __exit__(self, *a):
        self.close()
        return False



def ln_tile(p, x_f32, xn_out_bf, tmp, stat, eng="vector"):
    raise NotImplementedError


def emit_transpose(p, dst_sb, src_sb, pst, tok_in, sem_t="st", sem_v="sv"):
    ident = p._ident
    t1 = p.do("tensor", lambda t, o=pst, i=src_sb, idn=ident: t.transpose(o[:], i, idn[:]),
              inc=sem_t)
    p.wait("vector", t1)
    t2 = p.do("vector", lambda v, o=dst_sb, i=pst: v.tensor_copy(out=o, in_=i[:]), inc=sem_v)
    return t2



def build_program(layers=None, dumps=None):
    layers = _LAYERS if layers is None else layers
    dumps = _DUMPS if dumps is None else dumps
    nc = bass.Bass("TRN2", target_bir_lowering=False)
    c = Ctx(nc)

    dt_in = {}
    def din(name, shape, dt=F32):
        ap = nc.dram_tensor(name, list(shape), dt, kind="ExternalInput").ap()
        c.dram[name] = ap
        return ap

    x_in = din("L0_attn_x", (B_PER_CORE, T, D))
    din("L0_attn_w_qkv", (D, 3 * D))
    din("L0_attn_w_o", (D, D))
    din("L0_attn_log_scale", (1, 16))
    din("L0_ffn_w_gate", (2816, D))
    din("L0_ffn_w_up", (2816, D))
    din("L0_ffn_w_down", (D, 2816))
    for s in ("f", "b"):
        din(f"L1_attn_inproj_{s}", (D, 4096))
        din(f"L1_attn_conv_w_{s}", (2048, 4))
        din(f"L1_attn_conv_b_{s}", (1, 2048))
        din(f"L1_attn_xproj_{s}", (2048, 96))
        din(f"L1_attn_dt_w_{s}", (64, 2048))
        din(f"L1_attn_dt_b_{s}", (1, 2048))
        din(f"L1_attn_Alog_{s}", (2048, 16))
        din(f"L1_attn_Dp_{s}", (1, 2048))
        din(f"L1_attn_outproj_{s}", (2048, D))
    din("L1_attn_gate_w", (2 * D, D))
    din("L1_attn_gate_b", (1, D))
    din("L1_attn_proj_w", (D, D))
    din("L1_attn_proj_b", (1, D))
    din("L1_ffn_ln_scale", (1, D))
    din("L1_ffn_ln_bias", (1, D))
    din("L1_ffn_gate_wT", (D, 8))
    din("L1_ffn_Wg", (8, D, 2048))
    din("L1_ffn_Wu", (8, D, 2048))
    din("L1_ffn_Wd", (8, 2048, D))
    din("L1_ffn_sWg", (D, 2048))
    din("L1_ffn_sWu", (D, 2048))
    din("L1_ffn_sWd", (2048, D))
    din("L2_attn_Wq", (D, D))
    din("L2_attn_Wk", (D, D))
    din("L2_attn_Wv", (D, D))
    din("L2_attn_Wo", (D, D))
    din("L2_ffn_ln_gamma", (1, D))
    din("L2_ffn_ln_beta", (1, D))
    din("L2_ffn_fc1_q", (D, 4096), I32)
    din("L2_ffn_fc1_scales", (8, 4096))
    din("L2_ffn_fc1_bias", (1, 4096))
    din("L2_ffn_fc2_q", (4096, D), I32)
    din("L2_ffn_fc2_scales", (32, D))
    din("L2_ffn_fc2_bias", (1, D))
    din("L3_attn_w_qkvT", (D, 3 * D))
    din("L3_attn_b_qkv", (1, 3 * D))
    din("L3_attn_w_projT", (D, D))
    din("L3_attn_b_proj", (1, D))
    din("L3_ffn_WgT", (D, 8))
    din("L3_ffn_W1", (8, D, 4096))
    din("L3_ffn_b1", (8, 4096))
    din("L3_ffn_W2", (8, 4096, D))
    din("L3_ffn_b2", (8, D))
    din("c_pos", (1, T))
    din("c_invfreq", (1, 512))

    y_out = nc.dram_tensor("y", [B_PER_CORE, T, D], F32, kind="ExternalOutput").ap()
    c.dram["y"] = y_out

    def scr(name, shape, dt=F32):
        ap = nc.dram_tensor(name, list(shape), dt, kind="Internal").ap()
        c.dram[name] = ap
        return ap

    xres = scr("xres", (B_PER_CORE, T, D))
    xn_tok = scr("xn_tok", (B_PER_CORE, T, D), BF16)
    xnT = scr("xnT", (B_PER_CORE, D, T), BF16)
    s_buf = scr("s_buf", (B_PER_CORE, T, D))
    g_buf = scr("g_buf", (B_PER_CORE, T, D))
    xqT = scr("xqT", (B_PER_CORE, D, T), BF16)
    wgqT = scr("wgqT", (D, 2816), BF16)
    scr("dbg_qt", (128, 8 * 2048), BF16)
    scr("dbg_kt", (128, 8 * 2048), BF16)
    scr("dbg_v", (128, 16 * 1024), BF16)
    scr("dbg_aoT", (128, 8 * 2048), BF16)
    scr("dbg_probs", (128, 2048), BF16)
    scr("wobf", (D, D), BF16)
    scr("l1_uT", (2048, T), BF16)
    scr("l1_zsT", (2048, T), BF16)
    scr("l1_ucT", (2048, T), BF16)
    scr("l1_dT", (2048, T))
    scr("l1_bc", (32, T))
    scr("l1_ygT", (2048, T), BF16)
    scr("l1_yfT", (D, T), BF16)
    scr("l1_moew", (B_PER_CORE, T, 8))
    scr("l2_aoT", (D, T), BF16)
    scr("l2_x2T", (D, T), BF16)
    scr("l2_hT", (4096, T), BF16)
    scr("l2_w1", (D, 4096), BF16)
    scr("l2_w2", (4096, D), BF16)
    scr("l3_aoT", (D, T), BF16)
    scr("l3_p", (B_PER_CORE, T, 8))
    scr("l3_pT", (16, T))
    scr("l3_slot", (T, 16))
    scr("l3_Xg", (513, D), BF16)
    scr("l3_G", (513, 1))
    scr("l3_tokid", (513, 1), U32)
    scr("l3_zero", (T, D))
    scr("l3_bufs", (8, T, D))
    scr("l1_dbg_pr", (B_PER_CORE, T, 8))
    scr("l1_dbg_mx", (B_PER_CORE, T, 8))
    scr("l1_dbg_lg", (B_PER_CORE, T, 8))
    scr("l1_ybT", (D, T), BF16)
    wuqT = scr("wuqT", (D, 2816), BF16)
    wdqT = scr("wdqT", (2816, D), BF16)

    with c.stack:
        ident = c.stack.enter_context(nc.sbuf_tensor("g_ident", [128, 128], BF16))
        with Ph(c, name=_phname()) as p:
            p.do("gpsimd", lambda g: g.memset(ident[:], 0.0))
            p.do("gpsimd", lambda g: g.affine_select(
                out=ident[:], in_=ident[:], compare_op=ALU.not_equal, fill=1.0,
                base=0, pattern=[[-1, 128]], channel_multiplier=1), inc="sg")
        c.barrier()

        for l in layers:
            ph_ln(c, l, ident)
            c.barrier()
            if l == 0:
                ph_l0_quant_weights(c, ident)
                c.barrier()
                ph_l0_attn(c, ident)
                c.barrier()
                ph_l0_ffn(c, ident)
                c.barrier()
            elif l == 1:
                ph_l1_attn(c, ident)
                c.barrier()
                ph_l1_ffn(c, ident)
                c.barrier()
            elif l == 2:
                ph_l2_attn(c, ident)
                c.barrier()
                ph_l2_ffn(c, ident)
                c.barrier()
            elif l == 3:
                ph_l3_attn(c, ident)
                c.barrier()
                ph_l3_ffn(c, ident)
                c.barrier()
            ph_combine(c, l)
            c.barrier()

        for name in dumps:
            src = c.dram[name]
            dmp = nc.dram_tensor("dump_" + name, list(src.shape), src.dtype,
                                 kind="ExternalOutput").ap()
            with Ph(c, name=_phname()) as p:
                p.dma("sync", dmp[:], src[:])
            c.barrier()

        with Ph(c, name=_phname()) as p:
            snap = {k: v for k, v in c.n.items() if v}
            for s, v in snap.items():
                p.wait("sync", (s, v))
    return nc



def ph_ln(c, l, ident):
    nc = c.nc
    xsrc = c.dram["L0_attn_x"] if l == 0 else c.dram["xres"]
    with Ph(c, name=_phname()) as p:
        p._ident = ident
        xt = p.sb("ln_x", [128, D], F32)
        xm = p.sb("ln_xm", [128, D], F32)
        sq = p.sb("ln_sq", [128, D], F32)
        xnf = p.sb("ln_xnf", [128, D], F32)
        xnb = p.sb("ln_xnb", [128, D], BF16)
        st = p.sb("ln_stat", [128, 8], F32)
        pst = p.ps("ln_pst", [128, 128], BF16)
        tb8 = p.sb("ln_tb8", [128, 8 * 128], BF16)
        q_absmax = p.sb("ln_qam", [128, 64], F32)
        q_rinv = p.sb("ln_qrinv", [128, 64], F32)
        q_v = p.sb("ln_qv", [128, D], F32)
        q_t = p.sb("ln_qt", [128, D], F32)
        q_u = p.sb("ln_qu", [128, D], F32)
        q_w = p.sb("ln_qw", [128, D], F32)
        q_m = p.sb("ln_qm", [128, D], U8)
        q_out = p.sb("ln_qout", [128, D], BF16)

        for b in range(B_PER_CORE):
            for tt in range(NTT):
                tok = ln_body(p, c, l, b, tt, xsrc, xt, xm, sq, xnf, xnb, st,
                              pst, tb8, q_absmax, q_rinv, q_v, q_t, q_u, q_w, q_m, q_out)


def ln_body(p, c, l, b, tt, xsrc, xt, xm, sq, xnf, xnb, st, pst, tb8,
            q_absmax, q_rinv, q_v, q_t, q_u, q_w, q_m, q_out):
    nc = c.nc
    r0 = tt * 128
    p.fence_all("sync")
    ld = p.dma("sync", xt[:], xsrc[b, r0:r0 + 128, :])
    p.wait("vector", ld)
    rs = p.do("vector", lambda v: v.reduce_sum(st[:, 0:1], xt[:], axis=AX.X), inc="sv")
    p.wait("vector", rs)
    nm = p.do("vector", lambda v: v.tensor_scalar(
        out=st[:, 1:2], in0=st[:, 0:1], scalar1=-1.0 / D, scalar2=None, op0=ALU.mult),
        inc="sv")
    p.wait("scalar", nm)
    a1 = p.do("scalar", lambda s: s.activation(out=xm[:], in_=xt[:], func=AF.Identity,
                                               bias=st[:, 1:2], scale=1.0), inc="ss")
    a2 = p.do("scalar", lambda s: s.activation(out=sq[:], in_=xm[:], func=AF.Square,
                                               accum_out=st[:, 2:3]), inc="ss")
    p.wait("vector", a2)
    sv1 = p.do("vector", lambda v: v.tensor_scalar(out=st[:, 3:4], in0=st[:, 2:3],
                                                   scalar1=1.0 / D, scalar2=1e-5,
                                                   op0=ALU.mult, op1=ALU.add), inc="sv")
    p.wait("scalar", sv1)
    a3 = p.do("scalar", lambda s: s.activation(out=st[:, 4:5], in_=st[:, 3:4], func=AF.Sqrt),
              inc="ss")
    p.wait("vector", a3)
    rc = p.do("vector", lambda v: v.reciprocal(st[:, 5:6], st[:, 4:5]), inc="sv")
    p.wait("vector", rc)
    p.wait("vector", a2)
    a4 = p.do("vector", lambda v: v.tensor_tensor(
        out=xnf[:], in0=xm[:], in1=st[:, 5:6].to_broadcast([128, D]), op=ALU.mult),
        inc="sv")
    a5 = p.do("vector", lambda v: v.tensor_copy(out=xnb[:], in_=xnf[:]), inc="sv")
    p.wait("sync", a5)
    p.dma("sync", c.dram["xn_tok"][b, r0:r0 + 128, :], xnb[:])
    p.wait("tensor", a5)
    cv = None
    for k in range(8):
        tr = p.do("tensor", lambda t, k=k: t.transpose(pst[:], xnb[:, k * 128:(k + 1) * 128],
                                                       p._ident[:]), inc="st")
        p.wait("vector", tr)
        cv = p.do("vector", lambda v, k=k: v.tensor_copy(out=tb8[:, k * 128:(k + 1) * 128],
                                                         in_=pst[:]), inc="sv")
        p.wait("tensor", cv)
    p.wait("sync", cv)
    sd = p.dma("sync", c.dram["xnT"][b, :, r0:r0 + 128].rearrange("(k p) t -> p k t", p=128),
               tb8[:].rearrange("p (k t) -> p k t", k=8))
    p.wait("vector", sd)

    if l == 0:
        p.wait("vector", a4)
        qt = fp4_quant(p, c, q_out, xnf, q_absmax, q_rinv, q_v, q_t, q_u, q_w, q_m, D)
        p.wait("tensor", qt)
        cv = None
        for k in range(8):
            tr = p.do("tensor", lambda t, k=k: t.transpose(pst[:], q_out[:, k * 128:(k + 1) * 128],
                                                           p._ident[:]), inc="st")
            p.wait("vector", tr)
            cv = p.do("vector", lambda v, k=k: v.tensor_copy(out=tb8[:, k * 128:(k + 1) * 128],
                                                             in_=pst[:]), inc="sv")
            p.wait("tensor", cv)
        p.wait("sync", cv)
        sd = p.dma("sync", c.dram["xqT"][b, :, r0:r0 + 128].rearrange("(k p) t -> p k t", p=128),
                   tb8[:].rearrange("p (k t) -> p k t", k=8))
        p.wait("vector", sd)
    return None


def fp4_quant(p, c, out_bf, src_f32, am, rinv, v_, t_, u_, w_, m_, width):
    G = width // 16
    r1 = p.do("vector", lambda v: v.tensor_reduce(
        am[:, 0:G].rearrange("p (g k) -> p g k", k=1),
        src_f32.rearrange("p (g k) -> p g k", k=16),
        axis=AX.X, op=ALU.max, apply_absolute_value=True), inc="sv")
    p.wait("vector", r1)
    r2 = p.do("vector", lambda v: v.tensor_scalar(out=am[:, 0:G], in0=am[:, 0:G],
                                                  scalar1=6e-12, scalar2=None, op0=ALU.max),
              inc="sv")
    p.wait("vector", r2)
    r3 = p.do("vector", lambda v: v.reciprocal(rinv[:, 0:G], am[:, 0:G]), inc="sv")
    p.wait("vector", r3)
    r4 = p.do("vector", lambda v: v.tensor_scalar(out=rinv[:, 0:G], in0=rinv[:, 0:G],
                                                  scalar1=6.0, scalar2=None, op0=ALU.mult),
              inc="sv")
    p.wait("vector", r4)

    def mul_bcast(v):
        a = src_f32.rearrange("p (g k) -> p g k", k=16)
        bmat = rinv[:, 0:G].rearrange("p (g k) -> p g k", k=1).to_broadcast([128, G, 16])
        return v.tensor_tensor(out=v_[:, 0:width].rearrange("p (g k) -> p g k", k=16),
                               in0=a, in1=bmat, op=ALU.mult)
    r5 = p.do("vector", mul_bcast, inc="sv")
    p.wait("vector", r5)
    p.do("vector", lambda v: v.tensor_scalar(out=t_[:, 0:width], in0=v_[:, 0:width], scalar1=2.0,
                                             scalar2=MAGIC, op0=ALU.mult, op1=ALU.add))
    p.do("vector", lambda v: v.tensor_scalar(out=u_[:, 0:width], in0=t_[:, 0:width], scalar1=MAGIC,
                                             scalar2=0.5, op0=ALU.subtract, op1=ALU.mult))
    C = float(2 ** 22 + 1)
    p.do("vector", lambda v: v.tensor_scalar(out=t_[:, 0:width], in0=v_[:, 0:width], scalar1=C,
                                             scalar2=None, op0=ALU.mult))
    p.do("vector", lambda v: v.tensor_tensor(out=w_[:, 0:width], in0=t_[:, 0:width], in1=v_[:, 0:width], op=ALU.subtract))
    p.do("vector", lambda v: v.tensor_tensor(out=w_[:, 0:width], in0=t_[:, 0:width], in1=w_[:, 0:width], op=ALU.subtract))
    p.do("vector", lambda v: v.tensor_scalar(out=w_[:, 0:width], in0=w_[:, 0:width], scalar1=6.0,
                                             scalar2=-6.0, op0=ALU.min, op1=ALU.max))
    p.do("vector", lambda v: v.tensor_scalar(out=t_[:, 0:width], in0=v_[:, 0:width], scalar1=-2.0,
                                             scalar2=2.0, op0=ALU.max, op1=ALU.min))
    mk = p.do("vector", lambda v: v.tensor_tensor(out=m_[:, 0:width], in0=v_[:, 0:width], in1=t_[:, 0:width],
                                                  op=ALU.is_equal), inc="sv")
    p.wait("vector", mk)
    se = p.do("vector", lambda v: v.select(v_[:, 0:width], m_[:, 0:width], u_[:, 0:width], w_[:, 0:width]), inc="sv")
    r6 = p.do("vector", lambda v: v.tensor_scalar(out=am[:, 0:G], in0=am[:, 0:G],
                                                  scalar1=1.0 / 6.0, scalar2=None,
                                                  op0=ALU.mult), inc="sv")
    p.wait("vector", se)
    p.wait("vector", r6)

    def mul_deq(v):
        a = v_[:, 0:width].rearrange("p (g k) -> p g k", k=16)
        bmat = am[:, 0:G].rearrange("p (g k) -> p g k", k=1).to_broadcast([128, G, 16])
        return v.tensor_tensor(out=out_bf.rearrange("p (g k) -> p g k", k=16),
                               in0=a, in1=bmat, op=ALU.mult)
    tok = p.do("vector", mul_deq, inc="sv")
    return tok


TWO_PI = 2.0 * math.pi


def emit_rangered_sin(p, c, theta_ps, out_sb, tmp1, tmp2, shift, width):
    p.do("vector", lambda v: v.tensor_scalar(out=tmp1[:, 0:width], in0=theta_ps,
                                             scalar1=1.0 / TWO_PI, scalar2=shift,
                                             op0=ALU.mult, op1=ALU.add))
    p.do("vector", lambda v: v.tensor_scalar(out=tmp2[:, 0:width], in0=tmp1[:, 0:width],
                                             scalar1=MAGIC, scalar2=MAGIC,
                                             op0=ALU.add, op1=ALU.subtract))
    p.do("vector", lambda v: v.tensor_tensor(out=tmp1[:, 0:width], in0=tmp1[:, 0:width],
                                             in1=tmp2[:, 0:width], op=ALU.subtract))
    t = p.do("vector", lambda v: v.tensor_scalar(out=tmp2[:, 0:width], in0=tmp1[:, 0:width],
                                                 scalar1=TWO_PI, scalar2=None,
                                                 op0=ALU.mult), inc="sv")
    p.wait("scalar", t)
    return p.do("scalar", lambda s: s.activation(out=out_sb, in_=tmp2[:, 0:width],
                                                 func=AF.Sin), inc="ss")


def ph_l0_quant_weights(c, ident):
    with Ph(c, name=_phname()) as p:
        p._ident = ident
        wt = p.sb("qw_x", [128, 2816], F32)
        am = p.sb("qw_am", [128, 2816 // 16], F32)
        ri = p.sb("qw_ri", [128, 2816 // 16], F32)
        v_ = p.sb("qw_v", [128, 2816], F32)
        t_ = p.sb("qw_t", [128, 2816], F32)
        u_ = p.sb("qw_u", [128, 2816], F32)
        w_ = p.sb("qw_w", [128, 2816], F32)
        m_ = p.sb("qw_m", [128, 2816], U8)
        qo = p.sb("qw_qo", [128, 2816], BF16)
        pst = p.ps("qw_pst", [128, 128], BF16)
        tb = p.sb("qw_tb", [128, 128], BF16)

        def one(src_name, dst_name, nrows, ncols):
            nrt = nrows // 128
            nct = ncols // 128
            for rt in range(nrt):
                p.fence_all("sync")
                ld = p.dma("sync", wt[:, 0:ncols], c.dram[src_name][rt * 128:(rt + 1) * 128, :])
                p.wait("vector", ld)
                qt = fp4_quant(p, c, qo[:, 0:ncols], wt[:, 0:ncols], am, ri, v_, t_, u_, w_, m_, ncols)
                p.wait("tensor", qt)
                for k in range(nct):
                    tr = p.do("tensor", lambda t, k=k: t.transpose(
                        pst[:], qo[:, k * 128:(k + 1) * 128], p._ident[:]), inc="st")
                    p.wait("vector", tr)
                    cv = p.do("vector", lambda v: v.tensor_copy(out=tb[:], in_=pst[:]), inc="sv")
                    p.wait("sync", cv)
                    sd = p.dma("sync", c.dram[dst_name][k * 128:(k + 1) * 128,
                                                        rt * 128:(rt + 1) * 128], tb[:])
                    p.wait("tensor", cv)
                    p.wait("vector", sd)

        one("L0_ffn_w_gate", "wgqT", 2816, D)
        one("L0_ffn_w_up", "wuqT", 2816, D)
        one("L0_ffn_w_down", "wdqT", D, 2816)
        for rt in range(8):
            p.fence_all("sync")
            ld = p.dma("sync", wt[:, 0:D], c.dram["L0_attn_w_o"][rt * 128:(rt + 1) * 128, :])
            p.wait("vector", ld)
            cb = p.do("vector", lambda v: v.tensor_copy(out=qo[:, 0:D], in_=wt[:, 0:D]),
                      inc="sv")
            p.wait("sync", cb)
            sd = p.dma("sync", c.dram["wobf"][rt * 128:(rt + 1) * 128, :], qo[:, 0:D])
            p.wait("vector", sd)


def ph_l0_attn(c, ident):
    nc = c.nc
    NH = 16
    with ExitStack() as outer:
        oe = outer.enter_context
        wq_sb = oe(nc.sbuf_tensor("a0_wqkv", [128, 8 * 3072], BF16))
        qt_sb = oe(nc.sbuf_tensor("a0_qt", [128, 8 * 2048], BF16))
        kt_sb = oe(nc.sbuf_tensor("a0_kt", [128, 8 * 2048], BF16))
        v_sb = oe(nc.sbuf_tensor("a0_v", [128, 16 * 1024], BF16))
        aoT_sb = oe(nc.sbuf_tensor("a0_aoT", [128, 8 * 2048], BF16))
        throw = oe(nc.sbuf_tensor("a0_throw", [1, 512], F32))
        cmask = oe(nc.sbuf_tensor("a0_cmask", [128, 128], F32))
        p = Ph(c, name=_phname())
        p._ident = ident
        wo_sb = p.sb("a0_wo", [128, 8 * 512], BF16)
        wld = p.sb("a0_wld", [128, 1024], F32)
        xnt_sb = p.sb("a0_xnt", [128, 8 * 128], BF16)
        ifr = p.sb("a0_ifr", [1, 512], F32)
        lsc = p.sb("a0_lsc", [1, 16], F32)
        sc_r = p.sb("a0_scr", [1, 16], F32)
        sin_sb = p.sb("a0_sin", [128, 512], BF16)
        cos_sb = p.sb("a0_cos", [128, 512], BF16)
        tmp1 = p.sb("a0_tmp1", [128, 512], F32)
        tmp2 = p.sb("a0_tmp2", [128, 512], F32)
        rop = p.sb("a0_rop", [128, 1024], BF16)
        pos_t = p.sb("a0_pos", [1, 128], F32)
        ps_th = p.ps("a0_ps_th", [128, 512], F32)
        ps_qkv = [p.ps(f"a0_ps_qkv{i}", [128, 512], F32) for i in range(6)]
        ps_tp = p.ps("a0_ps_tp", [128, 128], BF16)

        p.do("gpsimd", lambda g: g.memset(cmask[:], 0.0))
        gm = p.do("gpsimd", lambda g: g.affine_select(
            out=cmask[:], in_=cmask[:], compare_op=ALU.is_ge, fill=-1e30,
            base=0, pattern=[[-1, 128]], channel_multiplier=1), inc="sg")
        p.wait("vector", gm)

        for kb in range(8):
            for ch in range(3):
                ld = p.dma("sync", wld[:],
                           c.dram["L0_attn_w_qkv"][kb * 128:(kb + 1) * 128,
                                                   ch * 1024:(ch + 1) * 1024])
                p.wait("vector", ld)
                cv = p.do("vector", lambda v, kb=kb, ch=ch: v.tensor_copy(
                    out=wq_sb[:, kb * 3072 + ch * 1024: kb * 3072 + (ch + 1) * 1024],
                    in_=wld[:]), inc="sv")
                p.wait("sync", cv)
        l1 = p.dma("sync", lsc[:], c.dram["L0_attn_log_scale"][:])
        l2 = p.dma("sync", ifr[:], c.dram["c_invfreq"][:])
        p.wait("scalar", l1)
        e1 = p.do("scalar", lambda s: s.activation(out=sc_r[:], in_=lsc[:], func=AF.Exp),
                  inc="ss")
        p.wait("vector", e1)
        p.wait("vector", l2)
        ttok = None
        for h in range(16):
            ttok = p.do("vector", lambda v, h=h: v.tensor_tensor(
                out=throw[0:1, h * 32:(h + 1) * 32], in0=ifr[0:1, h * 32:(h + 1) * 32],
                in1=sc_r[0:1, h:h + 1].to_broadcast([1, 32]), op=ALU.mult), inc="sv")
        p.wait("tensor", ttok)

        for b in range(B_PER_CORE):
            prev_mm = None
            prev_vv = None
            for tt in range(NTT):
                r0 = tt * 128
                p.fence_all("gpsimd")
                pl = p.dma("gpsimd", pos_t[:], c.dram["c_pos"][0:1, r0:r0 + 128])
                p.wait("tensor", pl)
                th = p.do("tensor", lambda t: t.matmul(
                    ps_th[:], pos_t[0:1, :], throw[0:1, :],
                    start=True, stop=True), inc="st")
                p.wait("vector", th)
                s_sin = emit_rangered_sin(p, c, ps_th[:], sin_sb[:], tmp1, tmp2, 0.0, 512)
                p.wait("vector", s_sin)
                s_cos = emit_rangered_sin(p, c, ps_th[:], cos_sb[:], tmp1, tmp2, 0.25, 512)
                p.wait("sync", prev_mm)
                lds = p.dma("sync", xnt_sb[:].rearrange("p (k t) -> p k t", k=8),
                            c.dram["xnT"][b, :, r0:r0 + 128]
                            .rearrange("(k p) t -> p k t", p=128))
                p.wait("tensor", lds)
                p.wait("tensor", s_cos)
                p.wait("tensor", prev_vv)
                mm = None
                for kb in range(8):
                    for sl in range(6):
                        mm = p.do("tensor", lambda t, kb=kb, sl=sl: t.matmul(
                            ps_qkv[sl][:], xnt_sb[:, kb * 128:(kb + 1) * 128],
                            wq_sb[:, kb * 3072 + sl * 512: kb * 3072 + (sl + 1) * 512],
                            start=(kb == 0), stop=(kb == 7)),
                            inc="st" if (kb == 7 and sl == 5) else None)
                prev_mm = mm
                p.wait("vector", mm)
                p.wait("vector", s_cos)
                p.wait("vector", s_sin)
                for qk in range(2):
                    rlast = None
                    for sl2 in range(2):
                        sl = qk * 2 + sl2
                        pss = ps_qkv[sl]
                        hw = 256
                        cbase = sl2 * 256
                        x1 = pss[:].rearrange("p (n two) -> p n two", two=2)
                        cosh = cos_sb[:, cbase:cbase + hw]
                        sinh = sin_sb[:, cbase:cbase + hw]
                        ro = rop[:].rearrange("p (n two) -> p n two", two=2)
                        t3 = tmp1[:, 0:hw].rearrange("p (n k) -> p n k", k=1)
                        t4 = tmp2[:, 0:hw].rearrange("p (n k) -> p n k", k=1)
                        p.do("vector", lambda v, x1=x1, cosh=cosh, t3=t3: v.tensor_tensor(
                            out=t3, in0=x1[:, :, 0:1], in1=cosh, op=ALU.mult))
                        p.do("vector", lambda v, x1=x1, sinh=sinh, t4=t4: v.tensor_tensor(
                            out=t4, in0=x1[:, :, 1:2], in1=sinh, op=ALU.mult))
                        p.do("vector", lambda v, ro=ro, sl2=sl2, t3=t3, t4=t4: v.tensor_tensor(
                            out=ro[:, sl2 * 256:(sl2 + 1) * 256, 0:1], in0=t3,
                            in1=t4, op=ALU.subtract))
                        p.do("vector", lambda v, x1=x1, sinh=sinh, t3=t3: v.tensor_tensor(
                            out=t3, in0=x1[:, :, 0:1], in1=sinh, op=ALU.mult))
                        p.do("vector", lambda v, x1=x1, cosh=cosh, t4=t4: v.tensor_tensor(
                            out=t4, in0=x1[:, :, 1:2], in1=cosh, op=ALU.mult))
                        rlast = p.do("vector", lambda v, ro=ro, sl2=sl2, t3=t3, t4=t4:
                                     v.tensor_tensor(
                            out=ro[:, sl2 * 256:(sl2 + 1) * 256, 1:2], in0=t3,
                            in1=t4, op=ALU.add), inc="sv")
                    p.wait("tensor", rlast)
                    dst = qt_sb if qk == 0 else kt_sb
                    cv = None
                    for k in range(8):
                        tr = p.do("tensor", lambda t, k=k: t.transpose(
                            ps_tp[:], rop[:, k * 128:(k + 1) * 128], p._ident[:]), inc="st")
                        p.wait("vector", tr)
                        cv = p.do("vector", lambda v, k=k, dst=dst, r0=r0: v.tensor_copy(
                            out=dst[:, k * 2048 + r0: k * 2048 + r0 + 128], in_=ps_tp[:]),
                            inc="sv")
                        p.wait("tensor", cv)
                    p.wait("vector", cv)
                vv = None
                for sl2 in range(2):
                    vv = p.do("vector", lambda v, sl2=sl2, tt=tt: v.tensor_copy(
                        out=v_sb[:, tt * 1024 + sl2 * 512: tt * 1024 + (sl2 + 1) * 512],
                        in_=ps_qkv[4 + sl2][:]), inc="sv")
                prev_vv = vv

            p.close()
            c.barrier()
            if b == 0 and getattr(c, "_dbg_l0", False):
                with Ph(c) as pd:
                    pd.dma("sync", c.dram["dbg_qt"][:], qt_sb[:])
                    pd.dma("sync", c.dram["dbg_kt"][:], kt_sb[:])
                    pd.dma("sync", c.dram["dbg_v"][:], v_sb[:])
                c.barrier()
            p = Ph(c, name=_phname())
            p._ident = ident
            probs = p.sb("a0_probs", [128, 2048], BF16)
            probsT = p.sb("a0_probsT", [128, 128], BF16)
            ao = p.sb("a0_ao", [128, 64], BF16)
            aopad = p.sb("a0_aopad", [128, 128], BF16)
            stat = p.sb("a0_stat", [128, 4], F32)
            ps_sc = p.ps("a0_ps_sc", [128, 2048], F32)
            ps_tp = p.ps("a0_ps_tp", [128, 128], BF16)
            ps_ao = p.ps("a0_ps_ao", [128, 64], F32)
            prev_ex = None
            prev_ev = None
            prev_trlast = None
            prev_cv2 = None
            for h in range(NH):
                pb = (h % 2) * 64
                jb = h // 2
                for qi in range(NTT):
                    q0 = qi * 128
                    nk = (qi + 1) * 128
                    p.wait("tensor", prev_ex)
                    mm = None
                    nchunk = (nk + 511) // 512
                    for ch in range(nchunk):
                        w = min(512, nk - ch * 512)
                        mm = p.do("tensor", lambda t, ch=ch, w=w, pb=pb, jb=jb, q0=q0: t.matmul(
                            ps_sc[:, ch * 512: ch * 512 + w],
                            qt_sb[pb:pb + 64, jb * 2048 + q0: jb * 2048 + q0 + 128],
                            kt_sb[pb:pb + 64, jb * 2048 + ch * 512: jb * 2048 + ch * 512 + w],
                            start=True, stop=True),
                            inc="st" if ch == nchunk - 1 else None)
                    p.wait("vector", mm)
                    msk = p.do("vector", lambda v, q0=q0: v.tensor_tensor(
                        out=ps_sc[:, q0:q0 + 128], in0=ps_sc[:, q0:q0 + 128],
                        in1=cmask[:], op=ALU.add), inc="sv")
                    p.wait("vector", msk)
                    r1 = p.do("vector", lambda v, nk=nk: v.reduce_max(
                        stat[:, 0:1], ps_sc[:, 0:nk], axis=AX.X, negate=True), inc="sv")
                    p.wait("vector", r1)
                    r2 = p.do("vector", lambda v: v.tensor_scalar(
                        out=stat[:, 1:2], in0=stat[:, 0:1], scalar1=0.125, scalar2=None,
                        op0=ALU.mult), inc="sv")
                    p.wait("scalar", r2)
                    p.wait("scalar", prev_trlast)
                    ex = p.do("scalar", lambda s, nk=nk: s.activation(
                        out=probs[:, 0:nk], in_=ps_sc[:, 0:nk], func=AF.Exp,
                        bias=stat[:, 1:2], scale=0.125, accum_out=stat[:, 2:3]), inc="ss")
                    prev_ex = ex
                    p.wait("vector", ex)
                    r3 = p.do("vector", lambda v: v.reciprocal(stat[:, 3:4], stat[:, 2:3]),
                              inc="sv")
                    p.wait("tensor", ex)
                    p.wait("tensor", prev_ev)
                    amm = None
                    trlast = None
                    prev_amm = None
                    for j in range(qi + 1):
                        p.wait("tensor", prev_cv2)
                        tr = p.do("tensor", lambda t, j=j: t.transpose(
                            ps_tp[:], probs[:, j * 128:(j + 1) * 128], p._ident[:]), inc="st")
                        trlast = tr
                        prev_cv2 = None
                        p.wait("vector", tr)
                        p.wait("vector", prev_amm)
                        cv = p.do("vector", lambda v: v.tensor_copy(out=probsT[:], in_=ps_tp[:]),
                                  inc="sv")
                        p.wait("tensor", cv)
                        amm = p.do("tensor", lambda t, j=j, qi=qi, h=h: t.matmul(
                            ps_ao[:], probsT[:],
                            v_sb[:, j * 1024 + h * 64: j * 1024 + h * 64 + 64],
                            start=(j == 0), stop=(j == qi)), inc="st")
                        prev_amm = amm
                    prev_trlast = trlast
                    p.wait("vector", r3)
                    p.wait("vector", amm)
                    ev = p.do("vector", lambda v: v.tensor_tensor(
                        out=ao[:], in0=ps_ao[:], in1=stat[:, 3:4].to_broadcast([128, 64]),
                        op=ALU.mult), inc="sv")
                    prev_ev = ev
                    p.wait("vector", ev)
                    p.do("vector", lambda v, pb=pb: v.memset(aopad[:, 64 - (64 - pb) % 128 if False else (64 if pb == 0 else 0):(128 if pb == 0 else 64)], 0.0))
                    cp2 = p.do("vector", lambda v, pb=pb: v.tensor_copy(
                        out=aopad[:, pb:pb + 64], in_=ao[:]), inc="sv")
                    p.wait("tensor", cp2)
                    tr2 = p.do("tensor", lambda t: t.transpose(ps_tp[:], aopad[:], p._ident[:]),
                               inc="st")
                    p.wait("vector", tr2)
                    cv2 = p.do("vector", lambda v, pb=pb, jb=jb, q0=q0: v.tensor_copy(
                        out=aoT_sb[pb:pb + 64, jb * 2048 + q0: jb * 2048 + q0 + 128],
                        in_=ps_tp[pb:pb + 64, :]), inc="sv")
                    prev_cv2 = cv2
                    p.wait("tensor", cv2)

            p.close()
            c.barrier()
            if b == 0 and getattr(c, "_dbg_l0", False):
                with Ph(c) as pd:
                    pd.dma("sync", c.dram["dbg_aoT"][:], aoT_sb[:])
                c.barrier()
            p = Ph(c, name=_phname())
            p._ident = ident
            wo_sb = p.sb("a0_wo", [128, 8 * 512], BF16)
            tmp1 = p.sb("a0_tmp1", [128, 512], F32)
            ps_s = p.ps("a0_ps_s", [128, 512], F32)
            for half in range(2):
                p.fence_all("gpsimd")
                wl = p.dma("gpsimd", wo_sb[:].rearrange("p (k t) -> p k t", k=8),
                           c.dram["wobf"][:, half * 512:(half + 1) * 512]
                           .rearrange("(k p) t -> p k t", p=128))
                p.wait("tensor", wl)
                for tt in range(NTT):
                    r0 = tt * 128
                    mm = None
                    for kb in range(8):
                        mm = p.do("tensor", lambda t, kb=kb, r0=r0: t.matmul(
                            ps_s[:], aoT_sb[:, kb * 2048 + r0: kb * 2048 + r0 + 128],
                            wo_sb[:, kb * 512:(kb + 1) * 512],
                            start=(kb == 0), stop=(kb == 7)),
                            inc="st" if kb == 7 else None)
                    p.wait("vector", mm)
                    cv = p.do("vector", lambda v: v.tensor_copy(out=tmp1[:], in_=ps_s[:]),
                              inc="sv")
                    p.wait("sync", cv)
                    sd = p.dma("sync", c.dram["s_buf"][b, r0:r0 + 128,
                                                       half * 512:(half + 1) * 512], tmp1[:])
                    p.wait("vector", sd)
                    p.wait("tensor", cv)
            p.close()
            c.barrier()
            if b == 0:
                p = Ph(c, name=_phname())
                p._ident = ident
                wld = p.sb("a0_wld", [128, 512], F32)
                xnt_sb = p.sb("a0_xnt", [128, 8 * 128], BF16)
                sin_sb = p.sb("a0_sin", [128, 512], BF16)
                cos_sb = p.sb("a0_cos", [128, 512], BF16)
                tmp1 = p.sb("a0_tmp1", [128, 512], F32)
                tmp2 = p.sb("a0_tmp2", [128, 512], F32)
                rop = p.sb("a0_rop", [128, 1024], BF16)
                pos_t = p.sb("a0_pos", [1, 128], F32)
                ps_th = p.ps("a0_ps_th", [128, 512], F32)
                ps_qkv = [p.ps(f"a0_ps_qkv{i}", [128, 512], F32) for i in range(6)]
                ps_tp = p.ps("a0_ps_tp", [128, 128], BF16)


def ph_l0_ffn(c, ident):
    nc = c.nc
    ISL = [512] * 5 + [256]
    with ExitStack() as outer:
        oe = outer.enter_context
        hqT = oe(nc.sbuf_tensor("f0_hqT", [128, 22 * 2048], BF16))
        for b in range(B_PER_CORE):
            p = Ph(c, name=_phname())
            p._ident = ident
            wg_sl = p.sb("f0_wg", [128, 8 * 512], BF16)
            wu_sl = p.sb("f0_wu", [128, 8 * 512], BF16)
            xqt = p.sb("f0_xqt", [128, 8 * 128], BF16)
            gsil = p.sb("f0_gsil", [128, 512], F32)
            hft = p.sb("f0_hf", [128, 512], F32)
            q_am = p.sb("f0_am", [128, 32], F32)
            q_ri = p.sb("f0_ri", [128, 32], F32)
            q_v = p.sb("f0_qv", [128, 512], F32)
            q_t = p.sb("f0_qt", [128, 512], F32)
            q_u = p.sb("f0_qu", [128, 512], F32)
            q_w = p.sb("f0_qw", [128, 512], F32)
            q_m = p.sb("f0_qm", [128, 512], U8)
            hqb = p.sb("f0_hqb", [128, 512], BF16)
            ps_g = p.ps("f0_ps_g", [128, 512], F32)
            ps_u = p.ps("f0_ps_u", [128, 512], F32)
            ps_tp = p.ps("f0_ps_tp", [128, 128], BF16)
            i0 = 0
            prev_hmul = None
            prev_sg = None
            for isl, w in enumerate(ISL):
                p.fence_all("sync")
                wl = p.dma("sync", wg_sl[:].rearrange("p (k t) -> p k t", k=8)[:, :, 0:w],
                           c.dram["wgqT"][:, i0:i0 + w].rearrange("(k p) t -> p k t", p=128))
                wl = p.dma("sync", wu_sl[:].rearrange("p (k t) -> p k t", k=8)[:, :, 0:w],
                           c.dram["wuqT"][:, i0:i0 + w].rearrange("(k p) t -> p k t", p=128))
                p.wait("tensor", wl)
                for tt in range(NTT):
                    r0 = tt * 128
                    p.fence_all("gpsimd")
                    xl = p.dma("gpsimd", xqt[:].rearrange("p (k t) -> p k t", k=8),
                               c.dram["xqT"][b, :, r0:r0 + 128]
                               .rearrange("(k p) t -> p k t", p=128))
                    p.wait("tensor", xl)
                    p.wait("tensor", prev_hmul)
                    p.wait("tensor", prev_sg)
                    mmg = mmu = None
                    for kb in range(8):
                        mmg = p.do("tensor", lambda t, kb=kb, w=w: t.matmul(
                            ps_g[:, 0:w], xqt[:, kb * 128:(kb + 1) * 128],
                            wg_sl[:, kb * 512: kb * 512 + w],
                            start=(kb == 0), stop=(kb == 7)),
                            inc="st" if kb == 7 else None)
                    for kb in range(8):
                        mmu = p.do("tensor", lambda t, kb=kb, w=w: t.matmul(
                            ps_u[:, 0:w], xqt[:, kb * 128:(kb + 1) * 128],
                            wu_sl[:, kb * 512: kb * 512 + w],
                            start=(kb == 0), stop=(kb == 7)),
                            inc="st" if kb == 7 else None)
                    p.wait("scalar", mmg)
                    sg = p.do("scalar", lambda s, w=w: s.activation(
                        out=gsil[:, 0:w], in_=ps_g[:, 0:w], func=AF.Silu), inc="ss")
                    prev_sg = sg
                    p.wait("vector", sg)
                    p.wait("vector", mmu)
                    hm = p.do("vector", lambda v, w=w: v.tensor_tensor(
                        out=hft[:, 0:w], in0=gsil[:, 0:w], in1=ps_u[:, 0:w],
                        op=ALU.mult), inc="sv")
                    prev_hmul = hm
                    p.wait("vector", hm)
                    qt_ = fp4_quant(p, c, hqb[:, 0:w], hft[:, 0:w], q_am, q_ri,
                                    q_v, q_t, q_u, q_w, q_m, w)
                    p.wait("tensor", qt_)
                    for j in range(w // 128):
                        ib = i0 // 128 + j
                        tr = p.do("tensor", lambda t, j=j: t.transpose(
                            ps_tp[:], hqb[:, j * 128:(j + 1) * 128], p._ident[:]), inc="st")
                        p.wait("vector", tr)
                        cv = p.do("vector", lambda v, ib=ib, r0=r0: v.tensor_copy(
                            out=hqT[:, ib * 2048 + r0: ib * 2048 + r0 + 128], in_=ps_tp[:]),
                            inc="sv")
                        p.wait("tensor", cv)
                i0 += w
            p.close()
            c.barrier()
            p = Ph(c, name=_phname())
            p._ident = ident
            wd_sl = p.sb("f0_wd", [128, 22 * 512], BF16)
            gev = p.sb("f0_gev", [128, 512], F32)
            ps_d = p.ps("f0_ps_d", [128, 512], F32)
            for half in range(2):
                p.fence_all("sync")
                wl = p.dma("sync", wd_sl[:].rearrange("p (k t) -> p k t", k=22),
                           c.dram["wdqT"][:, half * 512:(half + 1) * 512]
                           .rearrange("(k p) t -> p k t", p=128))
                p.wait("tensor", wl)
                for tt in range(NTT):
                    r0 = tt * 128
                    mm = None
                    for ib in range(22):
                        mm = p.do("tensor", lambda t, ib=ib, r0=r0: t.matmul(
                            ps_d[:], hqT[:, ib * 2048 + r0: ib * 2048 + r0 + 128],
                            wd_sl[:, ib * 512:(ib + 1) * 512],
                            start=(ib == 0), stop=(ib == 21)),
                            inc="st" if ib == 21 else None)
                    p.wait("vector", mm)
                    cv = p.do("vector", lambda v: v.tensor_copy(out=gev[:], in_=ps_d[:]),
                              inc="sv")
                    p.wait("sync", cv)
                    sd = p.dma("sync", c.dram["g_buf"][b, r0:r0 + 128,
                                                       half * 512:(half + 1) * 512], gev[:])
                    p.wait("vector", sd)
                    p.wait("tensor", cv)
            p.close()
            c.barrier()

def ph_l1_attn(c, ident):
    nc = c.nc
    DI = 2048
    for b in range(B_PER_CORE):
        for dr in ("f", "b"):
            rev = (dr == "b")
            _l1_inproj(c, ident, b, dr, rev)
            c.barrier()
            _l1_conv(c, ident, b, dr)
            c.barrier()
            _l1_xproj_delta(c, ident, b, dr)
            c.barrier()
            _l1_scan(c, ident, b, dr, rev)
            c.barrier()
            _l1_outproj(c, ident, b, dr)
            c.barrier()
        _l1_gate_proj(c, ident, b)
        c.barrier()


def _l1_inproj(c, ident, b, dr, rev):
    with Ph(c, name=_phname()) as p:
        p._ident = ident
        win = p.sb("i1_win", [128, 8 * 4096], BF16)
        wld = p.sb("i1_wld", [128, 4096], F32)
        xnc = p.sb("i1_xnc", [128, 8 * 512], BF16)
        ev = p.sb("i1_ev", [128, 512], BF16)
        ev2 = p.sb("i1_ev2", [128, 512], BF16)
        ps = p.ps("i1_ps", [128, 512], F32)
        wname = f"L1_attn_inproj_{dr}"
        for kb in range(8):
            ld = p.dma("sync", wld[:], c.dram[wname][kb * 128:(kb + 1) * 128, :])
            p.wait("vector", ld)
            cv = p.do("vector", lambda v, kb=kb: v.tensor_copy(
                out=win[:, kb * 4096:(kb + 1) * 4096], in_=wld[:]), inc="sv")
            p.wait("sync", cv)
        prev_ev = None
        for ts4 in range(4):
            t0 = ts4 * 512
            p.fence_all("gpsimd")
            xl = p.dma("gpsimd", xnc[:].rearrange("p (k t) -> p k t", k=8),
                       c.dram["xnT"][b, :, t0:t0 + 512].rearrange("(k p) t -> p k t", p=128))
            p.wait("tensor", xl)
            for mf in range(32):
                p.wait("tensor", prev_ev)
                mm = None
                for kb in range(8):
                    mm = p.do("tensor", lambda t, kb=kb, mf=mf: t.matmul(
                        ps[:], win[:, kb * 4096 + mf * 128: kb * 4096 + mf * 128 + 128],
                        xnc[:, kb * 512:(kb + 1) * 512],
                        start=(kb == 0), stop=(kb == 7)),
                        inc="st" if kb == 7 else None)
                tdst = (3 - ts4) * 512 if rev else t0
                if mf < 16:
                    p.wait("vector", mm)
                    if rev:
                        cv = p.do("vector", lambda v: v.tensor_copy(
                            out=ev[:], in_=ps[:, ::-1]), inc="sv")
                    else:
                        cv = p.do("vector", lambda v: v.tensor_copy(
                            out=ev[:], in_=ps[:]), inc="sv")
                    p.wait("sync", cv)
                    sd = p.dma("sync", c.dram["l1_uT"][mf * 128:(mf + 1) * 128,
                                                       tdst:tdst + 512], ev[:])
                    p.wait("vector", sd)
                    prev_ev = cv
                else:
                    mz = mf - 16
                    p.wait("scalar", mm)
                    sg = p.do("scalar", lambda s: s.activation(
                        out=ev2[:], in_=ps[:], func=AF.Silu), inc="ss")
                    p.wait("vector", sg)
                    if rev:
                        cv = p.do("vector", lambda v: v.tensor_copy(
                            out=ev[:], in_=ev2[:, ::-1]), inc="sv")
                    else:
                        cv = p.do("vector", lambda v: v.tensor_copy(
                            out=ev[:], in_=ev2[:]), inc="sv")
                    p.wait("sync", cv)
                    sd = p.dma("sync", c.dram["l1_zsT"][mz * 128:(mz + 1) * 128,
                                                        tdst:tdst + 512], ev[:])
                    p.wait("vector", sd)
                    prev_ev = cv


def _l1_conv(c, ident, b, dr):
    with Ph(c, name=_phname()) as p:
        p._ident = ident
        upad = p.sb("c1_upad", [128, 2051], BF16)
        cw = p.sb("c1_cw", [128, 4], F32)
        cb = p.sb("c1_cb", [128, 1], F32)
        mk = [p.sb(f"c1_mk{k}", [128, T], F32) for k in range(4)]
        acc = p.sb("c1_acc", [128, T], F32)
        ucb = p.sb("c1_ucb", [128, T], BF16)
        for dt in range(16):
            p.fence_all("sync")
            p.fence_all("scalar")
            p.do("vector", lambda v: v.memset(upad[:, 0:3], 0.0))
            l1 = p.dma("sync", upad[:, 3:2051],
                       c.dram["l1_uT"][dt * 128:(dt + 1) * 128, :])
            l2 = p.dma("sync", cw[:], c.dram[f"L1_attn_conv_w_{dr}"][dt * 128:(dt + 1) * 128, :])
            l3 = p.dma("sync", cb[:],
                       c.dram[f"L1_attn_conv_b_{dr}"][0:1, dt * 128:(dt + 1) * 128]
                       .rearrange("a b -> b a"), allow_slow_non_contiguous=True)
            p.wait("scalar", l1)
            p.wait("scalar", l2)
            p.wait("vector", l3)
            aa = []
            for k in range(4):
                aa.append(p.do("scalar", lambda s, k=k: s.activation(
                    out=mk[k][:], in_=upad[:, k:k + 2048], func=AF.Identity,
                    scale=cw[:, k:k + 1]), inc="ss"))
            for k in range(4):
                p.wait("vector", aa[k])
            p.do("vector", lambda v: v.tensor_tensor(
                out=acc[:], in0=mk[0][:], in1=cb[:, 0:1].to_broadcast([128, T]), op=ALU.add))
            p.do("vector", lambda v: v.tensor_tensor(out=acc[:], in0=acc[:], in1=mk[1][:],
                                                     op=ALU.add))
            p.do("vector", lambda v: v.tensor_tensor(out=acc[:], in0=acc[:], in1=mk[2][:],
                                                     op=ALU.add))
            s4 = p.do("vector", lambda v: v.tensor_tensor(
                out=acc[:], in0=acc[:], in1=mk[3][:], op=ALU.add), inc="sv")
            p.wait("scalar", s4)
            sg = p.do("scalar", lambda s: s.activation(out=ucb[:], in_=acc[:], func=AF.Silu),
                      inc="ss")
            p.wait("sync", sg)
            sd = p.dma("sync", c.dram["l1_ucT"][dt * 128:(dt + 1) * 128, :], ucb[:])
            p.wait("scalar", sd)
            p.wait("vector", sd)


def _l1_xproj_delta(c, ident, b, dr):
    with Ph(c, name=_phname()) as p:
        p._ident = ident
        xpj = p.sb("x1_xpj", [128, 16 * 96], BF16)
        wld = p.sb("x1_wld", [128, 96], F32)
        ucc = p.sb("x1_ucc", [128, 16 * 512], BF16)
        dtT = p.sb("x1_dtT", [64, 4 * 512], BF16)
        bcv = p.sb("x1_bcv", [128, 4 * 512], F32)
        dtw = p.sb("x1_dtw", [64, 2048], BF16)
        dtwl = p.sb("x1_dtwl", [64, 2048], F32)
        dtb = p.sb("x1_dtb", [128, 1], F32)
        sp0 = p.sb("x1_sp0", [128, 512], F32)
        spa = p.sb("x1_spa", [128, 512], F32)
        spe = p.sb("x1_spe", [128, 512], F32)
        spr = p.sb("x1_spr", [128, 512], F32)
        ps = p.ps("x1_ps", [128, 512], F32)
        for kb in range(16):
            ld = p.dma("sync", wld[:], c.dram[f"L1_attn_xproj_{dr}"][kb * 128:(kb + 1) * 128, :])
            p.wait("vector", ld)
            cv = p.do("vector", lambda v, kb=kb: v.tensor_copy(
                out=xpj[:, kb * 96:(kb + 1) * 96], in_=wld[:]), inc="sv")
            p.wait("sync", cv)
        ld = p.dma("sync", dtwl[:], c.dram[f"L1_attn_dt_w_{dr}"][:, :])
        p.wait("vector", ld)
        cv = p.do("vector", lambda v: v.tensor_copy(out=dtw[:], in_=dtwl[:]), inc="sv")
        p.wait("tensor", cv)
        prev_e1 = None
        prev_e2 = None
        for ts4 in range(4):
            t0 = ts4 * 512
            p.fence_all("gpsimd")
            ul = p.dma("gpsimd", ucc[:].rearrange("p (k t) -> p k t", k=16),
                       c.dram["l1_ucT"][:, t0:t0 + 512].rearrange("(k p) t -> p k t", p=128))
            p.wait("tensor", ul)
            mm = None
            for kb in range(16):
                mm = p.do("tensor", lambda t, kb=kb: t.matmul(
                    ps[0:96, :], xpj[:, kb * 96:(kb + 1) * 96],
                    ucc[:, kb * 512:(kb + 1) * 512],
                    start=(kb == 0), stop=(kb == 15)),
                    inc="st" if kb == 15 else None)
            p.wait("vector", mm)
            e1 = p.do("vector", lambda v, ts4=ts4: v.tensor_copy(
                out=dtT[:, ts4 * 512:(ts4 + 1) * 512], in_=ps[0:64, :]), inc="sv")
            e2 = p.do("vector", lambda v, ts4=ts4: v.tensor_copy(
                out=bcv[64:96, ts4 * 512:(ts4 + 1) * 512], in_=ps[64:96, :]), inc="sv")
            prev_e1, prev_e2 = e1, e2
            p.wait("tensor", e1)
            p.wait("tensor", e2)
        p.wait("sync", prev_e2)
        sd = p.dma("sync", c.dram["l1_bc"][:, :], bcv[64:96, :])
        p.wait("vector", sd)
        prev_del = None
        prev_m1 = None
        for dblk in range(16):
            p.wait("gpsimd", prev_m1)
            lb = p.dma("gpsimd", dtb[:],
                       c.dram[f"L1_attn_dt_b_{dr}"][0:1, dblk * 128:(dblk + 1) * 128]
                       .rearrange("a b -> b a"), allow_slow_non_contiguous=True)
            p.wait("vector", lb)
            for ts4 in range(4):
                p.wait("tensor", prev_del)
                mm = p.do("tensor", lambda t, dblk=dblk, ts4=ts4: t.matmul(
                    ps[:], dtw[:, dblk * 128:(dblk + 1) * 128],
                    dtT[:, ts4 * 512:(ts4 + 1) * 512],
                    start=True, stop=True), inc="st")
                p.wait("vector", mm)
                m1 = p.do("vector", lambda v: v.tensor_tensor(
                    out=sp0[:], in0=ps[:], in1=dtb[:, 0:1].to_broadcast([128, 512]),
                    op=ALU.add), inc="sv")
                prev_m1 = m1
                p.wait("scalar", m1)
                m2 = p.do("scalar", lambda s: s.activation(
                    out=spa[:], in_=sp0[:], func=AF.Abs), inc="ss")
                m3 = p.do("scalar", lambda s: s.activation(
                    out=spe[:], in_=spa[:], func=AF.Exp, scale=-1.0), inc="ss")
                m4 = p.do("scalar", lambda s: s.activation(
                    out=spe[:], in_=spe[:], func=AF.Ln, bias=1.0), inc="ss")
                p.wait("vector", m4)
                m5 = p.do("vector", lambda v: v.tensor_scalar(
                    out=spr[:], in0=sp0[:], scalar1=0.0, scalar2=None,
                    op0=ALU.max))
                m6 = p.do("vector", lambda v: v.tensor_tensor(
                    out=sp0[:], in0=spr[:], in1=spe[:], op=ALU.add), inc="sv")
                prev_del = m6
                p.wait("sync", m6)
                sd = p.dma("sync", c.dram["l1_dT"][dblk * 128:(dblk + 1) * 128,
                                                   ts4 * 512:(ts4 + 1) * 512], sp0[:])
                p.wait("vector", sd)


def _l1_scan(c, ident, b, dr, rev):
    DI = 2048
    with Ph(c, name=_phname()) as p:
        p._ident = ident
        a_all = p.sb("s1_a", [128, 16 * 16], F32)
        dp_all = p.sb("s1_dp", [128, 16], F32)
        alog = p.sb("s1_alog", [128, 16], F32)
        ones_r = p.sb("s1_ones", [1, 128], F32)
        br = [p.sb(f"s1_br{i}", [1, T], F32) for i in range(2)]
        cr = [p.sb(f"s1_cr{i}", [1, T], F32) for i in range(2)]
        bb = [p.sb(f"s1_bb{i}", [128, T], BF16) for i in range(2)]
        cbb = [p.sb(f"s1_cb{i}", [128, T], BF16) for i in range(2)]
        dlt = p.sb("s1_dlt", [128, 4 * T], F32)
        duq = p.sb("s1_du", [128, 4 * T], BF16)
        yac = p.sb("s1_yac", [128, 4 * T], F32)
        ucb = p.sb("s1_ucb", [128, T], BF16)
        da = [p.sb(f"s1_da{i}", [128, T], F32) for i in range(2)]
        xx = [p.sb(f"s1_xx{i}", [128, T], BF16) for i in range(2)]
        hh = [p.sb(f"s1_hh{i}", [128, T], BF16) for i in range(2)]
        hc = [p.sb(f"s1_hc{i}", [128, T], BF16) for i in range(2)]
        ucdp = p.sb("s1_ucdp", [128, T], F32)
        zs = p.sb("s1_zs", [128, T], BF16)
        ps_bc = p.ps("s1_ps_bc", [128, 512], F32)
        p.do("vector", lambda v: v.memset(ones_r[:], 1.0))
        for dt in range(16):
            la = p.dma("sync", alog[:], c.dram[f"L1_attn_Alog_{dr}"][dt * 128:(dt + 1) * 128, :])
            p.wait("scalar", la)
            ea = p.do("scalar", lambda s: s.activation(out=alog[:], in_=alog[:], func=AF.Exp),
                      inc="ss")
            p.wait("vector", ea)
            na = p.do("vector", lambda v, dt=dt: v.tensor_scalar(
                out=a_all[:, dt * 16:(dt + 1) * 16], in0=alog[:], scalar1=-1.0,
                scalar2=None, op0=ALU.mult), inc="sv")
            p.wait("sync", na)
        ldp = p.dma("sync", dp_all[:, 0:16],
                    c.dram[f"L1_attn_Dp_{dr}"][0:1, :].rearrange("a (dt p) -> (a p) dt", p=128),
                    allow_slow_non_contiguous=True)
        p.wait("scalar", ldp)
        p.wait("scalar", na)
        prev_q_ya = None

        for qtr in range(4):
            q0 = qtr * 4
            p.fence_all("sync")
            dl = None
            for i in range(4):
                dl = p.dma("sync", dlt[:, i * T:(i + 1) * T],
                           c.dram["l1_dT"][(q0 + i) * 128:(q0 + i + 1) * 128, :])
            p.wait("vector", dl)
            p.wait("gpsimd", dl)
            p.wait("scalar", dl)
            duk = None
            for i in range(4):
                ul = p.dma("gpsimd", ucb[:], c.dram["l1_ucT"][(q0 + i) * 128:(q0 + i + 1) * 128, :])
                p.wait("vector", ul)
                duk = p.do("vector", lambda v, i=i: v.tensor_tensor(
                    out=duq[:, i * T:(i + 1) * T], in0=dlt[:, i * T:(i + 1) * T],
                    in1=ucb[:], op=ALU.mult), inc="sv")
                p.wait("gpsimd", duk)
            p.wait("vector", prev_q_ya)
            mz = p.do("vector", lambda v: v.memset(yac[:], 0.0), inc="sv")
            p.wait("scalar", mz)
            p.wait("tensor", mz)
            p.wait("gpsimd", mz)

            NITER = 16 * 4
            tok_hs = [None] * NITER
            tok_ad = [None] * NITER
            tok_hcm = [None] * NITER
            tok_ya = [None] * NITER
            tok_bev = [None] * 16
            tok_lbc = [None] * 16

            def emit_bcast(n):
                sl_ = n % 2
                if n >= 2 and tok_ya[(n - 2) * 4 + 3] is not None:
                    p.wait("sync", tok_ya[(n - 2) * 4 + 3])
                    p.wait("tensor", tok_ya[(n - 2) * 4 + 3])
                lb = p.dma("sync", br[sl_][:], c.dram["l1_bc"][n:n + 1, :])
                lc = p.dma("sync", cr[sl_][:], c.dram["l1_bc"][16 + n:17 + n, :])
                p.wait("tensor", lc)
                bev = None
                for sl2 in range(4):
                    bmm = p.do("tensor", lambda t, sl2=sl2, sl_=sl_: t.matmul(
                        ps_bc[:], ones_r[0:1, :], br[sl_][0:1, sl2 * 512:(sl2 + 1) * 512],
                        start=True, stop=True), inc="st")
                    p.wait("vector", bmm)
                    bev = p.do("vector", lambda v, sl2=sl2, sl_=sl_: v.tensor_copy(
                        out=bb[sl_][:, sl2 * 512:(sl2 + 1) * 512], in_=ps_bc[:]), inc="sv")
                    p.wait("tensor", bev)
                for sl2 in range(4):
                    cmm = p.do("tensor", lambda t, sl2=sl2, sl_=sl_: t.matmul(
                        ps_bc[:], ones_r[0:1, :], cr[sl_][0:1, sl2 * 512:(sl2 + 1) * 512],
                        start=True, stop=True), inc="st")
                    p.wait("vector", cmm)
                    bev = p.do("vector", lambda v, sl2=sl2, sl_=sl_: v.tensor_copy(
                        out=cbb[sl_][:, sl2 * 512:(sl2 + 1) * 512], in_=ps_bc[:]), inc="sv")
                    p.wait("tensor", bev)
                tok_bev[n] = bev

            emit_bcast(0)
            for n in range(16):
                sl_ = n % 2
                if n + 1 < 16:
                    emit_bcast(n + 1)
                for i in range(4):
                    k = n * 4 + i
                    kb2 = k % 2
                    dt = q0 + i
                    if k >= 2:
                        p.wait("scalar", tok_hs[k - 2])
                    ad = p.do("scalar", lambda s, i=i, dt=dt, n=n, kb2=kb2: s.activation(
                        out=da[kb2][:], in_=dlt[:, i * T:(i + 1) * T], func=AF.Exp,
                        scale=a_all[:, dt * 16 + n: dt * 16 + n + 1]), inc="ss")
                    tok_ad[k] = ad
                    p.wait("vector", tok_bev[n])
                    xm = p.do("vector", lambda v, i=i, kb2=kb2, sl_=sl_: v.tensor_tensor(
                        out=xx[kb2][:], in0=duq[:, i * T:(i + 1) * T], in1=bb[sl_][:],
                        op=ALU.mult), inc="sv")
                    p.wait("vector", ad)
                    hs = p.do("vector", lambda v, kb2=kb2: v.tensor_tensor_scan(
                        out=hh[kb2][:], data0=da[kb2][:], data1=xx[kb2][:], initial=0.0,
                        op0=ALU.mult, op1=ALU.add), inc="sv")
                    tok_hs[k] = hs
                    if k >= 2:
                        p.wait("vector", tok_ya[k - 2])
                    hcm = p.do("vector", lambda v, kb2=kb2, sl_=sl_: v.tensor_tensor(
                        out=hc[kb2][:], in0=hh[kb2][:], in1=cbb[sl_][:], op=ALU.mult),
                        inc="sv")
                    tok_hcm[k] = hcm
                    p.wait("gpsimd", hcm)
                    ya = p.do("gpsimd", lambda g, i=i, kb2=kb2: g.tensor_tensor(
                        out=yac[:, i * T:(i + 1) * T], in0=yac[:, i * T:(i + 1) * T],
                        in1=hc[kb2][:], op=ALU.add), inc="sg")
                    tok_ya[k] = ya
            for i in range(4):
                dt = q0 + i
                p.fence_all("gpsimd")
                ul = p.dma("gpsimd", ucb[:], c.dram["l1_ucT"][dt * 128:(dt + 1) * 128, :])
                zl = p.dma("gpsimd", zs[:], c.dram["l1_zsT"][dt * 128:(dt + 1) * 128, :])
                p.wait("scalar", ul)
                p.wait("scalar", tok_hs[63])
                ud = p.do("scalar", lambda s, dt=dt: s.activation(
                    out=ucdp[:], in_=ucb[:], func=AF.Identity,
                    scale=dp_all[:, dt:dt + 1]), inc="ss")
                p.wait("vector", ud)
                p.wait("vector", tok_ya[63])
                yf = p.do("vector", lambda v, i=i: v.tensor_tensor(
                    out=ucdp[:], in0=yac[:, i * T:(i + 1) * T], in1=ucdp[:],
                    op=ALU.add), inc="sv")
                p.wait("vector", zl)
                p.wait("vector", yf)
                if rev:
                    yg = p.do("vector", lambda v: v.tensor_tensor(
                        out=hh[0][:], in0=ucdp[:], in1=zs[:], op=ALU.mult), inc="sv")
                    p.wait("vector", yg)
                    yg = p.do("vector", lambda v: v.tensor_copy(
                        out=hc[0][:], in_=hh[0][:, ::-1]), inc="sv")
                else:
                    yg = p.do("vector", lambda v: v.tensor_tensor(
                        out=hc[0][:], in0=ucdp[:], in1=zs[:], op=ALU.mult), inc="sv")
                p.wait("sync", yg)
                sd = p.dma("sync", c.dram["l1_ygT"][dt * 128:(dt + 1) * 128, :], hc[0][:])
                p.wait("vector", sd)
                p.wait("gpsimd", sd)
            prev_q_ya = tok_ya[63]


def _l1_outproj(c, ident, b, dr):
    with Ph(c, name=_phname()) as p:
        p._ident = ident
        opw = p.sb("o1_opw", [128, 16 * 1024], BF16)
        wld = p.sb("o1_wld", [128, 1024], F32)
        ygc = p.sb("o1_ygc", [128, 16 * 512], BF16)
        ev = p.sb("o1_ev", [128, 512], BF16)
        ps = p.ps("o1_ps", [128, 512], F32)
        wname = f"L1_attn_outproj_{dr}"
        dstname = "l1_yfT" if dr == "f" else "l1_ybT"
        for kb in range(16):
            ld = p.dma("sync", wld[:], c.dram[wname][kb * 128:(kb + 1) * 128, :])
            p.wait("vector", ld)
            cv = p.do("vector", lambda v, kb=kb: v.tensor_copy(
                out=opw[:, kb * 1024:(kb + 1) * 1024], in_=wld[:]), inc="sv")
            p.wait("sync", cv)
        prev_ev = None
        for ts4 in range(4):
            t0 = ts4 * 512
            p.fence_all("gpsimd")
            yl = p.dma("gpsimd", ygc[:].rearrange("p (k t) -> p k t", k=16),
                       c.dram["l1_ygT"][:, t0:t0 + 512].rearrange("(k p) t -> p k t", p=128))
            p.wait("tensor", yl)
            for mf in range(8):
                p.wait("tensor", prev_ev)
                mm = None
                for kb in range(16):
                    mm = p.do("tensor", lambda t, kb=kb, mf=mf: t.matmul(
                        ps[:], opw[:, kb * 1024 + mf * 128: kb * 1024 + mf * 128 + 128],
                        ygc[:, kb * 512:(kb + 1) * 512],
                        start=(kb == 0), stop=(kb == 15)),
                        inc="st" if kb == 15 else None)
                p.wait("vector", mm)
                cv = p.do("vector", lambda v: v.tensor_copy(out=ev[:], in_=ps[:]), inc="sv")
                prev_ev = cv
                p.wait("sync", cv)
                sd = p.dma("sync", c.dram[dstname][mf * 128:(mf + 1) * 128, t0:t0 + 512], ev[:])
                p.wait("vector", sd)


def _l1_gate_proj(c, ident, b):
    with Ph(c, name=_phname()) as p:
        p._ident = ident
        gw = p.sb("g1_gw", [128, 16 * 1024], BF16)
        pw = p.sb("g1_pw", [128, 8 * 1024], BF16)
        wld = p.sb("g1_wld", [128, 1024], F32)
        yc = p.sb("g1_yc", [128, 16 * 512], BF16)
        gb_c = p.sb("g1_gb", [128, 8], F32)
        pb_r = p.sb("g1_pb", [1, 1024], F32)
        ones_r = p.sb("g1_ones", [1, 128], F32)
        gsig = p.sb("g1_gsig", [128, 512], BF16)
        ycmb = p.sb("g1_ycmb", [128, 8 * 512], BF16)
        sev = p.sb("g1_sev", [128, 512], F32)
        tmp = p.sb("g1_tmp", [128, 512], F32)
        ps = p.ps("g1_ps", [128, 512], F32)
        ps2 = p.ps("g1_ps2", [128, 512], F32)
        p.do("vector", lambda v: v.memset(ones_r[:], 1.0))
        for kb in range(16):
            ld = p.dma("sync", wld[:], c.dram["L1_attn_gate_w"][kb * 128:(kb + 1) * 128, :])
            p.wait("vector", ld)
            cv = p.do("vector", lambda v, kb=kb: v.tensor_copy(
                out=gw[:, kb * 1024:(kb + 1) * 1024], in_=wld[:]), inc="sv")
            p.wait("sync", cv)
        for kb in range(8):
            ld = p.dma("sync", wld[:], c.dram["L1_attn_proj_w"][kb * 128:(kb + 1) * 128, :])
            p.wait("vector", ld)
            cv = p.do("vector", lambda v, kb=kb: v.tensor_copy(
                out=pw[:, kb * 1024:(kb + 1) * 1024], in_=wld[:]), inc="sv")
            p.wait("sync", cv)
        l1 = p.dma("sync", gb_c[:, 0:8],
                   c.dram["L1_attn_gate_b"][0:1, :].rearrange("a (m p) -> (a p) m", p=128),
                   allow_slow_non_contiguous=True)
        l2 = p.dma("sync", pb_r[:], c.dram["L1_attn_proj_b"][:, :])
        p.wait("scalar", l1)
        p.wait("tensor", l2)
        prev_pj = None
        for ts4 in range(4):
            t0 = ts4 * 512
            p.fence_all("gpsimd")
            yl = p.dma("gpsimd", yc[:, 0:8 * 512].rearrange("p (k t) -> p k t", k=8),
                       c.dram["l1_yfT"][:, t0:t0 + 512].rearrange("(k p) t -> p k t", p=128))
            yl = p.dma("gpsimd", yc[:, 8 * 512:16 * 512].rearrange("p (k t) -> p k t", k=8),
                       c.dram["l1_ybT"][:, t0:t0 + 512].rearrange("(k p) t -> p k t", p=128))
            p.wait("tensor", yl)
            for mf in range(8):
                p.wait("tensor", prev_pj)
                mm = None
                for kb in range(16):
                    mm = p.do("tensor", lambda t, kb=kb, mf=mf: t.matmul(
                        ps[:], gw[:, kb * 1024 + mf * 128: kb * 1024 + mf * 128 + 128],
                        yc[:, kb * 512:(kb + 1) * 512],
                        start=(kb == 0), stop=(kb == 15)),
                        inc="st" if kb == 15 else None)
                p.wait("scalar", mm)
                sg = p.do("scalar", lambda s, mf=mf: s.activation(
                    out=gsig[:], in_=ps[:], func=AF.Sigmoid,
                    bias=gb_c[:, mf:mf + 1], scale=1.0), inc="ss")
                p.wait("vector", sg)
                d1 = p.do("vector", lambda v, mf=mf: v.tensor_tensor(
                    out=tmp[:], in0=yc[:, mf * 512:(mf + 1) * 512],
                    in1=yc[:, (8 + mf) * 512:(9 + mf) * 512], op=ALU.subtract), inc="sv")
                p.wait("vector", d1)
                d2 = p.do("vector", lambda v: v.tensor_tensor(
                    out=tmp[:], in0=tmp[:], in1=gsig[:], op=ALU.mult))
                d3 = p.do("vector", lambda v, mf=mf: v.tensor_tensor(
                    out=ycmb[:, mf * 512:(mf + 1) * 512], in0=tmp[:],
                    in1=yc[:, (8 + mf) * 512:(9 + mf) * 512], op=ALU.add), inc="sv")
                prev_pj = d3
                p.wait("tensor", d3)
            for tq in range(4):
                tok0 = t0 + tq * 128
                for half in range(2):
                    mm = None
                    for kb in range(8):
                        mm = p.do("tensor", lambda t, kb=kb, tq=tq, half=half: t.matmul(
                            ps2[:], ycmb[:, kb * 512 + tq * 128: kb * 512 + tq * 128 + 128],
                            pw[:, kb * 1024 + half * 512: kb * 1024 + (half + 1) * 512],
                            start=(kb == 0), stop=False), inc=None)
                    mm = p.do("tensor", lambda t, half=half: t.matmul(
                        ps2[:], ones_r[0:1, :], pb_r[0:1, half * 512:(half + 1) * 512],
                        start=False, stop=True), inc="st")
                    p.wait("vector", mm)
                    cv = p.do("vector", lambda v: v.tensor_copy(out=sev[:], in_=ps2[:]),
                              inc="sv")
                    p.wait("sync", cv)
                    sd = p.dma("sync", c.dram["s_buf"][b, tok0:tok0 + 128,
                                                       half * 512:(half + 1) * 512], sev[:])
                    p.wait("vector", sd)
                    p.wait("tensor", cv)
                prev_pj = cv

def ph_l1_ffn(c, ident):
    _l1_router(c, ident)
    c.barrier()
    for e in range(-1, 8):
        _l1_expert(c, ident, e)
        c.barrier()


def _l1_router(c, ident):
    nc = c.nc
    with Ph(c, name=_phname()) as p:
        p._ident = ident
        xt = p.sb("r1_x", [128, D], BF16)
        xm = p.sb("r1_xm", [128, D], F32)
        sq = p.sb("r1_sq", [128, D], F32)
        x2 = p.sb("r1_x2", [128, D], BF16)
        x2T = p.sb("r1_x2T", [128, 8 * 128], BF16)
        gamb = p.sb("r1_gamb", [128, D], F32)
        betb = p.sb("r1_betb", [128, D], F32)
        grow = p.sb("r1_grow", [1, D], F32)
        gwt = p.sb("r1_gwt", [128, 8 * 8], BF16)
        gwl = p.sb("r1_gwl", [128, 8], F32)
        st = p.sb("r1_st", [128, 8], F32)
        pr = p.sb("r1_pr", [128, 8], F32)
        mx8 = p.sb("r1_mx8", [128, 8], F32)
        msk = p.sb("r1_msk", [128, 8], U8)
        wv = p.sb("r1_wv", [128, 8], F32)
        ps_b = p.ps("r1_ps_b", [128, 512], F32)
        ps_t = p.ps("r1_ps_t", [128, 128], BF16)
        ps_l = p.ps("r1_ps_l", [128, 8], F32)
        ones_r = p.sb("r1_ones", [1, 128], F32)
        p.do("vector", lambda v: v.memset(ones_r[:], 1.0))
        last_bc_mm = None
        for nm, dst in (("L1_ffn_ln_scale", gamb), ("L1_ffn_ln_bias", betb)):
            p.wait("sync", last_bc_mm)
            ld = p.dma("sync", grow[:], c.dram[nm][:, :])
            p.wait("tensor", ld)
            for sl in range(2):
                mm = p.do("tensor", lambda t, sl=sl: t.matmul(
                    ps_b[:], ones_r[0:1, :], grow[0:1, sl * 512:(sl + 1) * 512],
                    start=True, stop=True), inc="st")
                last_bc_mm = mm
                p.wait("vector", mm)
                cv = p.do("vector", lambda v, sl=sl, dst=dst: v.tensor_copy(
                    out=dst[:, sl * 512:(sl + 1) * 512], in_=ps_b[:]), inc="sv")
                p.wait("tensor", cv)
        gwl8 = p.sb("r1_gwl8", [128, 64], F32)
        ld = p.dma("sync", gwl8[:].rearrange("p (k e) -> p k e", k=8),
                   c.dram["L1_ffn_gate_wT"][:, :].rearrange("(k p) e -> p k e", p=128))
        p.wait("vector", ld)
        cv = p.do("vector", lambda v: v.tensor_copy(out=gwt[:], in_=gwl8[:]), inc="sv")
        p.wait("sync", cv)
        for b in range(B_PER_CORE):
            for tt in range(NTT):
                r0 = tt * 128
                p.fence_all("sync")
                ld = p.dma("sync", xt[:], c.dram["xn_tok"][b, r0:r0 + 128, :])
                p.wait("vector", ld)
                rs = p.do("vector", lambda v: v.reduce_sum(st[:, 0:1], xt[:], axis=AX.X),
                          inc="sv")
                p.wait("vector", rs)
                nm_ = p.do("vector", lambda v: v.tensor_scalar(
                    out=st[:, 1:2], in0=st[:, 0:1], scalar1=1.0 / D, scalar2=None,
                    op0=ALU.mult), inc="sv")
                p.wait("vector", nm_)
                xms = p.do("vector", lambda v: v.tensor_tensor(
                    out=xm[:], in0=xt[:], in1=st[:, 1:2].to_broadcast([128, D]),
                    op=ALU.subtract), inc="sv")
                p.wait("scalar", xms)
                a2 = p.do("scalar", lambda s: s.activation(out=sq[:], in_=xm[:], func=AF.Square,
                                                           accum_out=st[:, 2:3]), inc="ss")
                p.wait("vector", a2)
                v1 = p.do("vector", lambda v: v.tensor_scalar(
                    out=st[:, 3:4], in0=st[:, 2:3], scalar1=1.0 / D, scalar2=1e-5,
                    op0=ALU.mult, op1=ALU.add), inc="sv")
                p.wait("scalar", v1)
                a3 = p.do("scalar", lambda s: s.activation(out=st[:, 4:5], in_=st[:, 3:4],
                                                           func=AF.Sqrt), inc="ss")
                p.wait("vector", a3)
                rc = p.do("vector", lambda v: v.reciprocal(st[:, 5:6], st[:, 4:5]), inc="sv")
                p.wait("vector", rc)
                n1 = p.do("vector", lambda v: v.tensor_tensor(
                    out=xm[:], in0=xm[:], in1=st[:, 5:6].to_broadcast([128, D]),
                    op=ALU.mult), inc="sv")
                p.wait("vector", n1)
                n2 = p.do("vector", lambda v: v.tensor_tensor(
                    out=xm[:], in0=xm[:], in1=gamb[:], op=ALU.mult))
                n3 = p.do("vector", lambda v: v.tensor_tensor(
                    out=x2[:], in0=xm[:], in1=betb[:], op=ALU.add), inc="sv")
                p.wait("tensor", n3)
                cvl = None
                for k in range(8):
                    tr = p.do("tensor", lambda t, k=k: t.transpose(
                        ps_t[:], x2[:, k * 128:(k + 1) * 128], p._ident[:]), inc="st")
                    p.wait("vector", tr)
                    cvl = p.do("vector", lambda v, k=k: v.tensor_copy(
                        out=x2T[:, k * 128:(k + 1) * 128], in_=ps_t[:]), inc="sv")
                    p.wait("tensor", cvl)
                mm = None
                for kb in range(8):
                    mm = p.do("tensor", lambda t, kb=kb: t.matmul(
                        ps_l[:], x2T[:, kb * 128:(kb + 1) * 128], gwt[:, kb * 8:(kb + 1) * 8],
                        start=(kb == 0), stop=(kb == 7)), inc="st" if kb == 7 else None)
                p.wait("vector", mm)
                if getattr(c, "_dbg_l1r", False):
                    dl_ = p.do("vector", lambda v: v.tensor_copy(out=wv[:], in_=ps_l[:]),
                               inc="sv")
                    p.wait("sync", dl_)
                    dd = p.dma("sync", c.dram["l1_dbg_lg"][b, r0:r0 + 128, :], wv[:])
                    p.wait("vector", dd)
                r1 = p.do("vector", lambda v: v.reduce_max(st[:, 6:7], ps_l[:], axis=AX.X,
                                                           negate=True), inc="sv")
                p.wait("vector", r1)
                sb_ = p.do("vector", lambda v: v.tensor_tensor(
                    out=pr[:], in0=ps_l[:], in1=st[:, 6:7].to_broadcast([128, 8]),
                    op=ALU.add), inc="sv")
                p.wait("scalar", sb_)
                ex = p.do("scalar", lambda s: s.activation(out=pr[:], in_=pr[:], func=AF.Exp,
                                                           accum_out=st[:, 7:8]), inc="ss")
                p.wait("vector", ex)
                rc2 = p.do("vector", lambda v: v.reciprocal(st[:, 0:1], st[:, 7:8]), inc="sv")
                p.wait("vector", rc2)
                pn = p.do("vector", lambda v: v.tensor_tensor(
                    out=pr[:], in0=pr[:], in1=st[:, 0:1].to_broadcast([128, 8]),
                    op=ALU.mult), inc="sv")
                p.wait("vector", pn)
                m8 = p.do("vector", lambda v: v.max(out=mx8[:], in_=pr[:]), inc="sv")
                p.wait("vector", m8)
                w1 = p.do("vector", lambda v: v.tensor_tensor(
                    out=st[:, 1:2], in0=mx8[:, 0:1], in1=mx8[:, 1:2], op=ALU.add), inc="sv")
                p.wait("vector", w1)
                w2 = p.do("vector", lambda v: v.tensor_scalar(
                    out=st[:, 2:3], in0=st[:, 1:2], scalar1=1e-5, scalar2=None,
                    op0=ALU.add), inc="sv")
                p.wait("vector", w2)
                w3 = p.do("vector", lambda v: v.reciprocal(st[:, 3:4], st[:, 2:3]), inc="sv")
                p.wait("vector", w3)
                c1 = p.do("vector", lambda v: v.tensor_tensor(
                    out=msk[:], in0=pr[:], in1=mx8[:, 1:2].to_broadcast([128, 8]),
                    op=ALU.is_ge), inc="sv")
                p.wait("vector", c1)
                c2 = p.do("vector", lambda v: v.tensor_tensor(
                    out=wv[:], in0=pr[:], in1=st[:, 3:4].to_broadcast([128, 8]),
                    op=ALU.mult), inc="sv")
                p.wait("vector", c2)
                zz = p.do("vector", lambda v: v.memset(st[:, 4:5], 0.0), inc="sv")
                p.wait("vector", zz)
                c4 = p.do("vector", lambda v: v.select(wv[:], msk[:], wv[:],
                                                       st[:, 4:5].to_broadcast([128, 8])),
                          inc="sv")
                p.wait("sync", c4)
                sd = p.dma("sync", c.dram["l1_moew"][b, r0:r0 + 128, :], wv[:])
                p.wait("vector", sd)
                if getattr(c, "_dbg_l1r", False):
                    d1 = p.dma("sync", c.dram["l1_dbg_pr"][b, r0:r0 + 128, :], pr[:])
                    d2 = p.dma("sync", c.dram["l1_dbg_mx"][b, r0:r0 + 128, :], mx8[:])
                    p.wait("vector", d1)
                    p.wait("vector", d2)


def _l1_expert(c, ident, e):
    nc = c.nc
    with Ph(c, name=_phname()) as p:
        p._ident = ident
        wg_e = p.sb("e1_wg", [128, 8 * 2048], BF16)
        wu_e = p.sb("e1_wu", [128, 8 * 2048], BF16)
        wd_e = p.sb("e1_wd", [128, 16 * 1024], BF16)
        wld = p.sb("e1_wld", [128, 2048], F32)
        xnc = p.sb("e1_xnc", [128, 8 * 512], BF16)
        itT = p.sb("e1_itT", [128, 16 * 512], BF16)
        gsil = p.sb("e1_gsil", [128, 512], F32)
        wcol = p.sb("e1_wcol", [128, 1], F32)
        acct = p.sb("e1_acct", [128, 512], F32)
        wout = p.sb("e1_wout", [128, 512], F32)
        ps_g = p.ps("e1_ps_g", [128, 512], F32)
        ps_u = p.ps("e1_ps_u", [128, 512], F32)
        ps_d = p.ps("e1_ps_d", [128, 512], F32)
        if e < 0:
            srcs = [("L1_ffn_sWg", wg_e, 2048, None), ("L1_ffn_sWu", wu_e, 2048, None),
                    ("L1_ffn_sWd", wd_e, 1024, None)]
        else:
            srcs = [("L1_ffn_Wg", wg_e, 2048, e), ("L1_ffn_Wu", wu_e, 2048, e),
                    ("L1_ffn_Wd", wd_e, 1024, e)]
        for nm, dst, ncol, ei in srcs:
            nkb = 8 if ncol == 2048 else 16
            for kb in range(nkb):
                if ei is None:
                    src = c.dram[nm][kb * 128:(kb + 1) * 128, :]
                else:
                    src = c.dram[nm][ei, kb * 128:(kb + 1) * 128, :]
                ld = p.dma("sync", wld[:, 0:ncol], src)
                p.wait("vector", ld)
                cv = p.do("vector", lambda v, kb=kb, dst=dst, ncol=ncol: v.tensor_copy(
                    out=dst[:, kb * ncol:(kb + 1) * ncol], in_=wld[:, 0:ncol]), inc="sv")
                p.wait("sync", cv)
        prev_sd = None
        for b in range(B_PER_CORE):
            prev_it = None
            prev_ev = None
            for ts4 in range(4):
                t0 = ts4 * 512
                p.fence_all("gpsimd")
                xl = p.dma("gpsimd", xnc[:].rearrange("p (k t) -> p k t", k=8),
                           c.dram["xnT"][b, :, t0:t0 + 512].rearrange("(k p) t -> p k t", p=128))
                p.wait("tensor", xl)
                for ib in range(16):
                    p.wait("tensor", prev_it)
                    mmg = None
                    for kb in range(8):
                        mmg = p.do("tensor", lambda t, kb=kb, ib=ib: t.matmul(
                            ps_g[:], wg_e[:, kb * 2048 + ib * 128: kb * 2048 + ib * 128 + 128],
                            xnc[:, kb * 512:(kb + 1) * 512],
                            start=(kb == 0), stop=(kb == 7)), inc="st" if kb == 7 else None)
                    mmu = None
                    for kb in range(8):
                        mmu = p.do("tensor", lambda t, kb=kb, ib=ib: t.matmul(
                            ps_u[:], wu_e[:, kb * 2048 + ib * 128: kb * 2048 + ib * 128 + 128],
                            xnc[:, kb * 512:(kb + 1) * 512],
                            start=(kb == 0), stop=(kb == 7)), inc="st" if kb == 7 else None)
                    p.wait("scalar", mmg)
                    sg = p.do("scalar", lambda s: s.activation(
                        out=gsil[:], in_=ps_g[:], func=AF.Silu), inc="ss")
                    p.wait("vector", sg)
                    p.wait("vector", mmu)
                    it = p.do("vector", lambda v, ib=ib: v.tensor_tensor(
                        out=itT[:, ib * 512:(ib + 1) * 512], in0=gsil[:], in1=ps_u[:],
                        op=ALU.mult), inc="sv")
                    prev_it = it
                p.wait("tensor", prev_it)
                for tq in range(4):
                    tok0 = t0 + tq * 128
                    lw = p.dma("gpsimd", wcol[:],
                               c.dram["l1_moew"][b, tok0:tok0 + 128, e:e + 1],
                               allow_slow_non_contiguous=True) if e >= 0 else None
                    if e >= 0:
                        p.wait("scalar", lw)
                    for half in range(2):
                        mm = None
                        for kb in range(16):
                            mm = p.do("tensor", lambda t, kb=kb, tq=tq, half=half: t.matmul(
                                ps_d[:], itT[:, kb * 512 + tq * 128: kb * 512 + tq * 128 + 128],
                                wd_e[:, kb * 1024 + half * 512: kb * 1024 + (half + 1) * 512],
                                start=(kb == 0), stop=(kb == 15)),
                                inc="st" if kb == 15 else None)
                        if e < 0:
                            p.wait("vector", mm)
                            cv = p.do("vector", lambda v: v.tensor_copy(
                                out=wout[:], in_=ps_d[:]), inc="sv")
                            p.wait("sync", cv)
                            sd = p.dma("sync", c.dram["g_buf"][b, tok0:tok0 + 128,
                                                              half * 512:(half + 1) * 512],
                                       wout[:])
                            p.wait("vector", sd)
                            p.wait("tensor", cv)
                            prev_ev = cv
                        else:
                            p.wait("scalar", mm)
                            wo_ = p.do("scalar", lambda s: s.activation(
                                out=wout[:], in_=ps_d[:], func=AF.Identity,
                                scale=wcol[:, 0:1]), inc="ss")
                            p.wait("sync", prev_sd)
                            la = p.dma("sync", acct[:],
                                       c.dram["g_buf"][b, tok0:tok0 + 128,
                                                       half * 512:(half + 1) * 512])
                            p.wait("vector", la)
                            p.wait("vector", wo_)
                            ad = p.do("vector", lambda v: v.tensor_tensor(
                                out=acct[:], in0=acct[:], in1=wout[:], op=ALU.add), inc="sv")
                            p.wait("sync", ad)
                            sd = p.dma("sync", c.dram["g_buf"][b, tok0:tok0 + 128,
                                                              half * 512:(half + 1) * 512],
                                       acct[:])
                            prev_sd = sd
                            p.wait("vector", sd)
                            p.wait("scalar", sd)
                            p.wait("tensor", wo_)
                            prev_ev = wo_
                    if e >= 0:
                        p.wait("gpsimd", prev_ev)

def ph_l2_attn(c, ident):
    nc = c.nc
    with ExitStack() as outer:
        oe = outer.enter_context
        c._l2_qt = oe(nc.sbuf_tensor("l2_qt", [128, 8 * 2048], BF16))
        c._l2_kt = oe(nc.sbuf_tensor("l2_kt", [128, 8 * 2048], BF16))
        c._l2_ktok = oe(nc.sbuf_tensor("l2_ktok", [128, 16 * 1024], BF16))
        c._l2_vt = oe(nc.sbuf_tensor("l2_vt", [128, 16 * 1024], BF16))
        for b in range(B_PER_CORE):
            _l2_proj(c, ident, b)
            c.barrier()
            _l2_linattn(c, ident, b)
            c.barrier()
            _l2_out(c, ident, b)
            c.barrier()


def _elu1_evac(p, c, ps, out_bf, t1, t2, width=512):
    e1 = p.do("vector", lambda v: v.tensor_scalar(
        out=t1[:, 0:width], in0=ps, scalar1=0.0, scalar2=None, op0=ALU.min), inc="sv")
    e2 = p.do("vector", lambda v: v.tensor_scalar(
        out=t2[:, 0:width], in0=ps, scalar1=0.0, scalar2=None, op0=ALU.max), inc="sv")
    p.wait("scalar", e1)
    e3 = p.do("scalar", lambda s: s.activation(out=t1[:, 0:width], in_=t1[:, 0:width],
                                               func=AF.Exp), inc="ss")
    p.wait("vector", e3)
    e4 = p.do("vector", lambda v: v.tensor_tensor(out=out_bf, in0=t2[:, 0:width],
                                                  in1=t1[:, 0:width], op=ALU.add), inc="sv")
    return e4


def _l2_proj(c, ident, b):
    nc = c.nc
    with Ph(c, name=_phname()) as p:
        p._ident = ident
        qt = c._l2_qt; kt = c._l2_kt; ktok = c._l2_ktok; vt = c._l2_vt
        wsb = p.sb("p2_w", [128, 8 * 1024], BF16)
        wld = p.sb("p2_wld", [128, 1024], F32)
        xnc = p.sb("p2_xnc", [128, 8 * 512], BF16)
        xnt = p.sb("p2_xnt", [128, 8 * 128], BF16)
        t1 = p.sb("p2_t1", [128, 512], F32)
        t2 = p.sb("p2_t2", [128, 512], F32)
        ps = p.ps("p2_ps", [128, 512], F32)

        def load_w(name):
            for kb in range(8):
                ld = p.dma("sync", wld[:], c.dram[name][kb * 128:(kb + 1) * 128, :])
                p.wait("vector", ld)
                cv = p.do("vector", lambda v, kb=kb: v.tensor_copy(
                    out=wsb[:, kb * 1024:(kb + 1) * 1024], in_=wld[:]), inc="sv")
                p.wait("sync", cv)
            return cv

        for wname, dst, feat in (("L2_attn_Wq", qt, True), ("L2_attn_Wk", kt, True)):
            lw = load_w(wname)
            p.wait("tensor", lw)
            prev_e = None
            for ts4 in range(4):
                t0 = ts4 * 512
                p.fence_all("gpsimd")
                xl = p.dma("gpsimd", xnc[:].rearrange("p (k t) -> p k t", k=8),
                           c.dram["xnT"][b, :, t0:t0 + 512].rearrange("(k p) t -> p k t", p=128))
                p.wait("tensor", xl)
                for mf in range(8):
                    p.wait("tensor", prev_e)
                    mm = None
                    for kb in range(8):
                        mm = p.do("tensor", lambda t, kb=kb, mf=mf: t.matmul(
                            ps[:], wsb[:, kb * 1024 + mf * 128: kb * 1024 + mf * 128 + 128],
                            xnc[:, kb * 512:(kb + 1) * 512],
                            start=(kb == 0), stop=(kb == 7)),
                            inc="st" if kb == 7 else None)
                    p.wait("vector", mm)
                    ev = _elu1_evac(p, c, ps[:], dst[:, mf * 2048 + t0: mf * 2048 + t0 + 512],
                                    t1, t2)
                    prev_e = ev
        for wname, dst, do_elu in (("L2_attn_Wk", ktok, True), ("L2_attn_Wv", vt, False)):
            lw = load_w(wname)
            p.wait("tensor", lw)
            prev_e = None
            for tt in range(NTT):
                r0 = tt * 128
                p.fence_all("gpsimd")
                xl = p.dma("gpsimd", xnt[:].rearrange("p (k t) -> p k t", k=8),
                           c.dram["xnT"][b, :, r0:r0 + 128].rearrange("(k p) t -> p k t", p=128))
                p.wait("tensor", xl)
                for half in range(2):
                    p.wait("tensor", prev_e)
                    mm = None
                    for kb in range(8):
                        mm = p.do("tensor", lambda t, kb=kb, half=half: t.matmul(
                            ps[:], xnt[:, kb * 128:(kb + 1) * 128],
                            wsb[:, kb * 1024 + half * 512: kb * 1024 + (half + 1) * 512],
                            start=(kb == 0), stop=(kb == 7)),
                            inc="st" if kb == 7 else None)
                    p.wait("vector", mm)
                    if do_elu:
                        ev = _elu1_evac(p, c, ps[:],
                                        dst[:, tt * 1024 + half * 512: tt * 1024 + (half + 1) * 512],
                                        t1, t2)
                    else:
                        ev = p.do("vector", lambda v, tt=tt, half=half, dst=dst: v.tensor_copy(
                            out=dst[:, tt * 1024 + half * 512: tt * 1024 + (half + 1) * 512],
                            in_=ps[:]), inc="sv")
                    prev_e = ev


def _l2_linattn(c, ident, b):
    nc = c.nc
    NH, DH = 16, 64
    with Ph(c, name=_phname()) as p:
        p._ident = ident
        qt = c._l2_qt; kt = c._l2_kt; ktok = c._l2_ktok; vt = c._l2_vt
        trimask = p.sb("a2_tri", [128, 128], BF16)
        ones_c = p.sb("a2_ones", [128, 1], BF16)
        am = p.sb("a2_am", [128, 128], BF16)
        amT = p.sb("a2_amT", [128, 128], BF16)
        ssb = p.sb("a2_s", [128, 64], BF16)
        zsb = p.sb("a2_z", [128, 1], BF16)
        den = p.sb("a2_den", [128, 1], F32)
        ao = p.sb("a2_ao", [128, 64], BF16)
        aopad = p.sb("a2_aopad", [128, 128], BF16)
        ps_a = p.ps("a2_ps_a", [128, 128], F32)
        ps_t = p.ps("a2_ps_t", [128, 128], BF16)
        ps_o = p.ps("a2_ps_o", [128, 64], F32)
        ps_d = p.ps("a2_ps_d", [128, 1], F32)
        ps_S = p.ps("a2_ps_S", [128, 64], F32)
        ps_z = p.ps("a2_ps_z", [128, 1], F32)
        p.do("gpsimd", lambda g: g.memset(trimask[:], 1.0))
        gm = p.do("gpsimd", lambda g: g.affine_select(
            out=trimask[:], in_=trimask[:], compare_op=ALU.is_ge, fill=0.0,
            base=0, pattern=[[-1, 128]], channel_multiplier=1), inc="sg")
        p.do("vector", lambda v: v.memset(ones_c[:], 1.0))
        p.do("vector", lambda v: v.memset(aopad[:], 0.0))
        mzv = p.do("vector", lambda v: v.memset(den[:], 0.0), inc="sv")
        p.wait("vector", gm)
        p.wait("tensor", mzv)
        for h in range(NH):
            pb = (h % 2) * 64
            jb = h // 2
            prev_upd = None
            prev_cv2 = None
            for ch in range(NTT):
                t0 = ch * 128
                if ch > 0:
                    p.wait("vector", prev_upd)
                    sc1 = p.do("vector", lambda v, pb=pb: v.tensor_copy(
                        out=ssb[pb:pb + 64, :], in_=ps_S[pb:pb + 64, :]), inc="sv")
                    sc2 = p.do("vector", lambda v, pb=pb: v.tensor_copy(
                        out=zsb[pb:pb + 64, :], in_=ps_z[pb:pb + 64, :]), inc="sv")
                    p.wait("tensor", sc2)
                mm = p.do("tensor", lambda t, pb=pb, jb=jb, t0=t0: t.matmul(
                    ps_a[:], qt[pb:pb + 64, jb * 2048 + t0: jb * 2048 + t0 + 128],
                    kt[pb:pb + 64, jb * 2048 + t0: jb * 2048 + t0 + 128],
                    start=True, stop=True), inc="st")
                p.wait("vector", mm)
                msk = p.do("vector", lambda v: v.tensor_tensor(
                    out=am[:], in0=ps_a[:], in1=trimask[:], op=ALU.mult), inc="sv")
                p.wait("tensor", msk)
                tr = p.do("tensor", lambda t: t.transpose(ps_t[:], am[:], p._ident[:]),
                          inc="st")
                p.wait("vector", tr)
                cv = p.do("vector", lambda v: v.tensor_copy(out=amT[:], in_=ps_t[:]), inc="sv")
                p.wait("tensor", cv)
                mo = p.do("tensor", lambda t, h=h, ch=ch, t0=t0: t.matmul(
                    ps_o[:], amT[:], vt[:, ch * 1024 + h * 64: ch * 1024 + h * 64 + 64],
                    start=True, stop=(ch == 0)), inc="st")
                md = p.do("tensor", lambda t, ch=ch: t.matmul(
                    ps_d[:], amT[:], ones_c[:], start=True, stop=(ch == 0)), inc="st")
                if ch > 0:
                    mo = p.do("tensor", lambda t, pb=pb, jb=jb, t0=t0: t.matmul(
                        ps_o[:], qt[pb:pb + 64, jb * 2048 + t0: jb * 2048 + t0 + 128],
                        ssb[pb:pb + 64, :], start=False, stop=True), inc="st")
                    md = p.do("tensor", lambda t, pb=pb, jb=jb, t0=t0: t.matmul(
                        ps_d[:], qt[pb:pb + 64, jb * 2048 + t0: jb * 2048 + t0 + 128],
                        zsb[pb:pb + 64, :], start=False, stop=True), inc="st")
                tp = None if pb == 0 else (0, pb)
                upd1 = p.do("tensor", lambda t, h=h, ch=ch, pb=pb, tp=tp: t.matmul(
                    ps_S[pb:pb + 64, :], ktok[:, ch * 1024 + h * 64: ch * 1024 + h * 64 + 64],
                    vt[:, ch * 1024 + h * 64: ch * 1024 + h * 64 + 64],
                    start=(ch == 0), stop=(ch == NTT - 1), tile_position=tp), inc="st")
                upd2 = p.do("tensor", lambda t, h=h, ch=ch, pb=pb, tp=tp: t.matmul(
                    ps_z[pb:pb + 64, :], ktok[:, ch * 1024 + h * 64: ch * 1024 + h * 64 + 64],
                    ones_c[:], start=(ch == 0), stop=(ch == NTT - 1), tile_position=tp),
                    inc="st")
                prev_upd = upd2
                p.wait("vector", upd2)
                d1 = p.do("vector", lambda v: v.tensor_scalar(
                    out=den[:], in0=ps_d[:], scalar1=1e-6, scalar2=None, op0=ALU.add),
                    inc="sv")
                p.wait("vector", d1)
                d2 = p.do("vector", lambda v: v.reciprocal(den[:], den[:]), inc="sv")
                p.wait("vector", d2)
                ev = p.do("vector", lambda v: v.tensor_tensor(
                    out=ao[:], in0=ps_o[:], in1=den[:, 0:1].to_broadcast([128, 64]),
                    op=ALU.mult), inc="sv")
                p.wait("vector", ev)
                cp2 = p.do("vector", lambda v, pb=pb: v.tensor_copy(
                    out=aopad[:, pb:pb + 64], in_=ao[:]), inc="sv")
                zp = p.do("vector", lambda v, pb=pb: v.memset(
                    aopad[:, (64 if pb == 0 else 0):(128 if pb == 0 else 64)], 0.0), inc="sv")
                p.wait("tensor", cp2)
                p.wait("tensor", zp)
                tr2 = p.do("tensor", lambda t: t.transpose(ps_t[:], aopad[:], p._ident[:]),
                           inc="st")
                p.wait("vector", tr2)
                cv2 = p.do("vector", lambda v, pb=pb: v.tensor_copy(
                    out=am[pb:pb + 64, :], in_=ps_t[pb:pb + 64, :]), inc="sv")
                p.wait("sync", cv2)
                sd = p.dma("sync", c.dram["l2_aoT"][jb * 128 + pb:jb * 128 + pb + 64,
                                                    t0:t0 + 128], am[pb:pb + 64, :])
                p.wait("vector", sd)
                p.wait("tensor", sd)
                prev_cv2 = cv2


def _l2_out(c, ident, b):
    with Ph(c, name=_phname()) as p:
        p._ident = ident
        wo = p.sb("w2_wo", [128, 8 * 512], BF16)
        wld = p.sb("w2_wld", [128, 512], F32)
        aot = p.sb("w2_aot", [128, 8 * 128], BF16)
        sev = p.sb("w2_sev", [128, 512], F32)
        ps = p.ps("w2_ps", [128, 512], F32)
        for half in range(2):
            p.fence_all("sync")
            for kb in range(8):
                ld = p.dma("sync", wld[:], c.dram["L2_attn_Wo"][kb * 128:(kb + 1) * 128,
                                                                half * 512:(half + 1) * 512])
                p.wait("vector", ld)
                cv = p.do("vector", lambda v, kb=kb: v.tensor_copy(
                    out=wo[:, kb * 512:(kb + 1) * 512], in_=wld[:]), inc="sv")
                p.wait("sync", cv)
            p.wait("tensor", cv)
            for tt in range(NTT):
                r0 = tt * 128
                p.fence_all("gpsimd")
                al = None
                for kb in range(8):
                    al = p.dma("gpsimd", aot[:, kb * 128:(kb + 1) * 128],
                               c.dram["l2_aoT"][kb * 128:(kb + 1) * 128, r0:r0 + 128])
                p.wait("tensor", al)
                mm = None
                for kb in range(8):
                    mm = p.do("tensor", lambda t, kb=kb: t.matmul(
                        ps[:], aot[:, kb * 128:(kb + 1) * 128], wo[:, kb * 512:(kb + 1) * 512],
                        start=(kb == 0), stop=(kb == 7)), inc="st" if kb == 7 else None)
                p.wait("vector", mm)
                cv = p.do("vector", lambda v: v.tensor_copy(out=sev[:], in_=ps[:]), inc="sv")
                p.wait("sync", cv)
                sd = p.dma("sync", c.dram["s_buf"][b, r0:r0 + 128,
                                                   half * 512:(half + 1) * 512], sev[:])
                p.wait("vector", sd)
                p.wait("tensor", cv)

def ph_l2_ffn(c, ident):
    _l2_dequant(c, ident)
    c.barrier()
    for b in range(B_PER_CORE):
        _l2_x2(c, ident, b)
        c.barrier()
        _l2_fc1(c, ident, b)
        c.barrier()
        _l2_fc2(c, ident, b)
        c.barrier()


def _l2_dequant(c, ident):
    nc = c.nc
    with Ph(c, name=_phname()) as p:
        p._ident = ident
        qi = p.sb("d2_qi", [128, 4096], I32)
        qf = p.sb("d2_qf", [128, 4096], F32)
        scb = p.sb("d2_scb", [128, 4096], F32)
        srow = p.sb("d2_srow", [1, 4096], F32)
        ones_r = p.sb("d2_ones", [1, 128], F32)
        ob = p.sb("d2_ob", [128, 4096], BF16)
        ps = p.ps("d2_ps", [128, 512], F32)
        p.do("vector", lambda v: v.memset(ones_r[:], 1.0))

        def dq(qname, sname, dname, nrows, ncols):
            for kb in range(nrows // 128):
                p.fence_all("sync")
                l1 = p.dma("sync", qi[:, 0:ncols], c.dram[qname][kb * 128:(kb + 1) * 128, :])
                l2 = p.dma("sync", srow[:, 0:ncols], c.dram[sname][kb:kb + 1, :])
                p.wait("tensor", l2)
                bcl = None
                for sl in range(ncols // 512):
                    bm = p.do("tensor", lambda t, sl=sl: t.matmul(
                        ps[:], ones_r[0:1, :], srow[0:1, sl * 512:(sl + 1) * 512],
                        start=True, stop=True), inc="st")
                    p.wait("vector", bm)
                    bcl = p.do("vector", lambda v, sl=sl: v.tensor_copy(
                        out=scb[:, sl * 512:(sl + 1) * 512], in_=ps[:]), inc="sv")
                    p.wait("tensor", bcl)
                p.wait("vector", l1)
                c1 = p.do("vector", lambda v: v.tensor_copy(out=qf[:, 0:ncols],
                                                            in_=qi[:, 0:ncols]), inc="sv")
                p.wait("vector", c1)
                c2 = p.do("vector", lambda v: v.tensor_scalar(
                    out=qf[:, 0:ncols], in0=qf[:, 0:ncols], scalar1=-128.0, scalar2=None,
                    op0=ALU.add), inc="sv")
                p.wait("vector", c2)
                p.wait("vector", bcl)
                c3 = p.do("vector", lambda v: v.tensor_tensor(
                    out=ob[:, 0:ncols], in0=qf[:, 0:ncols], in1=scb[:, 0:ncols],
                    op=ALU.mult), inc="sv")
                p.wait("sync", c3)
                sd = p.dma("sync", c.dram[dname][kb * 128:(kb + 1) * 128, :], ob[:, 0:ncols])
                p.wait("vector", sd)

        dq("L2_ffn_fc1_q", "L2_ffn_fc1_scales", "l2_w1", 1024, 4096)
        dq("L2_ffn_fc2_q", "L2_ffn_fc2_scales", "l2_w2", 4096, 1024)


def _l2_x2(c, ident, b):
    nc = c.nc
    with Ph(c, name=_phname()) as p:
        p._ident = ident
        xt = p.sb("x2_x", [128, D], BF16)
        xm = p.sb("x2_xm", [128, D], F32)
        sq = p.sb("x2_sq", [128, D], F32)
        x2 = p.sb("x2_x2", [128, D], BF16)
        gamb = p.sb("x2_gamb", [128, D], F32)
        betb = p.sb("x2_betb", [128, D], F32)
        grow = p.sb("x2_grow", [1, D], F32)
        st = p.sb("x2_st", [128, 8], F32)
        tb = p.sb("x2_tb", [128, 8 * 128], BF16)
        ones_r = p.sb("x2_ones", [1, 128], F32)
        ps_b = p.ps("x2_ps_b", [128, 512], F32)
        ps_t = p.ps("x2_ps_t", [128, 128], BF16)
        p.do("vector", lambda v: v.memset(ones_r[:], 1.0))
        last_mm = None
        for nm, dst in (("L2_ffn_ln_gamma", gamb), ("L2_ffn_ln_beta", betb)):
            p.wait("sync", last_mm)
            ld = p.dma("sync", grow[:], c.dram[nm][:, :])
            p.wait("tensor", ld)
            for sl in range(2):
                mm = p.do("tensor", lambda t, sl=sl: t.matmul(
                    ps_b[:], ones_r[0:1, :], grow[0:1, sl * 512:(sl + 1) * 512],
                    start=True, stop=True), inc="st")
                last_mm = mm
                p.wait("vector", mm)
                cv = p.do("vector", lambda v, sl=sl, dst=dst: v.tensor_copy(
                    out=dst[:, sl * 512:(sl + 1) * 512], in_=ps_b[:]), inc="sv")
                p.wait("tensor", cv)
        for tt in range(NTT):
            r0 = tt * 128
            p.fence_all("sync")
            ld = p.dma("sync", xt[:], c.dram["xn_tok"][b, r0:r0 + 128, :])
            p.wait("vector", ld)
            rs = p.do("vector", lambda v: v.reduce_sum(st[:, 0:1], xt[:], axis=AX.X), inc="sv")
            p.wait("vector", rs)
            nm_ = p.do("vector", lambda v: v.tensor_scalar(
                out=st[:, 1:2], in0=st[:, 0:1], scalar1=1.0 / D, scalar2=None,
                op0=ALU.mult), inc="sv")
            p.wait("vector", nm_)
            x1_ = p.do("vector", lambda v: v.tensor_tensor(
                out=xm[:], in0=xt[:], in1=st[:, 1:2].to_broadcast([128, D]),
                op=ALU.subtract), inc="sv")
            p.wait("scalar", x1_)
            a2 = p.do("scalar", lambda s: s.activation(out=sq[:], in_=xm[:], func=AF.Square,
                                                       accum_out=st[:, 2:3]), inc="ss")
            p.wait("vector", a2)
            v1 = p.do("vector", lambda v: v.tensor_scalar(
                out=st[:, 3:4], in0=st[:, 2:3], scalar1=1.0 / D, scalar2=1e-5,
                op0=ALU.mult, op1=ALU.add), inc="sv")
            p.wait("scalar", v1)
            a3 = p.do("scalar", lambda s: s.activation(out=st[:, 4:5], in_=st[:, 3:4],
                                                       func=AF.Sqrt), inc="ss")
            p.wait("vector", a3)
            rc = p.do("vector", lambda v: v.reciprocal(st[:, 5:6], st[:, 4:5]), inc="sv")
            p.wait("vector", rc)
            n1 = p.do("vector", lambda v: v.tensor_tensor(
                out=xm[:], in0=xm[:], in1=st[:, 5:6].to_broadcast([128, D]), op=ALU.mult),
                inc="sv")
            p.wait("vector", n1)
            n2 = p.do("vector", lambda v: v.tensor_tensor(
                out=xm[:], in0=xm[:], in1=gamb[:], op=ALU.mult))
            n3 = p.do("vector", lambda v: v.tensor_tensor(
                out=x2[:], in0=xm[:], in1=betb[:], op=ALU.add), inc="sv")
            p.wait("tensor", n3)
            cv = None
            for k in range(8):
                tr = p.do("tensor", lambda t, k=k: t.transpose(
                    ps_t[:], x2[:, k * 128:(k + 1) * 128], p._ident[:]), inc="st")
                p.wait("vector", tr)
                cv = p.do("vector", lambda v, k=k: v.tensor_copy(
                    out=tb[:, k * 128:(k + 1) * 128], in_=ps_t[:]), inc="sv")
                p.wait("tensor", cv)
            p.wait("sync", cv)
            sd = p.dma("sync", c.dram["l2_x2T"][:, r0:r0 + 128].rearrange("(k p) t -> p k t", p=128),
                       tb[:].rearrange("p (k t) -> p k t", k=8))
            p.wait("vector", sd)


def _l2_fc1(c, ident, b):
    nc = c.nc
    with Ph(c, name=_phname()) as p:
        p._ident = ident
        w1c = p.sb("f2_w1c", [128, 8 * 128], BF16)
        x2c = p.sb("f2_x2c", [128, 8 * 512], BF16)
        b1c = p.sb("f2_b1c", [128, 1], F32)
        hev = p.sb("f2_hev", [128, 512], BF16)
        ps = p.ps("f2_ps", [128, 512], F32)
        for nb in range(32):
            p.fence_all("sync")
            wl = p.dma("sync", w1c[:].rearrange("p (k t) -> p k t", k=8),
                       c.dram["l2_w1"][:, nb * 128:(nb + 1) * 128]
                       .rearrange("(k p) t -> p k t", p=128))
            bl = p.dma("sync", b1c[:],
                       c.dram["L2_ffn_fc1_bias"][0:1, nb * 128:(nb + 1) * 128]
                       .rearrange("a b -> b a"), allow_slow_non_contiguous=True)
            p.wait("tensor", wl)
            p.wait("scalar", bl)
            prev_ev = None
            for ts4 in range(4):
                t0 = ts4 * 512
                p.fence_all("gpsimd")
                xl = p.dma("gpsimd", x2c[:].rearrange("p (k t) -> p k t", k=8),
                           c.dram["l2_x2T"][:, t0:t0 + 512].rearrange("(k p) t -> p k t", p=128))
                p.wait("tensor", xl)
                p.wait("tensor", prev_ev)
                mm = None
                for kb in range(8):
                    mm = p.do("tensor", lambda t, kb=kb: t.matmul(
                        ps[:], w1c[:, kb * 128:(kb + 1) * 128], x2c[:, kb * 512:(kb + 1) * 512],
                        start=(kb == 0), stop=(kb == 7)), inc="st" if kb == 7 else None)
                p.wait("scalar", mm)
                ev = p.do("scalar", lambda s: s.activation(
                    out=hev[:], in_=ps[:], func=AF.Silu, bias=b1c[:, 0:1], scale=1.0),
                    inc="ss")
                prev_ev = ev
                p.wait("sync", ev)
                sd = p.dma("sync", c.dram["l2_hT"][nb * 128:(nb + 1) * 128, t0:t0 + 512],
                           hev[:])
                p.wait("scalar", sd)


def _l2_fc2(c, ident, b):
    nc = c.nc
    with Ph(c, name=_phname()) as p:
        p._ident = ident
        w2c = p.sb("g2_w2c", [128, 32 * 512], BF16)
        wld = p.sb("g2_wld", [128, 512], BF16)
        htc = p.sb("g2_htc", [128, 32 * 128], BF16)
        b2r = p.sb("g2_b2r", [1, D], F32)
        xnt_ = p.sb("g2_xn", [128, 512], BF16)
        gev = p.sb("g2_gev", [128, 512], F32)
        ones_r = p.sb("g2_ones", [1, 128], F32)
        ps = p.ps("g2_ps", [128, 512], F32)
        p.do("vector", lambda v: v.memset(ones_r[:], 1.0))
        lb = p.dma("sync", b2r[:], c.dram["L2_ffn_fc2_bias"][:, :])
        p.wait("tensor", lb)
        for half in range(2):
            p.fence_all("sync")
            wl = p.dma("sync", w2c[:].rearrange("p (k t) -> p k t", k=32),
                       c.dram["l2_w2"][:, half * 512:(half + 1) * 512]
                       .rearrange("(k p) t -> p k t", p=128))
            p.wait("tensor", wl)
            for tt in range(NTT):
                r0 = tt * 128
                p.fence_all("gpsimd")
                hl = p.dma("gpsimd", htc[:].rearrange("p (k t) -> p k t", k=32),
                           c.dram["l2_hT"][:, r0:r0 + 128].rearrange("(k p) t -> p k t", p=128))
                p.wait("tensor", hl)
                mm = None
                for kb in range(32):
                    mm = p.do("tensor", lambda t, kb=kb, half=half: t.matmul(
                        ps[:], htc[:, kb * 128:(kb + 1) * 128], w2c[:, kb * 512:(kb + 1) * 512],
                        start=(kb == 0), stop=False), inc=None)
                mm = p.do("tensor", lambda t, half=half: t.matmul(
                    ps[:], ones_r[0:1, :], b2r[0:1, half * 512:(half + 1) * 512],
                    start=False, stop=True), inc="st")
                xl = p.dma("sync", xnt_[:], c.dram["xn_tok"][b, r0:r0 + 128,
                                                             half * 512:(half + 1) * 512])
                p.wait("vector", mm)
                p.wait("vector", xl)
                ev = p.do("vector", lambda v: v.scalar_tensor_tensor(
                    out=gev[:], in0=ps[:], scalar=0.5, in1=xnt_[:],
                    op0=ALU.mult, op1=ALU.add), inc="sv")
                p.wait("sync", ev)
                sd = p.dma("sync", c.dram["g_buf"][b, r0:r0 + 128,
                                                   half * 512:(half + 1) * 512], gev[:])
                p.wait("vector", sd)
                p.wait("tensor", ev)

def ph_l3_attn(c, ident):
    nc = c.nc
    NH, DH = 8, 128
    with ExitStack() as outer:
        oe = outer.enter_context
        qt_sb = oe(nc.sbuf_tensor("a3_qt", [128, 8 * 2048], BF16))
        kt_sb = oe(nc.sbuf_tensor("a3_kt", [128, 8 * 2048], BF16))
        v_sb = oe(nc.sbuf_tensor("a3_v", [128, 16 * 1024], BF16))
        for b in range(B_PER_CORE):
            p = Ph(c, name=_phname())
            p._ident = ident
            wq_sb = p.sb("a3_wqkv", [128, 8 * 3072], BF16)
            wld = p.sb("a3_wld", [128, 1536], F32)
            bqr = p.sb("a3_bq", [1, 3072], F32)
            ones_r = p.sb("a3_ones", [1, 128], F32)
            xnt = p.sb("a3_xnt", [128, 8 * 128], BF16)
            evb = p.sb("a3_evb", [128, 512], BF16)
            ps_qkv = [p.ps(f"a3_ps{i}", [128, 512], F32) for i in range(6)]
            ps_tp = p.ps("a3_ps_tp", [128, 128], BF16)
            p.do("vector", lambda v: v.memset(ones_r[:], 1.0))
            for kb in range(8):
                for ch in range(2):
                    ld = p.dma("sync", wld[:], c.dram["L3_attn_w_qkvT"][kb * 128:(kb + 1) * 128,
                                                                        ch * 1536:(ch + 1) * 1536])
                    p.wait("vector", ld)
                    cv = p.do("vector", lambda v, kb=kb, ch=ch: v.tensor_copy(
                        out=wq_sb[:, kb * 3072 + ch * 1536: kb * 3072 + (ch + 1) * 1536],
                        in_=wld[:]), inc="sv")
                    p.wait("sync", cv)
            lb = p.dma("sync", bqr[:], c.dram["L3_attn_b_qkv"][:, :])
            p.wait("tensor", lb)
            p.wait("tensor", cv)
            prev_ev = None
            for tt in range(NTT):
                r0 = tt * 128
                p.fence_all("gpsimd")
                xl = p.dma("gpsimd", xnt[:].rearrange("p (k t) -> p k t", k=8),
                           c.dram["xnT"][b, :, r0:r0 + 128].rearrange("(k p) t -> p k t", p=128))
                p.wait("tensor", xl)
                p.wait("tensor", prev_ev)
                mm = None
                for sl in range(6):
                    for kb in range(8):
                        mm = p.do("tensor", lambda t, kb=kb, sl=sl: t.matmul(
                            ps_qkv[sl][:], xnt[:, kb * 128:(kb + 1) * 128],
                            wq_sb[:, kb * 3072 + sl * 512: kb * 3072 + (sl + 1) * 512],
                            start=(kb == 0), stop=False), inc=None)
                    mm = p.do("tensor", lambda t, sl=sl: t.matmul(
                        ps_qkv[sl][:], ones_r[0:1, :], bqr[0:1, sl * 512:(sl + 1) * 512],
                        start=False, stop=True), inc="st")
                p.wait("vector", mm)
                lastv = None
                for sl in range(4):
                    dst = qt_sb if sl < 2 else kt_sb
                    cvb = p.do("vector", lambda v, sl=sl: v.tensor_copy(
                        out=evb[:], in_=ps_qkv[sl][:]), inc="sv")
                    p.wait("tensor", cvb)
                    for j in range(4):
                        h = (sl % 2) * 4 + j
                        tr = p.do("tensor", lambda t, j=j: t.transpose(
                            ps_tp[:], evb[:, j * 128:(j + 1) * 128], p._ident[:]), inc="st")
                        p.wait("vector", tr)
                        cv2 = p.do("vector", lambda v, h=h, r0=r0, dst=dst: v.tensor_copy(
                            out=dst[:, h * 2048 + r0: h * 2048 + r0 + 128], in_=ps_tp[:]),
                            inc="sv")
                        p.wait("tensor", cv2)
                    lastv = cv2
                for sl in range(2):
                    lastv = p.do("vector", lambda v, sl=sl, tt=tt: v.tensor_copy(
                        out=v_sb[:, tt * 1024 + sl * 512: tt * 1024 + (sl + 1) * 512],
                        in_=ps_qkv[4 + sl][:]), inc="sv")
                prev_ev = lastv
            p.close()
            c.barrier()
            p = Ph(c, name=_phname())
            p._ident = ident
            mskL = p.sb("a3_mskL", [128, 128], F32)
            mskR = p.sb("a3_mskR", [128, 128], F32)
            probs = p.sb("a3_probs", [128, 384], BF16)
            probsT = p.sb("a3_probsT", [128, 128], BF16)
            ao = p.sb("a3_ao", [128, 128], BF16)
            stat = p.sb("a3_stat", [128, 4], F32)
            ps_sc = p.ps("a3_ps_sc", [128, 384], F32)
            ps_tp = p.ps("a3_ps_tp2", [128, 128], BF16)
            ps_ao = p.ps("a3_ps_ao", [128, 128], F32)
            p.do("gpsimd", lambda g: g.memset(mskL[:], 0.0))
            g1 = p.do("gpsimd", lambda g: g.affine_select(
                out=mskL[:], in_=mskL[:], compare_op=ALU.is_ge, fill=-1e30,
                base=-1, pattern=[[1, 128]], channel_multiplier=-1), inc="sg")
            p.do("gpsimd", lambda g: g.memset(mskR[:], 0.0))
            g2 = p.do("gpsimd", lambda g: g.affine_select(
                out=mskR[:], in_=mskR[:], compare_op=ALU.is_ge, fill=-1e30,
                base=-1, pattern=[[-1, 128]], channel_multiplier=1), inc="sg")
            p.wait("vector", g2)
            prev_ex = None
            prev_tr = None
            prev_cv2 = None
            for h in range(NH):
                for qi in range(NTT):
                    q0 = qi * 128
                    k_lo = max(0, qi - 1)
                    k_hi = min(NTT - 1, qi + 1)
                    nk = (k_hi - k_lo + 1) * 128
                    p.wait("tensor", prev_ex)
                    mm = p.do("tensor", lambda t, h=h, q0=q0, k_lo=k_lo, nk=nk: t.matmul(
                        ps_sc[:, 0:nk],
                        qt_sb[:, h * 2048 + q0: h * 2048 + q0 + 128],
                        kt_sb[:, h * 2048 + k_lo * 128: h * 2048 + k_lo * 128 + nk],
                        start=True, stop=True), inc="st")
                    p.wait("vector", mm)
                    mtok = mm
                    if qi - 1 >= 0:
                        mtok = p.do("vector", lambda v: v.tensor_tensor(
                            out=ps_sc[:, 0:128], in0=ps_sc[:, 0:128], in1=mskL[:],
                            op=ALU.add), inc="sv")
                    if qi + 1 <= NTT - 1:
                        off = (k_hi - k_lo) * 128
                        mtok = p.do("vector", lambda v, off=off: v.tensor_tensor(
                            out=ps_sc[:, off:off + 128], in0=ps_sc[:, off:off + 128],
                            in1=mskR[:], op=ALU.add), inc="sv")
                    p.wait("vector", mtok)
                    r1 = p.do("vector", lambda v, nk=nk: v.reduce_max(
                        stat[:, 0:1], ps_sc[:, 0:nk], axis=AX.X, negate=True), inc="sv")
                    p.wait("vector", r1)
                    r2 = p.do("vector", lambda v: v.tensor_scalar(
                        out=stat[:, 1:2], in0=stat[:, 0:1], scalar1=1.0 / 32.0, scalar2=None,
                        op0=ALU.mult), inc="sv")
                    p.wait("scalar", r2)
                    p.wait("scalar", prev_tr)
                    ex = p.do("scalar", lambda s, nk=nk: s.activation(
                        out=probs[:, 0:nk], in_=ps_sc[:, 0:nk], func=AF.Exp,
                        bias=stat[:, 1:2], scale=1.0 / 32.0, accum_out=stat[:, 2:3]),
                        inc="ss")
                    prev_ex = ex
                    p.wait("vector", ex)
                    r3 = p.do("vector", lambda v: v.reciprocal(stat[:, 3:4], stat[:, 2:3]),
                              inc="sv")
                    p.wait("tensor", ex)
                    amm = None
                    prev_amm = None
                    for j in range(k_lo, k_hi + 1):
                        jj = j - k_lo
                        p.wait("tensor", prev_cv2)
                        tr = p.do("tensor", lambda t, jj=jj: t.transpose(
                            ps_tp[:], probs[:, jj * 128:(jj + 1) * 128], p._ident[:]),
                            inc="st")
                        prev_tr = tr
                        prev_cv2 = None
                        p.wait("vector", tr)
                        p.wait("vector", prev_amm)
                        cv = p.do("vector", lambda v: v.tensor_copy(out=probsT[:], in_=ps_tp[:]),
                                  inc="sv")
                        p.wait("tensor", cv)
                        amm = p.do("tensor", lambda t, j=j, jj=jj, h=h, k_lo=k_lo, k_hi=k_hi:
                                   t.matmul(
                            ps_ao[:], probsT[:],
                            v_sb[:, j * 1024 + (h % 8) * 128: j * 1024 + (h % 8) * 128 + 128],
                            start=(j == k_lo), stop=(j == k_hi)), inc="st")
                        prev_amm = amm
                    p.wait("vector", amm)
                    p.wait("vector", r3)
                    ev = p.do("vector", lambda v: v.tensor_tensor(
                        out=ao[:], in0=ps_ao[:], in1=stat[:, 3:4].to_broadcast([128, 128]),
                        op=ALU.mult), inc="sv")
                    p.wait("tensor", ev)
                    tr2 = p.do("tensor", lambda t: t.transpose(ps_tp[:], ao[:], p._ident[:]),
                               inc="st")
                    p.wait("vector", tr2)
                    cv2 = p.do("vector", lambda v: v.tensor_copy(out=probsT[:], in_=ps_tp[:]),
                               inc="sv")
                    prev_cv2 = cv2
                    p.wait("sync", cv2)
                    sd = p.dma("sync", c.dram["l3_aoT"][h * 128:(h + 1) * 128, q0:q0 + 128],
                               probsT[:])
                    p.wait("vector", sd)
                    p.wait("tensor", sd)
            p.close()
            c.barrier()
            p = Ph(c, name=_phname())
            p._ident = ident
            wo_sb = p.sb("a3_wo", [128, 8 * 512], BF16)
            wld = p.sb("a3_wld2", [128, 512], F32)
            aot = p.sb("a3_aot", [128, 8 * 128], BF16)
            bpr = p.sb("a3_bp", [1, 1024], F32)
            ones_r = p.sb("a3_ones2", [1, 128], F32)
            sev = p.sb("a3_sev", [128, 512], F32)
            ps = p.ps("a3_ps_o", [128, 512], F32)
            p.do("vector", lambda v: v.memset(ones_r[:], 1.0))
            lb = p.dma("sync", bpr[:], c.dram["L3_attn_b_proj"][:, :])
            p.wait("tensor", lb)
            for half in range(2):
                p.fence_all("sync")
                wld8 = wld
                ld = p.dma("sync", wld8[:, 0:512], c.dram["L3_attn_w_projT"][0:128,
                           half * 512:(half + 1) * 512])
                cv = None
                for kb in range(8):
                    ld = p.dma("sync", wld[:, 0:512],
                               c.dram["L3_attn_w_projT"][kb * 128:(kb + 1) * 128,
                                                         half * 512:(half + 1) * 512])
                    p.wait("vector", ld)
                    cv = p.do("vector", lambda v, kb=kb: v.tensor_copy(
                        out=wo_sb[:, kb * 512:(kb + 1) * 512], in_=wld[:, 0:512]), inc="sv")
                    p.wait("sync", cv)
                p.wait("tensor", cv)
                for tt in range(NTT):
                    r0 = tt * 128
                    p.fence_all("gpsimd")
                    al = p.dma("gpsimd", aot[:].rearrange("p (k t) -> p k t", k=8),
                               c.dram["l3_aoT"][:, r0:r0 + 128]
                               .rearrange("(k p) t -> p k t", p=128))
                    p.wait("tensor", al)
                    mm = None
                    for kb in range(8):
                        mm = p.do("tensor", lambda t, kb=kb: t.matmul(
                            ps[:], aot[:, kb * 128:(kb + 1) * 128],
                            wo_sb[:, kb * 512:(kb + 1) * 512],
                            start=(kb == 0), stop=False), inc=None)
                    mm = p.do("tensor", lambda t, half=half: t.matmul(
                        ps[:], ones_r[0:1, :], bpr[0:1, half * 512:(half + 1) * 512],
                        start=False, stop=True), inc="st")
                    p.wait("vector", mm)
                    cve = p.do("vector", lambda v: v.tensor_copy(out=sev[:], in_=ps[:]),
                               inc="sv")
                    p.wait("sync", cve)
                    sd = p.dma("sync", c.dram["s_buf"][b, r0:r0 + 128,
                                                       half * 512:(half + 1) * 512], sev[:])
                    p.wait("vector", sd)
                    p.wait("tensor", cve)
            p.close()
            c.barrier()

def ph_l3_ffn(c, ident):
    _l3_router(c, ident)
    c.barrier()
    _l3_threshold(c, ident)
    c.barrier()
    with Ph(c, name=_phname()) as p:
        zt = p.sb("z3_z", [128, D], F32)
        mz = p.do("vector", lambda v: v.memset(zt[:], 0.0), inc="sv")
        p.wait("sync", mz)
        for tt in range(NTT):
            p.dma("sync", c.dram["l3_zero"][tt * 128:(tt + 1) * 128, :], zt[:])
    c.barrier()
    bs = [_L3FFN_B] if _L3FFN_B is not None else list(range(B_PER_CORE))
    for b in bs:
        for e in range(8):
            _l3_gather(c, ident, b, e)
            c.barrier()
            _l3_expert(c, ident, b, e)
            c.barrier()
        _l3_accum(c, ident, b)
        c.barrier()


def _l3_router(c, ident):
    nc = c.nc
    with Ph(c, name=_phname()) as p:
        p._ident = ident
        gwt = p.sb("r3_gwt", [128, 8 * 8], BF16)
        gwl = p.sb("r3_gwl", [128, 8], F32)
        xnt = p.sb("r3_xnt", [128, 8 * 128], BF16)
        pr = p.sb("r3_pr", [128, 128], F32)
        idf = p.sb("r3_idf", [128, 128], F32)
        st = p.sb("r3_st", [128, 4], F32)
        prT = p.sb("r3_prT", [16, 128], F32)
        ps_l = p.ps("r3_ps_l", [128, 8], F32)
        ps_T = p.ps("r3_ps_T", [128, 128], F32)
        p.do("gpsimd", lambda g: g.memset(idf[:], 0.0))
        gi = p.do("gpsimd", lambda g: g.affine_select(
            out=idf[:], in_=idf[:], compare_op=ALU.not_equal, fill=1.0,
            base=0, pattern=[[-1, 128]], channel_multiplier=1), inc="sg")
        p.wait("tensor", gi)
        mzp = p.do("vector", lambda v: v.memset(pr[:], 0.0), inc="sv")
        gwl8 = p.sb("r3_gwl8", [128, 64], F32)
        ld = p.dma("sync", gwl8[:].rearrange("p (k e) -> p k e", k=8),
                   c.dram["L3_ffn_WgT"][:, :].rearrange("(k p) e -> p k e", p=128))
        p.wait("vector", ld)
        cv = p.do("vector", lambda v: v.tensor_copy(out=gwt[:], in_=gwl8[:]), inc="sv")
        p.wait("tensor", cv)
        for b in range(B_PER_CORE):
            for tt in range(NTT):
                r0 = tt * 128
                p.fence_all("gpsimd")
                xl = p.dma("gpsimd", xnt[:].rearrange("p (k t) -> p k t", k=8),
                           c.dram["xnT"][b, :, r0:r0 + 128].rearrange("(k p) t -> p k t", p=128))
                p.wait("tensor", xl)
                mm = None
                for kb in range(8):
                    mm = p.do("tensor", lambda t, kb=kb: t.matmul(
                        ps_l[:], xnt[:, kb * 128:(kb + 1) * 128], gwt[:, kb * 8:(kb + 1) * 8],
                        start=(kb == 0), stop=(kb == 7)), inc="st" if kb == 7 else None)
                p.wait("vector", mm)
                r1 = p.do("vector", lambda v: v.reduce_max(st[:, 0:1], ps_l[:], axis=AX.X,
                                                           negate=True), inc="sv")
                p.wait("vector", r1)
                sb_ = p.do("vector", lambda v: v.tensor_tensor(
                    out=pr[:, 0:8], in0=ps_l[:], in1=st[:, 0:1].to_broadcast([128, 8]),
                    op=ALU.add), inc="sv")
                p.wait("scalar", sb_)
                ex = p.do("scalar", lambda s: s.activation(out=pr[:, 0:8], in_=pr[:, 0:8],
                                                           func=AF.Exp,
                                                           accum_out=st[:, 1:2]), inc="ss")
                p.wait("vector", ex)
                rc = p.do("vector", lambda v: v.reciprocal(st[:, 2:3], st[:, 1:2]), inc="sv")
                p.wait("vector", rc)
                pn = p.do("vector", lambda v: v.tensor_tensor(
                    out=pr[:, 0:8], in0=pr[:, 0:8], in1=st[:, 2:3].to_broadcast([128, 8]),
                    op=ALU.mult), inc="sv")
                p.wait("sync", pn)
                sd1 = p.dma("sync", c.dram["l3_p"][b, r0:r0 + 128, :], pr[:, 0:8])
                p.wait("tensor", pn)
                tmm = p.do("tensor", lambda t: t.matmul(
                    ps_T[:], pr[:], idf[:], start=True, stop=True), inc="st")
                p.wait("vector", tmm)
                cvt = p.do("vector", lambda v: v.tensor_copy(out=prT[0:16, :],
                                                             in_=ps_T[0:16, :]), inc="sv")
                p.wait("sync", cvt)
                sd2 = p.dma("sync", c.dram["l3_pT"][8 * b:8 * b + 8, r0:r0 + 128],
                            prT[0:8, :])
                p.wait("vector", sd2)
                p.wait("vector", sd1)
                p.wait("tensor", sd2)


def _l3_threshold(c, ident):
    nc = c.nc
    NIT = 45
    with Ph(c, name=_phname()) as p:
        p._ident = ident
        pT = p.sb("t3_pT", [16, T], F32)
        lo = p.sb("t3_lo", [16, 1], F32)
        hi = p.sb("t3_hi", [16, 1], F32)
        mid = p.sb("t3_mid", [16, 1], F32)
        cnt = p.sb("t3_cnt", [16, 1], F32)
        prd = p.sb("t3_prd", [16, 1], U8)
        gt = p.sb("t3_gt", [16, T], F32)
        rnk = p.sb("t3_rnk", [16, T], F32)
        slot = p.sb("t3_slot", [16, T], F32)
        big = p.sb("t3_big", [16, 1], F32)
        msku = p.sb("t3_msku", [16, T], U8)
        on1 = p.sb("t3_on1", [16, T], F32)
        idf16 = p.sb("t3_id16", [16, 16], F32)
        ps_T = p.ps("t3_ps", [128, 16], F32)
        ld = p.dma("sync", pT[:], c.dram["l3_pT"][:, :])
        p.do("gpsimd", lambda g: g.memset(idf16[:], 0.0))
        gi = p.do("gpsimd", lambda g: g.affine_select(
            out=idf16[:], in_=idf16[:], compare_op=ALU.not_equal, fill=1.0,
            base=0, pattern=[[-1, 16]], channel_multiplier=1), inc="sg")
        mz1 = p.do("vector", lambda v: v.memset(lo[:], 0.0), inc="sv")
        mz2 = p.do("vector", lambda v: v.memset(hi[:], 1.0), inc="sv")
        mz3 = p.do("vector", lambda v: v.memset(big[:], 512.0), inc="sv")
        p.do("vector", lambda v: v.memset(on1[:], 1.0))
        p.wait("vector", ld)
        p.wait("vector", gi)
        for it in range(NIT):
            m1 = p.do("vector", lambda v: v.tensor_tensor(out=mid[:], in0=lo[:], in1=hi[:],
                                                          op=ALU.add), inc="sv")
            p.wait("vector", m1)
            m2 = p.do("vector", lambda v: v.tensor_scalar(out=mid[:], in0=mid[:],
                                                          scalar1=0.5, scalar2=None,
                                                          op0=ALU.mult), inc="sv")
            p.wait("vector", m2)
            m3 = p.do("vector", lambda v: v.tensor_tensor(
                out=gt[:], in0=pT[:], in1=mid[:, 0:1].to_broadcast([16, T]),
                op=ALU.is_gt), inc="sv")
            p.wait("vector", m3)
            m4 = p.do("vector", lambda v: v.reduce_sum(cnt[:], gt[:], axis=AX.X), inc="sv")
            p.wait("vector", m4)
            m5 = p.do("vector", lambda v: v.tensor_scalar(
                out=prd[:], in0=cnt[:], scalar1=512.0, scalar2=None, op0=ALU.is_ge),
                inc="sv")
            p.wait("vector", m5)
            m6 = p.do("vector", lambda v: v.select(lo[:], prd[:], mid[:], lo[:]), inc="sv")
            m7 = p.do("vector", lambda v: v.select(hi[:], prd[:], hi[:], mid[:]), inc="sv")
            p.wait("vector", m7)
        f1 = p.do("vector", lambda v: v.tensor_tensor(
            out=gt[:], in0=pT[:], in1=lo[:, 0:1].to_broadcast([16, T]), op=ALU.is_gt),
            inc="sv")
        p.wait("vector", f1)
        f1b = p.do("vector", lambda v: v.tensor_copy(out=msku[:], in_=gt[:]), inc="sv")
        f2 = p.do("vector", lambda v: v.tensor_tensor_scan(
            out=rnk[:], data0=on1[:], data1=gt[:], initial=0.0,
            op0=ALU.mult, op1=ALU.add), inc="sv")
        p.wait("vector", f2)
        f3 = p.do("vector", lambda v: v.tensor_scalar(
            out=rnk[:], in0=rnk[:], scalar1=-1.0, scalar2=None, op0=ALU.add), inc="sv")
        p.wait("vector", f3)
        p.wait("vector", f1b)
        f4 = p.do("vector", lambda v: v.select(slot[:], msku[:], rnk[:],
                                               big[:, 0:1].to_broadcast([16, T])), inc="sv")
        p.wait("vector", f4)
        f5 = p.do("vector", lambda v: v.tensor_scalar(
            out=slot[:], in0=slot[:], scalar1=512.0, scalar2=None, op0=ALU.min), inc="sv")
        p.wait("tensor", f5)
        sT = p.sb("t3_sT", [128, 16], F32)
        prev_cv = None
        for chk in range(NTT):
            t0 = chk * 128
            p.wait("tensor", prev_cv)
            mm = p.do("tensor", lambda t, t0=t0: t.matmul(
                ps_T[:], slot[:, t0:t0 + 128], idf16[:], start=True, stop=True), inc="st")
            p.wait("vector", mm)
            cv = p.do("vector", lambda v: v.tensor_copy(out=sT[:], in_=ps_T[:]), inc="sv")
            prev_cv = cv
            p.wait("sync", cv)
            sd = p.dma("sync", c.dram["l3_slot"][t0:t0 + 128, :], sT[:])
            p.wait("vector", sd)


def _l3_gather(c, ident, b, e):
    nc = c.nc
    from concourse.bass_core_rust import IndirectOffsetOnAxis
    with Ph(c, name=_phname()) as p:
        p._ident = ident
        slt = p.sb("g3_slt", [128, 16], F32)
        prt = p.sb("g3_prt", [128, 8], F32)
        slu = p.sb("g3_slu", [128, 1], U32)
        xnt_ = p.sb("g3_xn", [128, D], BF16)
        gcol = p.sb("g3_g", [128, 1], F32)
        tid = p.sb("g3_tid", [128, 1], U32)
        col = 8 * b + e
        for tt in range(NTT):
            r0 = tt * 128
            p.fence_all("sync")
            p.fence_all("gpsimd")
            l1 = p.dma("sync", slt[:], c.dram["l3_slot"][r0:r0 + 128, :])
            l2 = p.dma("sync", xnt_[:], c.dram["xn_tok"][b, r0:r0 + 128, :])
            l3 = p.dma("sync", prt[:], c.dram["l3_p"][b, r0:r0 + 128, :])
            p.wait("vector", l1)
            p.wait("vector", l3)
            cu = p.do("vector", lambda v: v.tensor_copy(out=slu[:], in_=slt[:, col:col + 1]),
                      inc="sv")
            p.wait("vector", cu)
            cg = p.do("vector", lambda v: v.tensor_copy(out=gcol[:], in_=prt[:, e:e + 1]),
                      inc="sv")
            ti = p.do("gpsimd", lambda g, r0=r0, e=e: g.iota(
                tid[:], pattern=[[0, 1]], base=r0 + e * T, channel_multiplier=1), inc="sg")
            p.wait("gpsimd", cu)
            p.wait("gpsimd", cg)
            p.wait("gpsimd", l2)
            s1 = p.do("gpsimd", lambda g: g.indirect_dma_start(
                c.dram["l3_Xg"][:, :], IndirectOffsetOnAxis(slu[:, 0:1], 0),
                xnt_[:], None), inc="dg", n=16)
            s2 = p.do("gpsimd", lambda g: g.indirect_dma_start(
                c.dram["l3_G"][:, :], IndirectOffsetOnAxis(slu[:, 0:1], 0),
                gcol[:], None), inc="dg", n=16)
            p.wait("gpsimd", ti)
            s3 = p.do("gpsimd", lambda g: g.indirect_dma_start(
                c.dram["l3_tokid"][:, :], IndirectOffsetOnAxis(slu[:, 0:1], 0),
                tid[:], None), inc="dg", n=16)
            p.wait("sync", s3)
            p.wait("vector", s3)


def _l3_expert(c, ident, b, e):
    nc = c.nc
    from concourse.bass_core_rust import IndirectOffsetOnAxis
    GC1 = 0.035677408136300523
    GC0 = 0.7978845608028654
    with Ph(c, name=_phname()) as p:
        p._ident = ident
        w1 = p.sb("e3_w1", [128, 8 * 4096], BF16)
        w2h = p.sb("e3_w2", [128, 32 * 1024], BF16)
        wld = p.sb("e3_wld", [128, 2048], F32)
        xg = p.sb("e3_xg", [128, D], BF16)
        xgT = p.sb("e3_xgT", [128, 8 * 512], BF16)
        h1T = p.sb("e3_h1T", [128, 32 * 512], BF16)
        b1c = p.sb("e3_b1c", [128, 1], F32)
        b2r = p.sb("e3_b2r", [1, D], F32)
        gc = p.sb("e3_gc", [128, 1], F32)
        tidu = p.sb("e3_tid", [128, 1], U32)
        xb = p.sb("e3_xb", [128, 512], F32)
        sq = p.sb("e3_sq", [128, 512], F32)
        mm_ = p.sb("e3_mm", [128, 512], F32)
        wout = p.sb("e3_wout", [128, D], F32)
        ones_r = p.sb("e3_ones", [1, 128], F32)
        ps = p.ps("e3_ps", [128, 512], F32)
        ps_t = p.ps("e3_ps_t", [128, 128], BF16)
        ps_d = p.ps("e3_ps_d", [128, 512], F32)
        p.do("vector", lambda v: v.memset(ones_r[:], 1.0))
        for kb in range(8):
            for ch in range(2):
                ld = p.dma("sync", wld[:], c.dram["L3_ffn_W1"][e, kb * 128:(kb + 1) * 128,
                                                               ch * 2048:(ch + 1) * 2048])
                p.wait("vector", ld)
                cv = p.do("vector", lambda v, kb=kb, ch=ch: v.tensor_copy(
                    out=w1[:, kb * 4096 + ch * 2048: kb * 4096 + (ch + 1) * 2048],
                    in_=wld[:]), inc="sv")
                p.wait("sync", cv)
        lb2 = p.dma("sync", b2r[:], c.dram["L3_ffn_b2"][e:e + 1, :])
        p.wait("tensor", cv)
        prev_mm = None
        for tq in range(4):
            r0 = tq * 128
            p.fence_all("gpsimd")
            xl = p.dma("gpsimd", xg[:], c.dram["l3_Xg"][r0:r0 + 128, :])
            p.wait("tensor", xl)
            for k in range(8):
                tr = p.do("tensor", lambda t, k=k: t.transpose(
                    ps_t[:], xg[:, k * 128:(k + 1) * 128], p._ident[:]), inc="st")
                p.wait("vector", tr)
                cv2 = p.do("vector", lambda v, k=k, r0=r0: v.tensor_copy(
                    out=xgT[:, k * 512 + r0: k * 512 + r0 + 128], in_=ps_t[:]), inc="sv")
                p.wait("tensor", cv2)
        p.wait("tensor", cv2)
        prev_h = None
        for nb in range(32):
            lb1 = p.dma("gpsimd", b1c[:],
                        c.dram["L3_ffn_b1"][e:e + 1, nb * 128:(nb + 1) * 128]
                        .rearrange("a b -> b a"), allow_slow_non_contiguous=True)
            p.wait("vector", lb1)
            p.wait("tensor", prev_h)
            mm = None
            for kb in range(8):
                mm = p.do("tensor", lambda t, kb=kb, nb=nb: t.matmul(
                    ps[:], w1[:, kb * 4096 + nb * 128: kb * 4096 + nb * 128 + 128],
                    xgT[:, kb * 512:(kb + 1) * 512],
                    start=(kb == 0), stop=(kb == 7)), inc="st" if kb == 7 else None)
            p.wait("vector", mm)
            h1 = p.do("vector", lambda v: v.tensor_tensor(
                out=xb[:], in0=ps[:], in1=b1c[:, 0:1].to_broadcast([128, 512]),
                op=ALU.add), inc="sv")
            p.wait("scalar", h1)
            h2 = p.do("scalar", lambda s: s.activation(out=sq[:], in_=xb[:], func=AF.Square),
                      inc="ss")
            p.wait("vector", h2)
            h3 = p.do("vector", lambda v: v.tensor_scalar(
                out=mm_[:], in0=sq[:], scalar1=GC1, scalar2=GC0, op0=ALU.mult, op1=ALU.add),
                inc="sv")
            p.wait("vector", h3)
            h4 = p.do("vector", lambda v: v.tensor_tensor(out=mm_[:], in0=mm_[:], in1=xb[:],
                                                          op=ALU.mult), inc="sv")
            p.wait("scalar", h4)
            h5 = p.do("scalar", lambda s: s.activation(out=mm_[:], in_=mm_[:], func=AF.Tanh),
                      inc="ss")
            p.wait("vector", h5)
            h6 = p.do("vector", lambda v: v.tensor_scalar(
                out=mm_[:], in0=mm_[:], scalar1=0.5, scalar2=0.5, op0=ALU.mult, op1=ALU.add),
                inc="sv")
            p.wait("vector", h6)
            h7 = p.do("vector", lambda v, nb=nb: v.tensor_tensor(
                out=h1T[:, nb * 512:(nb + 1) * 512], in0=mm_[:], in1=xb[:], op=ALU.mult),
                inc="sv")
            prev_h = h7
            p.wait("gpsimd", h1)
        p.wait("tensor", prev_h)
        zf = None
        for tt in range(NTT):
            zf = p.dma("sync", c.dram["l3_bufs"][e, tt * 128:(tt + 1) * 128, :],
                       c.dram["l3_zero"][tt * 128:(tt + 1) * 128, :])
        for kb in range(32):
            ld = p.dma("sync", wld[:, 0:1024], c.dram["L3_ffn_W2"][e, kb * 128:(kb + 1) * 128, :])
            p.wait("vector", ld)
            cw = p.do("vector", lambda v, kb=kb: v.tensor_copy(
                out=w2h[:, kb * 1024:(kb + 1) * 1024], in_=wld[:, 0:1024]), inc="sv")
            p.wait("sync", cw)
        p.wait("tensor", cw)
        p.wait("gpsimd", zf)
        p.wait("scalar", lb2)
        for tq in range(4):
            r0 = tq * 128
            p.fence_all("gpsimd")
            lg = p.dma("gpsimd", gc[:], c.dram["l3_G"][r0:r0 + 128, :])
            lt = p.dma("gpsimd", tidu[:], c.dram["l3_tokid"][r0:r0 + 128, :])
            for half in range(2):
                mm = None
                for kb in range(32):
                    mm = p.do("tensor", lambda t, kb=kb, r0=r0, half=half: t.matmul(
                        ps_d[:], h1T[:, kb * 512 + r0: kb * 512 + r0 + 128],
                        w2h[:, kb * 1024 + half * 512: kb * 1024 + (half + 1) * 512],
                        start=(kb == 0), stop=False), inc=None)
                mm = p.do("tensor", lambda t, half=half: t.matmul(
                    ps_d[:], ones_r[0:1, :], b2r[0:1, half * 512:(half + 1) * 512],
                    start=False, stop=True), inc="st")
                p.wait("scalar", mm)
                p.wait("scalar", lg)
                wo_ = p.do("scalar", lambda s, half=half: s.activation(
                    out=wout[:, half * 512:(half + 1) * 512], in_=ps_d[:], func=AF.Identity,
                    scale=gc[:, 0:1]), inc="ss")
                p.wait("tensor", wo_)
            p.wait("gpsimd", wo_)
            p.wait("gpsimd", lt)
            from concourse.bass_core_rust import IndirectOffsetOnAxis as IOA
            sc_ = p.do("gpsimd", lambda g: g.indirect_dma_start(
                c.dram["l3_bufs"].rearrange("e t d -> (e t) d"),
                    IOA(tidu[:, 0:1], 0), wout[:], None), inc="dg", n=16)
            p.wait("scalar", sc_)
            p.wait("gpsimd", sc_)


def _l3_accum(c, ident, b):
    nc = c.nc
    with Ph(c, name=_phname()) as p:
        acc8 = [p.sb(f"c3_acc8_{i}", [128, 8 * D], F32) for i in range(2)]
        toks = [None, None]
        prev = [None, None]
        for tt in range(NTT + 1):
            if tt < NTT:
                r0 = tt * 128
                bk = tt % 2
                buf = acc8[bk]
                if toks[bk] is not None:
                    p.wait("sync", toks[bk])
                    p.wait("gpsimd", toks[bk])
                src = c.dram["l3_bufs"][:, r0:r0 + 128, :]
                l1 = p.dma("sync", buf[:, 0:4 * D].rearrange("p (e d) -> p e d", e=4),
                           src[0:4].rearrange("e p d -> p e d"))
                l2 = p.dma("gpsimd", buf[:, 4 * D:8 * D].rearrange("p (e d) -> p e d", e=4),
                           src[4:8].rearrange("e p d -> p e d"))
                prev[bk] = (l1, l2)
            if tt > 0:
                r0p = (tt - 1) * 128
                bk = (tt - 1) % 2
                bufp = acc8[bk]
                pl1, pl2 = prev[bk]
                p.wait("vector", pl1)
                for e_ in range(1, 4):
                    p.do("vector", lambda v, e_=e_, bufp=bufp: v.tensor_add(
                        bufp[:, 0:D], bufp[:, 0:D], bufp[:, e_ * D:(e_ + 1) * D]))
                p.wait("gpsimd", pl2)
                gp = None
                for e_ in range(5, 8):
                    gp = p.do("gpsimd", lambda g, e_=e_, bufp=bufp: g.tensor_tensor(
                        out=bufp[:, 4 * D:5 * D], in0=bufp[:, 4 * D:5 * D],
                        in1=bufp[:, e_ * D:(e_ + 1) * D], op=ALU.add),
                        inc="sg" if e_ == 7 else None)
                p.wait("vector", gp)
                a = p.do("vector", lambda v, bufp=bufp: v.tensor_add(
                    bufp[:, 0:D], bufp[:, 0:D], bufp[:, 4 * D:5 * D]), inc="sv")
                p.wait("scalar", a)
                sd = p.do("scalar", lambda e, r0p=r0p, bufp=bufp: e.dma_start(
                    out=c.dram["g_buf"][b, r0p:r0p + 128, :], in_=bufp[:, 0:D]),
                    inc="dact", n=16)
                toks[bk] = sd


def ph_combine(c, l):
    with Ph(c, name=_phname()) as p:
        xt = p.sb("cb_x", [128, D], F32)
        st_ = p.sb("cb_s", [128, D], F32)
        gt = p.sb("cb_g", [128, D], F32)
        gs = p.sb("cb_gs", [128, D], F32)
        xsrc = c.dram["L0_attn_x"] if l == 0 else c.dram["xres"]
        for b in range(B_PER_CORE):
            for tt in range(NTT):
                r0 = tt * 128
                p.fence_all("sync")
                p.fence_all("gpsimd")
                l1 = p.dma("sync", xt[:], xsrc[b, r0:r0 + 128, :])
                l2 = p.dma("sync", st_[:], c.dram["s_buf"][b, r0:r0 + 128, :])
                l3 = p.dma("gpsimd", gt[:], c.dram["g_buf"][b, r0:r0 + 128, :])
                p.wait("scalar", l3)
                a1 = p.do("scalar", lambda s: s.activation(out=gs[:], in_=gt[:], func=AF.Silu),
                          inc="ss")
                p.wait("vector", l1)
                p.wait("vector", l2)
                p.wait("vector", a1)
                p.do("vector", lambda v: v.tensor_mul(gs[:], gs[:], st_[:]))
                cv = p.do("vector", lambda v: v.tensor_add(xt[:], xt[:], gs[:]), inc="sv")
                p.wait("sync", cv)
                p.dma("sync", c.dram["xres"][b, r0:r0 + 128, :], xt[:])
                if l == 3:
                    p.dma("sync", c.dram["y"][b, r0:r0 + 128, :], xt[:])



def _host_inputs(inputs, core):
    m = {}
    b0 = core * B_PER_CORE
    m["L0_attn_x"] = np.ascontiguousarray(inputs["L0_attn_x"][b0:b0 + B_PER_CORE])
    pass_through = [
        "L0_attn_w_qkv", "L0_attn_w_o",
        "L0_ffn_w_gate", "L0_ffn_w_up", "L0_ffn_w_down",
        "L1_ffn_Wg", "L1_ffn_Wu", "L1_ffn_Wd", "L1_ffn_sWg", "L1_ffn_sWu", "L1_ffn_sWd",
        "L2_attn_Wq", "L2_attn_Wk", "L2_attn_Wv", "L2_attn_Wo",
        "L2_ffn_fc1_q", "L2_ffn_fc1_scales", "L2_ffn_fc2_q", "L2_ffn_fc2_scales",
        "L1_attn_gate_w", "L1_attn_proj_w",
        "L3_ffn_W1", "L3_ffn_W2", "L3_ffn_b1", "L3_ffn_b2",
    ]
    for s in ("f", "b"):
        pass_through += [f"L1_attn_inproj_{s}", f"L1_attn_conv_w_{s}",
                         f"L1_attn_xproj_{s}", f"L1_attn_dt_w_{s}", f"L1_attn_Alog_{s}",
                         f"L1_attn_outproj_{s}"]
    for k in pass_through:
        m[k] = np.ascontiguousarray(inputs[k])
    row = ["L0_attn_log_scale", "L1_attn_gate_b", "L1_attn_proj_b",
           "L1_ffn_ln_scale", "L1_ffn_ln_bias", "L2_ffn_ln_gamma", "L2_ffn_ln_beta",
           "L2_ffn_fc1_bias", "L2_ffn_fc2_bias", "L3_attn_b_qkv", "L3_attn_b_proj"]
    for s in ("f", "b"):
        row += [f"L1_attn_conv_b_{s}", f"L1_attn_dt_b_{s}", f"L1_attn_Dp_{s}"]
    for k in row:
        m[k] = np.ascontiguousarray(inputs[k].reshape(1, -1))
    m["L1_ffn_gate_wT"] = np.ascontiguousarray(inputs["L1_ffn_gate_w"].T)
    m["L3_attn_w_qkvT"] = np.ascontiguousarray(inputs["L3_attn_w_qkv"].T)
    m["L3_attn_w_projT"] = np.ascontiguousarray(inputs["L3_attn_w_proj"].T)
    m["L3_ffn_WgT"] = np.ascontiguousarray(inputs["L3_ffn_Wg"].T)
    m["c_pos"] = np.arange(T, dtype=np.float32).reshape(1, T)
    gam = np.asarray(RIEMANN, dtype=np.float32)
    invfreq = (1.0 / (gam / gam[0])).astype(np.float32)
    m["c_invfreq"] = np.tile(invfreq, 16).reshape(1, 512)
    return m


def kernel(**inputs):
    nc = build_program()
    in_maps = [_host_inputs(inputs, core) for core in range(8)]
    res = run_bass_kernel_spmd(nc, in_maps, core_ids=list(range(8)))
    out = np.concatenate([res.results[i]["y"] for i in range(8)], axis=0)
    return out.astype(np.float32)


# version 13 (final)
# speedup vs baseline: 1.0353x; 1.0353x over previous
import math
from collections import defaultdict
from contextlib import ExitStack

import numpy as np
import concourse.bass as bass
import concourse.mybir as mybir
from concourse.bass_utils import run_bass_kernel_spmd

F32 = mybir.dt.float32
BF16 = mybir.dt.bfloat16
U8 = mybir.dt.uint8
U32 = mybir.dt.uint32
I32 = mybir.dt.int32
AF = mybir.ActivationFunctionType
ALU = mybir.AluOpType
AX = mybir.AxisListType

ENGINES = ("sync", "scalar", "vector", "tensor", "gpsimd")

B_PER_CORE = 2
T = 2048
D = 1024
NTT = T // 128
MAGIC = 12582912.0

RIEMANN = (14.134725142, 21.022039639, 25.01085758, 30.424876126, 32.935061588,
           37.586178159, 40.918719012, 43.327073281, 48.005150881, 49.773832478,
           52.970321478, 56.446247697, 59.347044003, 60.831778525, 65.112544048,
           67.079810529, 69.546401711, 72.067157674, 75.704690699, 77.144840069,
           79.33737502, 82.910380854, 84.735492981, 87.425274613, 88.809111208,
           92.491899271, 94.651344041, 95.870634228, 98.831194218, 101.317851006,
           103.72553804, 105.446623052)

_LAYERS = [0, 1, 2, 3]
_DUMPS = []
_L3FFN_B = None


class Ctx:
    def __init__(self, nc):
        self.nc = nc
        self.sem = {}
        self.n = defaultdict(int)
        self.stack = ExitStack()
        self.dram = {}

    def get_sem(self, name):
        if name not in self.sem:
            self.sem[name] = self.stack.enter_context(self.nc.semaphore(name))
        return self.sem[name]

    def bump(self, name, n):
        self.get_sem(name)
        self.n[name] += n
        return self.n[name]

    def barrier(self):
        snap = {k: v for k, v in self.n.items() if v}
        with self.nc.Block() as B:
            for eng in ENGINES:
                def f(e, snap=snap):
                    for s, v in snap.items():
                        e.wait_ge(self.sem[s], v)
                getattr(B, eng)(f)




import inspect as _inspect

def _phname():
    f = _inspect.currentframe().f_back
    return f.f_code.co_name + ":" + str(f.f_lineno)

_PH_SEQ = [0]


class Ph:

    def __init__(self, c: Ctx, name=None):
        self.c = c
        self.q = defaultdict(list)
        self.stack = ExitStack()
        _PH_SEQ[0] += 1
        self.tag = f"p{_PH_SEQ[0]}_"
        self.name = name or self.tag

    def sb(self, name, shape, dt=F32):
        return self.stack.enter_context(self.c.nc.sbuf_tensor(self.tag + name, shape, dt))

    def ps(self, name, shape, dt=F32):
        return self.stack.enter_context(self.c.nc.psum_tensor(self.tag + name, shape, dt))

    def emit(self, eng, fn):
        self.q[eng].append(fn)

    def wait(self, eng, tok):
        if tok is None:
            return
        sem, val = tok
        self.emit(eng, lambda e, s=sem, v=val: e.wait_ge(self.c.sem[s], v))

    def fence_all(self, eng):
        snap = {k: v for k, v in self.c.n.items() if v}
        self.emit(eng, lambda e, snap=snap: [e.wait_ge(self.c.sem[s], v)
                                             for s, v in snap.items()])

    def do(self, eng, fn, inc=None, n=1):
        if inc is not None:
            v = self.c.bump(inc, n)
            self.emit(eng, lambda e, fn=fn, s=inc, n=n: fn(e).then_inc(self.c.sem[s], n))
            return (inc, v)
        self.emit(eng, fn)
        return None

    def dma(self, q, out, in_, **kw):
        return self.do(q, lambda e, o=out, i=in_, kw=kw: e.dma_start(out=o, in_=i, **kw),
                       inc="d" + q[0], n=16)

    def close(self):
        with self.c.nc.named_scope(self.name):
            with self.c.nc.Block() as B:
                for eng in ENGINES:
                    if self.q[eng]:
                        getattr(B, eng)(lambda e, fns=list(self.q[eng]): [f(e) for f in fns])
        self.stack.close()

    def __enter__(self):
        return self

    def __exit__(self, *a):
        self.close()
        return False



def ln_tile(p, x_f32, xn_out_bf, tmp, stat, eng="vector"):
    raise NotImplementedError


def emit_transpose(p, dst_sb, src_sb, pst, tok_in, sem_t="st", sem_v="sv"):
    ident = p._ident
    t1 = p.do("tensor", lambda t, o=pst, i=src_sb, idn=ident: t.transpose(o[:], i, idn[:]),
              inc=sem_t)
    p.wait("vector", t1)
    t2 = p.do("vector", lambda v, o=dst_sb, i=pst: v.tensor_copy(out=o, in_=i[:]), inc=sem_v)
    return t2



def build_program(layers=None, dumps=None):
    layers = _LAYERS if layers is None else layers
    dumps = _DUMPS if dumps is None else dumps
    nc = bass.Bass("TRN2", target_bir_lowering=False)
    c = Ctx(nc)

    dt_in = {}
    def din(name, shape, dt=F32):
        ap = nc.dram_tensor(name, list(shape), dt, kind="ExternalInput").ap()
        c.dram[name] = ap
        return ap

    x_in = din("L0_attn_x", (B_PER_CORE, T, D))
    din("L0_attn_w_qkv", (D, 3 * D))
    din("L0_attn_w_o", (D, D))
    din("L0_attn_log_scale", (1, 16))
    din("L0_ffn_w_gate", (2816, D))
    din("L0_ffn_w_up", (2816, D))
    din("L0_ffn_w_down", (D, 2816))
    for s in ("f", "b"):
        din(f"L1_attn_inproj_{s}", (D, 4096))
        din(f"L1_attn_conv_w_{s}", (2048, 4))
        din(f"L1_attn_conv_b_{s}", (1, 2048))
        din(f"L1_attn_xproj_{s}", (2048, 96))
        din(f"L1_attn_dt_w_{s}", (64, 2048))
        din(f"L1_attn_dt_b_{s}", (1, 2048))
        din(f"L1_attn_Alog_{s}", (2048, 16))
        din(f"L1_attn_Dp_{s}", (1, 2048))
        din(f"L1_attn_outproj_{s}", (2048, D))
    din("L1_attn_gate_w", (2 * D, D))
    din("L1_attn_gate_b", (1, D))
    din("L1_attn_proj_w", (D, D))
    din("L1_attn_proj_b", (1, D))
    din("L1_ffn_ln_scale", (1, D))
    din("L1_ffn_ln_bias", (1, D))
    din("L1_ffn_gate_wT", (D, 8))
    din("L1_ffn_Wg", (8, D, 2048))
    din("L1_ffn_Wu", (8, D, 2048))
    din("L1_ffn_Wd", (8, 2048, D))
    din("L1_ffn_sWg", (D, 2048))
    din("L1_ffn_sWu", (D, 2048))
    din("L1_ffn_sWd", (2048, D))
    din("L2_attn_Wq", (D, D))
    din("L2_attn_Wk", (D, D))
    din("L2_attn_Wv", (D, D))
    din("L2_attn_Wo", (D, D))
    din("L2_ffn_ln_gamma", (1, D))
    din("L2_ffn_ln_beta", (1, D))
    din("L2_ffn_fc1_q", (D, 4096), I32)
    din("L2_ffn_fc1_scales", (8, 4096))
    din("L2_ffn_fc1_bias", (1, 4096))
    din("L2_ffn_fc2_q", (4096, D), I32)
    din("L2_ffn_fc2_scales", (32, D))
    din("L2_ffn_fc2_bias", (1, D))
    din("L3_attn_w_qkvT", (D, 3 * D))
    din("L3_attn_b_qkv", (1, 3 * D))
    din("L3_attn_w_projT", (D, D))
    din("L3_attn_b_proj", (1, D))
    din("L3_ffn_WgT", (D, 8))
    din("L3_ffn_W1", (8, D, 4096))
    din("L3_ffn_b1", (8, 4096))
    din("L3_ffn_W2", (8, 4096, D))
    din("L3_ffn_b2", (8, D))
    din("c_pos", (1, T))
    din("c_invfreq", (1, 512))

    y_out = nc.dram_tensor("y", [B_PER_CORE, T, D], F32, kind="ExternalOutput").ap()
    c.dram["y"] = y_out

    def scr(name, shape, dt=F32):
        ap = nc.dram_tensor(name, list(shape), dt, kind="Internal").ap()
        c.dram[name] = ap
        return ap

    xres = scr("xres", (B_PER_CORE, T, D))
    xn_tok = scr("xn_tok", (B_PER_CORE, T, D), BF16)
    xnT = scr("xnT", (B_PER_CORE, D, T), BF16)
    s_buf = scr("s_buf", (B_PER_CORE, T, D))
    g_buf = scr("g_buf", (B_PER_CORE, T, D))
    xqT = scr("xqT", (B_PER_CORE, D, T), BF16)
    wgqT = scr("wgqT", (D, 2816), BF16)
    scr("dbg_qt", (128, 8 * 2048), BF16)
    scr("dbg_kt", (128, 8 * 2048), BF16)
    scr("dbg_v", (128, 16 * 1024), BF16)
    scr("dbg_aoT", (128, 8 * 2048), BF16)
    scr("dbg_probs", (128, 2048), BF16)
    scr("wobf", (D, D), BF16)
    scr("l1_uT", (2048, T), BF16)
    scr("l1_zsT", (2048, T), BF16)
    scr("l1_ucT", (2048, T), BF16)
    scr("l1_dT", (2048, T))
    scr("l1_bc", (32, T))
    scr("l1_ygT", (2048, T), BF16)
    scr("l1_yfT", (D, T), BF16)
    scr("l1_moew", (B_PER_CORE, T, 8))
    scr("l2_aoT", (D, T), BF16)
    scr("l2_x2T", (D, T), BF16)
    scr("l2_hT", (4096, T), BF16)
    scr("l2_w1", (D, 4096), BF16)
    scr("l2_w2", (4096, D), BF16)
    scr("l3_aoT", (D, T), BF16)
    scr("l3_p", (B_PER_CORE, T, 8))
    scr("l3_pT", (16, T))
    scr("l3_slot", (T, 16))
    scr("l3_Xg", (513, D), BF16)
    scr("l3_G", (513, 1))
    scr("l3_tokid", (513, 1), U32)
    scr("l3_zero", (T, D))
    scr("l3_bufs", (8, T, D))
    scr("l1_dbg_pr", (B_PER_CORE, T, 8))
    scr("l1_dbg_mx", (B_PER_CORE, T, 8))
    scr("l1_dbg_lg", (B_PER_CORE, T, 8))
    scr("l1_ybT", (D, T), BF16)
    wuqT = scr("wuqT", (D, 2816), BF16)
    wdqT = scr("wdqT", (2816, D), BF16)

    with c.stack:
        ident = c.stack.enter_context(nc.sbuf_tensor("g_ident", [128, 128], BF16))
        with Ph(c, name=_phname()) as p:
            p.do("gpsimd", lambda g: g.memset(ident[:], 0.0))
            p.do("gpsimd", lambda g: g.affine_select(
                out=ident[:], in_=ident[:], compare_op=ALU.not_equal, fill=1.0,
                base=0, pattern=[[-1, 128]], channel_multiplier=1), inc="sg")
        c.barrier()

        for l in layers:
            ph_ln(c, l, ident)
            c.barrier()
            if l == 0:
                ph_l0_quant_weights(c, ident)
                c.barrier()
                ph_l0_attn(c, ident)
                c.barrier()
                ph_l0_ffn(c, ident)
                c.barrier()
            elif l == 1:
                ph_l1_attn(c, ident)
                c.barrier()
                ph_l1_ffn(c, ident)
                c.barrier()
            elif l == 2:
                ph_l2_attn(c, ident)
                c.barrier()
                ph_l2_ffn(c, ident)
                c.barrier()
            elif l == 3:
                ph_l3_attn(c, ident)
                c.barrier()
                ph_l3_ffn(c, ident)
                c.barrier()
            ph_combine(c, l)
            c.barrier()

        for name in dumps:
            src = c.dram[name]
            dmp = nc.dram_tensor("dump_" + name, list(src.shape), src.dtype,
                                 kind="ExternalOutput").ap()
            with Ph(c, name=_phname()) as p:
                p.dma("sync", dmp[:], src[:])
            c.barrier()

        with Ph(c, name=_phname()) as p:
            snap = {k: v for k, v in c.n.items() if v}
            for s, v in snap.items():
                p.wait("sync", (s, v))
    return nc



def ph_ln(c, l, ident):
    nc = c.nc
    xsrc = c.dram["L0_attn_x"] if l == 0 else c.dram["xres"]
    with Ph(c, name=_phname()) as p:
        p._ident = ident
        xt = p.sb("ln_x", [128, D], F32)
        xm = p.sb("ln_xm", [128, D], F32)
        sq = p.sb("ln_sq", [128, D], F32)
        xnf = p.sb("ln_xnf", [128, D], F32)
        xnb = p.sb("ln_xnb", [128, D], BF16)
        st = p.sb("ln_stat", [128, 8], F32)
        pst = p.ps("ln_pst", [128, 128], BF16)
        tb8 = p.sb("ln_tb8", [128, 8 * 128], BF16)
        q_absmax = p.sb("ln_qam", [128, 64], F32)
        q_rinv = p.sb("ln_qrinv", [128, 64], F32)
        q_v = p.sb("ln_qv", [128, D], F32)
        q_t = p.sb("ln_qt", [128, D], F32)
        q_u = p.sb("ln_qu", [128, D], F32)
        q_w = p.sb("ln_qw", [128, D], F32)
        q_m = p.sb("ln_qm", [128, D], U8)
        q_out = p.sb("ln_qout", [128, D], BF16)

        for b in range(B_PER_CORE):
            for tt in range(NTT):
                tok = ln_body(p, c, l, b, tt, xsrc, xt, xm, sq, xnf, xnb, st,
                              pst, tb8, q_absmax, q_rinv, q_v, q_t, q_u, q_w, q_m, q_out)


def ln_body(p, c, l, b, tt, xsrc, xt, xm, sq, xnf, xnb, st, pst, tb8,
            q_absmax, q_rinv, q_v, q_t, q_u, q_w, q_m, q_out):
    nc = c.nc
    r0 = tt * 128
    p.fence_all("sync")
    ld = p.dma("sync", xt[:], xsrc[b, r0:r0 + 128, :])
    p.wait("vector", ld)
    rs = p.do("vector", lambda v: v.reduce_sum(st[:, 0:1], xt[:], axis=AX.X), inc="sv")
    p.wait("vector", rs)
    nm = p.do("vector", lambda v: v.tensor_scalar(
        out=st[:, 1:2], in0=st[:, 0:1], scalar1=-1.0 / D, scalar2=None, op0=ALU.mult),
        inc="sv")
    p.wait("scalar", nm)
    a1 = p.do("scalar", lambda s: s.activation(out=xm[:], in_=xt[:], func=AF.Identity,
                                               bias=st[:, 1:2], scale=1.0), inc="ss")
    a2 = p.do("scalar", lambda s: s.activation(out=sq[:], in_=xm[:], func=AF.Square,
                                               accum_out=st[:, 2:3]), inc="ss")
    p.wait("vector", a2)
    sv1 = p.do("vector", lambda v: v.tensor_scalar(out=st[:, 3:4], in0=st[:, 2:3],
                                                   scalar1=1.0 / D, scalar2=1e-5,
                                                   op0=ALU.mult, op1=ALU.add), inc="sv")
    p.wait("scalar", sv1)
    a3 = p.do("scalar", lambda s: s.activation(out=st[:, 4:5], in_=st[:, 3:4], func=AF.Sqrt),
              inc="ss")
    p.wait("vector", a3)
    rc = p.do("vector", lambda v: v.reciprocal(st[:, 5:6], st[:, 4:5]), inc="sv")
    p.wait("vector", rc)
    p.wait("vector", a2)
    a4 = p.do("vector", lambda v: v.tensor_tensor(
        out=xnf[:], in0=xm[:], in1=st[:, 5:6].to_broadcast([128, D]), op=ALU.mult),
        inc="sv")
    a5 = p.do("vector", lambda v: v.tensor_copy(out=xnb[:], in_=xnf[:]), inc="sv")
    p.wait("sync", a5)
    p.dma("sync", c.dram["xn_tok"][b, r0:r0 + 128, :], xnb[:])
    p.wait("tensor", a5)
    cv = None
    for k in range(8):
        tr = p.do("tensor", lambda t, k=k: t.transpose(pst[:], xnb[:, k * 128:(k + 1) * 128],
                                                       p._ident[:]), inc="st")
        p.wait("vector", tr)
        cv = p.do("vector", lambda v, k=k: v.tensor_copy(out=tb8[:, k * 128:(k + 1) * 128],
                                                         in_=pst[:]), inc="sv")
        p.wait("tensor", cv)
    p.wait("sync", cv)
    sd = p.dma("sync", c.dram["xnT"][b, :, r0:r0 + 128].rearrange("(k p) t -> p k t", p=128),
               tb8[:].rearrange("p (k t) -> p k t", k=8))
    p.wait("vector", sd)

    if l == 0:
        p.wait("vector", a4)
        qt = fp4_quant(p, c, q_out, xnf, q_absmax, q_rinv, q_v, q_t, q_u, q_w, q_m, D)
        p.wait("tensor", qt)
        cv = None
        for k in range(8):
            tr = p.do("tensor", lambda t, k=k: t.transpose(pst[:], q_out[:, k * 128:(k + 1) * 128],
                                                           p._ident[:]), inc="st")
            p.wait("vector", tr)
            cv = p.do("vector", lambda v, k=k: v.tensor_copy(out=tb8[:, k * 128:(k + 1) * 128],
                                                             in_=pst[:]), inc="sv")
            p.wait("tensor", cv)
        p.wait("sync", cv)
        sd = p.dma("sync", c.dram["xqT"][b, :, r0:r0 + 128].rearrange("(k p) t -> p k t", p=128),
                   tb8[:].rearrange("p (k t) -> p k t", k=8))
        p.wait("vector", sd)
    return None


def fp4_quant(p, c, out_bf, src_f32, am, rinv, v_, t_, u_, w_, m_, width):
    G = width // 16
    r1 = p.do("vector", lambda v: v.tensor_reduce(
        am[:, 0:G].rearrange("p (g k) -> p g k", k=1),
        src_f32.rearrange("p (g k) -> p g k", k=16),
        axis=AX.X, op=ALU.max, apply_absolute_value=True), inc="sv")
    p.wait("vector", r1)
    r2 = p.do("vector", lambda v: v.tensor_scalar(out=am[:, 0:G], in0=am[:, 0:G],
                                                  scalar1=6e-12, scalar2=None, op0=ALU.max),
              inc="sv")
    p.wait("vector", r2)
    r3 = p.do("vector", lambda v: v.reciprocal(rinv[:, 0:G], am[:, 0:G]), inc="sv")
    p.wait("vector", r3)
    r4 = p.do("vector", lambda v: v.tensor_scalar(out=rinv[:, 0:G], in0=rinv[:, 0:G],
                                                  scalar1=6.0, scalar2=None, op0=ALU.mult),
              inc="sv")
    p.wait("vector", r4)

    def mul_bcast(v):
        a = src_f32.rearrange("p (g k) -> p g k", k=16)
        bmat = rinv[:, 0:G].rearrange("p (g k) -> p g k", k=1).to_broadcast([128, G, 16])
        return v.tensor_tensor(out=v_[:, 0:width].rearrange("p (g k) -> p g k", k=16),
                               in0=a, in1=bmat, op=ALU.mult)
    r5 = p.do("vector", mul_bcast, inc="sv")
    p.wait("vector", r5)
    p.do("vector", lambda v: v.tensor_scalar(out=t_[:, 0:width], in0=v_[:, 0:width], scalar1=2.0,
                                             scalar2=MAGIC, op0=ALU.mult, op1=ALU.add))
    p.do("vector", lambda v: v.tensor_scalar(out=u_[:, 0:width], in0=t_[:, 0:width], scalar1=MAGIC,
                                             scalar2=0.5, op0=ALU.subtract, op1=ALU.mult))
    C = float(2 ** 22 + 1)
    p.do("vector", lambda v: v.tensor_scalar(out=t_[:, 0:width], in0=v_[:, 0:width], scalar1=C,
                                             scalar2=None, op0=ALU.mult))
    p.do("vector", lambda v: v.tensor_tensor(out=w_[:, 0:width], in0=t_[:, 0:width], in1=v_[:, 0:width], op=ALU.subtract))
    p.do("vector", lambda v: v.tensor_tensor(out=w_[:, 0:width], in0=t_[:, 0:width], in1=w_[:, 0:width], op=ALU.subtract))
    p.do("vector", lambda v: v.tensor_scalar(out=w_[:, 0:width], in0=w_[:, 0:width], scalar1=6.0,
                                             scalar2=-6.0, op0=ALU.min, op1=ALU.max))
    p.do("vector", lambda v: v.tensor_scalar(out=t_[:, 0:width], in0=v_[:, 0:width], scalar1=-2.0,
                                             scalar2=2.0, op0=ALU.max, op1=ALU.min))
    mk = p.do("vector", lambda v: v.tensor_tensor(out=m_[:, 0:width], in0=v_[:, 0:width], in1=t_[:, 0:width],
                                                  op=ALU.is_equal), inc="sv")
    p.wait("vector", mk)
    se = p.do("vector", lambda v: v.select(v_[:, 0:width], m_[:, 0:width], u_[:, 0:width], w_[:, 0:width]), inc="sv")
    r6 = p.do("vector", lambda v: v.tensor_scalar(out=am[:, 0:G], in0=am[:, 0:G],
                                                  scalar1=1.0 / 6.0, scalar2=None,
                                                  op0=ALU.mult), inc="sv")
    p.wait("vector", se)
    p.wait("vector", r6)

    def mul_deq(v):
        a = v_[:, 0:width].rearrange("p (g k) -> p g k", k=16)
        bmat = am[:, 0:G].rearrange("p (g k) -> p g k", k=1).to_broadcast([128, G, 16])
        return v.tensor_tensor(out=out_bf.rearrange("p (g k) -> p g k", k=16),
                               in0=a, in1=bmat, op=ALU.mult)
    tok = p.do("vector", mul_deq, inc="sv")
    return tok


TWO_PI = 2.0 * math.pi


def emit_rangered_sin(p, c, theta_ps, out_sb, tmp1, tmp2, shift, width):
    p.do("vector", lambda v: v.tensor_scalar(out=tmp1[:, 0:width], in0=theta_ps,
                                             scalar1=1.0 / TWO_PI, scalar2=shift,
                                             op0=ALU.mult, op1=ALU.add))
    p.do("vector", lambda v: v.tensor_scalar(out=tmp2[:, 0:width], in0=tmp1[:, 0:width],
                                             scalar1=MAGIC, scalar2=MAGIC,
                                             op0=ALU.add, op1=ALU.subtract))
    p.do("vector", lambda v: v.tensor_tensor(out=tmp1[:, 0:width], in0=tmp1[:, 0:width],
                                             in1=tmp2[:, 0:width], op=ALU.subtract))
    t = p.do("vector", lambda v: v.tensor_scalar(out=tmp2[:, 0:width], in0=tmp1[:, 0:width],
                                                 scalar1=TWO_PI, scalar2=None,
                                                 op0=ALU.mult), inc="sv")
    p.wait("scalar", t)
    return p.do("scalar", lambda s: s.activation(out=out_sb, in_=tmp2[:, 0:width],
                                                 func=AF.Sin), inc="ss")


def ph_l0_quant_weights(c, ident):
    with Ph(c, name=_phname()) as p:
        p._ident = ident
        wt = p.sb("qw_x", [128, 2816], F32)
        am = p.sb("qw_am", [128, 2816 // 16], F32)
        ri = p.sb("qw_ri", [128, 2816 // 16], F32)
        v_ = p.sb("qw_v", [128, 2816], F32)
        t_ = p.sb("qw_t", [128, 2816], F32)
        u_ = p.sb("qw_u", [128, 2816], F32)
        w_ = p.sb("qw_w", [128, 2816], F32)
        m_ = p.sb("qw_m", [128, 2816], U8)
        qo = p.sb("qw_qo", [128, 2816], BF16)
        pst = p.ps("qw_pst", [128, 128], BF16)
        tb = p.sb("qw_tb", [128, 128], BF16)

        def one(src_name, dst_name, nrows, ncols):
            nrt = nrows // 128
            nct = ncols // 128
            for rt in range(nrt):
                p.fence_all("sync")
                ld = p.dma("sync", wt[:, 0:ncols], c.dram[src_name][rt * 128:(rt + 1) * 128, :])
                p.wait("vector", ld)
                qt = fp4_quant(p, c, qo[:, 0:ncols], wt[:, 0:ncols], am, ri, v_, t_, u_, w_, m_, ncols)
                p.wait("tensor", qt)
                for k in range(nct):
                    tr = p.do("tensor", lambda t, k=k: t.transpose(
                        pst[:], qo[:, k * 128:(k + 1) * 128], p._ident[:]), inc="st")
                    p.wait("vector", tr)
                    cv = p.do("vector", lambda v: v.tensor_copy(out=tb[:], in_=pst[:]), inc="sv")
                    p.wait("sync", cv)
                    sd = p.dma("sync", c.dram[dst_name][k * 128:(k + 1) * 128,
                                                        rt * 128:(rt + 1) * 128], tb[:])
                    p.wait("tensor", cv)
                    p.wait("vector", sd)

        one("L0_ffn_w_gate", "wgqT", 2816, D)
        one("L0_ffn_w_up", "wuqT", 2816, D)
        one("L0_ffn_w_down", "wdqT", D, 2816)
        for rt in range(8):
            p.fence_all("sync")
            ld = p.dma("sync", wt[:, 0:D], c.dram["L0_attn_w_o"][rt * 128:(rt + 1) * 128, :])
            p.wait("vector", ld)
            cb = p.do("vector", lambda v: v.tensor_copy(out=qo[:, 0:D], in_=wt[:, 0:D]),
                      inc="sv")
            p.wait("sync", cb)
            sd = p.dma("sync", c.dram["wobf"][rt * 128:(rt + 1) * 128, :], qo[:, 0:D])
            p.wait("vector", sd)


def ph_l0_attn(c, ident):
    nc = c.nc
    NH = 16
    with ExitStack() as outer:
        oe = outer.enter_context
        wq_sb = oe(nc.sbuf_tensor("a0_wqkv", [128, 8 * 3072], BF16))
        qt_sb = oe(nc.sbuf_tensor("a0_qt", [128, 8 * 2048], BF16))
        kt_sb = oe(nc.sbuf_tensor("a0_kt", [128, 8 * 2048], BF16))
        v_sb = oe(nc.sbuf_tensor("a0_v", [128, 16 * 1024], BF16))
        aoT_sb = oe(nc.sbuf_tensor("a0_aoT", [128, 8 * 2048], BF16))
        throw = oe(nc.sbuf_tensor("a0_throw", [1, 512], F32))
        cmask = oe(nc.sbuf_tensor("a0_cmask", [128, 128], F32))
        p = Ph(c, name=_phname())
        p._ident = ident
        wo_sb = p.sb("a0_wo", [128, 8 * 512], BF16)
        wld = p.sb("a0_wld", [128, 1024], F32)
        xnt_sb = p.sb("a0_xnt", [128, 8 * 128], BF16)
        ifr = p.sb("a0_ifr", [1, 512], F32)
        lsc = p.sb("a0_lsc", [1, 16], F32)
        sc_r = p.sb("a0_scr", [1, 16], F32)
        sin_sb = p.sb("a0_sin", [128, 512], BF16)
        cos_sb = p.sb("a0_cos", [128, 512], BF16)
        tmp1 = p.sb("a0_tmp1", [128, 512], F32)
        tmp2 = p.sb("a0_tmp2", [128, 512], F32)
        rop = p.sb("a0_rop", [128, 1024], BF16)
        pos_t = p.sb("a0_pos", [1, 128], F32)
        ps_th = p.ps("a0_ps_th", [128, 512], F32)
        ps_qkv = [p.ps(f"a0_ps_qkv{i}", [128, 512], F32) for i in range(6)]
        ps_tp = p.ps("a0_ps_tp", [128, 128], BF16)

        p.do("gpsimd", lambda g: g.memset(cmask[:], 0.0))
        gm = p.do("gpsimd", lambda g: g.affine_select(
            out=cmask[:], in_=cmask[:], compare_op=ALU.is_ge, fill=-1e30,
            base=0, pattern=[[-1, 128]], channel_multiplier=1), inc="sg")
        p.wait("vector", gm)

        for kb in range(8):
            for ch in range(3):
                ld = p.dma("sync", wld[:],
                           c.dram["L0_attn_w_qkv"][kb * 128:(kb + 1) * 128,
                                                   ch * 1024:(ch + 1) * 1024])
                p.wait("vector", ld)
                cv = p.do("vector", lambda v, kb=kb, ch=ch: v.tensor_copy(
                    out=wq_sb[:, kb * 3072 + ch * 1024: kb * 3072 + (ch + 1) * 1024],
                    in_=wld[:]), inc="sv")
                p.wait("sync", cv)
        l1 = p.dma("sync", lsc[:], c.dram["L0_attn_log_scale"][:])
        l2 = p.dma("sync", ifr[:], c.dram["c_invfreq"][:])
        p.wait("scalar", l1)
        e1 = p.do("scalar", lambda s: s.activation(out=sc_r[:], in_=lsc[:], func=AF.Exp),
                  inc="ss")
        p.wait("vector", e1)
        p.wait("vector", l2)
        ttok = None
        for h in range(16):
            ttok = p.do("vector", lambda v, h=h: v.tensor_tensor(
                out=throw[0:1, h * 32:(h + 1) * 32], in0=ifr[0:1, h * 32:(h + 1) * 32],
                in1=sc_r[0:1, h:h + 1].to_broadcast([1, 32]), op=ALU.mult), inc="sv")
        p.wait("tensor", ttok)

        for b in range(B_PER_CORE):
            prev_mm = None
            prev_vv = None
            for tt in range(NTT):
                r0 = tt * 128
                p.fence_all("gpsimd")
                pl = p.dma("gpsimd", pos_t[:], c.dram["c_pos"][0:1, r0:r0 + 128])
                p.wait("tensor", pl)
                th = p.do("tensor", lambda t: t.matmul(
                    ps_th[:], pos_t[0:1, :], throw[0:1, :],
                    start=True, stop=True), inc="st")
                p.wait("vector", th)
                s_sin = emit_rangered_sin(p, c, ps_th[:], sin_sb[:], tmp1, tmp2, 0.0, 512)
                p.wait("vector", s_sin)
                s_cos = emit_rangered_sin(p, c, ps_th[:], cos_sb[:], tmp1, tmp2, 0.25, 512)
                p.wait("sync", prev_mm)
                lds = p.dma("sync", xnt_sb[:].rearrange("p (k t) -> p k t", k=8),
                            c.dram["xnT"][b, :, r0:r0 + 128]
                            .rearrange("(k p) t -> p k t", p=128))
                p.wait("tensor", lds)
                p.wait("tensor", s_cos)
                p.wait("tensor", prev_vv)
                mm = None
                for kb in range(8):
                    for sl in range(6):
                        mm = p.do("tensor", lambda t, kb=kb, sl=sl: t.matmul(
                            ps_qkv[sl][:], xnt_sb[:, kb * 128:(kb + 1) * 128],
                            wq_sb[:, kb * 3072 + sl * 512: kb * 3072 + (sl + 1) * 512],
                            start=(kb == 0), stop=(kb == 7)),
                            inc="st" if (kb == 7 and sl == 5) else None)
                prev_mm = mm
                p.wait("vector", mm)
                p.wait("vector", s_cos)
                p.wait("vector", s_sin)
                for qk in range(2):
                    rlast = None
                    for sl2 in range(2):
                        sl = qk * 2 + sl2
                        pss = ps_qkv[sl]
                        hw = 256
                        cbase = sl2 * 256
                        x1 = pss[:].rearrange("p (n two) -> p n two", two=2)
                        cosh = cos_sb[:, cbase:cbase + hw]
                        sinh = sin_sb[:, cbase:cbase + hw]
                        ro = rop[:].rearrange("p (n two) -> p n two", two=2)
                        t3 = tmp1[:, 0:hw].rearrange("p (n k) -> p n k", k=1)
                        t4 = tmp2[:, 0:hw].rearrange("p (n k) -> p n k", k=1)
                        p.do("vector", lambda v, x1=x1, cosh=cosh, t3=t3: v.tensor_tensor(
                            out=t3, in0=x1[:, :, 0:1], in1=cosh, op=ALU.mult))
                        p.do("vector", lambda v, x1=x1, sinh=sinh, t4=t4: v.tensor_tensor(
                            out=t4, in0=x1[:, :, 1:2], in1=sinh, op=ALU.mult))
                        p.do("vector", lambda v, ro=ro, sl2=sl2, t3=t3, t4=t4: v.tensor_tensor(
                            out=ro[:, sl2 * 256:(sl2 + 1) * 256, 0:1], in0=t3,
                            in1=t4, op=ALU.subtract))
                        p.do("vector", lambda v, x1=x1, sinh=sinh, t3=t3: v.tensor_tensor(
                            out=t3, in0=x1[:, :, 0:1], in1=sinh, op=ALU.mult))
                        p.do("vector", lambda v, x1=x1, cosh=cosh, t4=t4: v.tensor_tensor(
                            out=t4, in0=x1[:, :, 1:2], in1=cosh, op=ALU.mult))
                        rlast = p.do("vector", lambda v, ro=ro, sl2=sl2, t3=t3, t4=t4:
                                     v.tensor_tensor(
                            out=ro[:, sl2 * 256:(sl2 + 1) * 256, 1:2], in0=t3,
                            in1=t4, op=ALU.add), inc="sv")
                    p.wait("tensor", rlast)
                    dst = qt_sb if qk == 0 else kt_sb
                    cv = None
                    for k in range(8):
                        tr = p.do("tensor", lambda t, k=k: t.transpose(
                            ps_tp[:], rop[:, k * 128:(k + 1) * 128], p._ident[:]), inc="st")
                        p.wait("vector", tr)
                        cv = p.do("vector", lambda v, k=k, dst=dst, r0=r0: v.tensor_copy(
                            out=dst[:, k * 2048 + r0: k * 2048 + r0 + 128], in_=ps_tp[:]),
                            inc="sv")
                        p.wait("tensor", cv)
                    p.wait("vector", cv)
                vv = None
                for sl2 in range(2):
                    vv = p.do("vector", lambda v, sl2=sl2, tt=tt: v.tensor_copy(
                        out=v_sb[:, tt * 1024 + sl2 * 512: tt * 1024 + (sl2 + 1) * 512],
                        in_=ps_qkv[4 + sl2][:]), inc="sv")
                prev_vv = vv

            p.close()
            c.barrier()
            if b == 0 and getattr(c, "_dbg_l0", False):
                with Ph(c) as pd:
                    pd.dma("sync", c.dram["dbg_qt"][:], qt_sb[:])
                    pd.dma("sync", c.dram["dbg_kt"][:], kt_sb[:])
                    pd.dma("sync", c.dram["dbg_v"][:], v_sb[:])
                c.barrier()
            p = Ph(c, name=_phname())
            p._ident = ident
            probs = p.sb("a0_probs", [128, 2048], BF16)
            probsT = p.sb("a0_probsT", [128, 128], BF16)
            ao = p.sb("a0_ao", [128, 64], BF16)
            aopad = p.sb("a0_aopad", [128, 128], BF16)
            stat = p.sb("a0_stat", [128, 4], F32)
            ps_sc = p.ps("a0_ps_sc", [128, 2048], F32)
            ps_tp = p.ps("a0_ps_tp", [128, 128], BF16)
            ps_ao = p.ps("a0_ps_ao", [128, 64], F32)
            prev_ex = None
            prev_ev = None
            prev_trlast = None
            prev_cv2 = None
            for h in range(NH):
                pb = (h % 2) * 64
                jb = h // 2
                for qi in range(NTT):
                    q0 = qi * 128
                    nk = (qi + 1) * 128
                    p.wait("tensor", prev_ex)
                    mm = None
                    nchunk = (nk + 511) // 512
                    for ch in range(nchunk):
                        w = min(512, nk - ch * 512)
                        mm = p.do("tensor", lambda t, ch=ch, w=w, pb=pb, jb=jb, q0=q0: t.matmul(
                            ps_sc[:, ch * 512: ch * 512 + w],
                            qt_sb[pb:pb + 64, jb * 2048 + q0: jb * 2048 + q0 + 128],
                            kt_sb[pb:pb + 64, jb * 2048 + ch * 512: jb * 2048 + ch * 512 + w],
                            start=True, stop=True),
                            inc="st" if ch == nchunk - 1 else None)
                    p.wait("vector", mm)
                    msk = p.do("vector", lambda v, q0=q0: v.tensor_tensor(
                        out=ps_sc[:, q0:q0 + 128], in0=ps_sc[:, q0:q0 + 128],
                        in1=cmask[:], op=ALU.add), inc="sv")
                    p.wait("vector", msk)
                    r1 = p.do("vector", lambda v, nk=nk: v.reduce_max(
                        stat[:, 0:1], ps_sc[:, 0:nk], axis=AX.X, negate=True), inc="sv")
                    p.wait("vector", r1)
                    r2 = p.do("vector", lambda v: v.tensor_scalar(
                        out=stat[:, 1:2], in0=stat[:, 0:1], scalar1=0.125, scalar2=None,
                        op0=ALU.mult), inc="sv")
                    p.wait("scalar", r2)
                    p.wait("scalar", prev_trlast)
                    ex = p.do("scalar", lambda s, nk=nk: s.activation(
                        out=probs[:, 0:nk], in_=ps_sc[:, 0:nk], func=AF.Exp,
                        bias=stat[:, 1:2], scale=0.125, accum_out=stat[:, 2:3]), inc="ss")
                    prev_ex = ex
                    p.wait("vector", ex)
                    r3 = p.do("vector", lambda v: v.reciprocal(stat[:, 3:4], stat[:, 2:3]),
                              inc="sv")
                    p.wait("tensor", ex)
                    p.wait("tensor", prev_ev)
                    amm = None
                    trlast = None
                    prev_amm = None
                    for j in range(qi + 1):
                        p.wait("tensor", prev_cv2)
                        tr = p.do("tensor", lambda t, j=j: t.transpose(
                            ps_tp[:], probs[:, j * 128:(j + 1) * 128], p._ident[:]), inc="st")
                        trlast = tr
                        prev_cv2 = None
                        p.wait("vector", tr)
                        p.wait("vector", prev_amm)
                        cv = p.do("vector", lambda v: v.tensor_copy(out=probsT[:], in_=ps_tp[:]),
                                  inc="sv")
                        p.wait("tensor", cv)
                        amm = p.do("tensor", lambda t, j=j, qi=qi, h=h: t.matmul(
                            ps_ao[:], probsT[:],
                            v_sb[:, j * 1024 + h * 64: j * 1024 + h * 64 + 64],
                            start=(j == 0), stop=(j == qi)), inc="st")
                        prev_amm = amm
                    prev_trlast = trlast
                    p.wait("vector", r3)
                    p.wait("vector", amm)
                    ev = p.do("vector", lambda v: v.tensor_tensor(
                        out=ao[:], in0=ps_ao[:], in1=stat[:, 3:4].to_broadcast([128, 64]),
                        op=ALU.mult), inc="sv")
                    prev_ev = ev
                    p.wait("vector", ev)
                    p.do("vector", lambda v, pb=pb: v.memset(aopad[:, 64 - (64 - pb) % 128 if False else (64 if pb == 0 else 0):(128 if pb == 0 else 64)], 0.0))
                    cp2 = p.do("vector", lambda v, pb=pb: v.tensor_copy(
                        out=aopad[:, pb:pb + 64], in_=ao[:]), inc="sv")
                    p.wait("tensor", cp2)
                    tr2 = p.do("tensor", lambda t: t.transpose(ps_tp[:], aopad[:], p._ident[:]),
                               inc="st")
                    p.wait("vector", tr2)
                    cv2 = p.do("vector", lambda v, pb=pb, jb=jb, q0=q0: v.tensor_copy(
                        out=aoT_sb[pb:pb + 64, jb * 2048 + q0: jb * 2048 + q0 + 128],
                        in_=ps_tp[pb:pb + 64, :]), inc="sv")
                    prev_cv2 = cv2
                    p.wait("tensor", cv2)

            p.close()
            c.barrier()
            if b == 0 and getattr(c, "_dbg_l0", False):
                with Ph(c) as pd:
                    pd.dma("sync", c.dram["dbg_aoT"][:], aoT_sb[:])
                c.barrier()
            p = Ph(c, name=_phname())
            p._ident = ident
            wo_sb = p.sb("a0_wo", [128, 8 * 512], BF16)
            tmp1 = p.sb("a0_tmp1", [128, 512], F32)
            ps_s = p.ps("a0_ps_s", [128, 512], F32)
            for half in range(2):
                p.fence_all("gpsimd")
                wl = p.dma("gpsimd", wo_sb[:].rearrange("p (k t) -> p k t", k=8),
                           c.dram["wobf"][:, half * 512:(half + 1) * 512]
                           .rearrange("(k p) t -> p k t", p=128))
                p.wait("tensor", wl)
                for tt in range(NTT):
                    r0 = tt * 128
                    mm = None
                    for kb in range(8):
                        mm = p.do("tensor", lambda t, kb=kb, r0=r0: t.matmul(
                            ps_s[:], aoT_sb[:, kb * 2048 + r0: kb * 2048 + r0 + 128],
                            wo_sb[:, kb * 512:(kb + 1) * 512],
                            start=(kb == 0), stop=(kb == 7)),
                            inc="st" if kb == 7 else None)
                    p.wait("vector", mm)
                    cv = p.do("vector", lambda v: v.tensor_copy(out=tmp1[:], in_=ps_s[:]),
                              inc="sv")
                    p.wait("sync", cv)
                    sd = p.dma("sync", c.dram["s_buf"][b, r0:r0 + 128,
                                                       half * 512:(half + 1) * 512], tmp1[:])
                    p.wait("vector", sd)
                    p.wait("tensor", cv)
            p.close()
            c.barrier()
            if b == 0:
                p = Ph(c, name=_phname())
                p._ident = ident
                wld = p.sb("a0_wld", [128, 512], F32)
                xnt_sb = p.sb("a0_xnt", [128, 8 * 128], BF16)
                sin_sb = p.sb("a0_sin", [128, 512], BF16)
                cos_sb = p.sb("a0_cos", [128, 512], BF16)
                tmp1 = p.sb("a0_tmp1", [128, 512], F32)
                tmp2 = p.sb("a0_tmp2", [128, 512], F32)
                rop = p.sb("a0_rop", [128, 1024], BF16)
                pos_t = p.sb("a0_pos", [1, 128], F32)
                ps_th = p.ps("a0_ps_th", [128, 512], F32)
                ps_qkv = [p.ps(f"a0_ps_qkv{i}", [128, 512], F32) for i in range(6)]
                ps_tp = p.ps("a0_ps_tp", [128, 128], BF16)


def ph_l0_ffn(c, ident):
    nc = c.nc
    ISL = [512] * 5 + [256]
    with ExitStack() as outer:
        oe = outer.enter_context
        hqT = oe(nc.sbuf_tensor("f0_hqT", [128, 22 * 2048], BF16))
        for b in range(B_PER_CORE):
            p = Ph(c, name=_phname())
            p._ident = ident
            wg_sl = p.sb("f0_wg", [128, 8 * 512], BF16)
            wu_sl = p.sb("f0_wu", [128, 8 * 512], BF16)
            xqt = p.sb("f0_xqt", [128, 8 * 128], BF16)
            gsil = p.sb("f0_gsil", [128, 512], F32)
            hft = p.sb("f0_hf", [128, 512], F32)
            q_am = p.sb("f0_am", [128, 32], F32)
            q_ri = p.sb("f0_ri", [128, 32], F32)
            q_v = p.sb("f0_qv", [128, 512], F32)
            q_t = p.sb("f0_qt", [128, 512], F32)
            q_u = p.sb("f0_qu", [128, 512], F32)
            q_w = p.sb("f0_qw", [128, 512], F32)
            q_m = p.sb("f0_qm", [128, 512], U8)
            hqb = p.sb("f0_hqb", [128, 512], BF16)
            ps_g = p.ps("f0_ps_g", [128, 512], F32)
            ps_u = p.ps("f0_ps_u", [128, 512], F32)
            ps_tp = p.ps("f0_ps_tp", [128, 128], BF16)
            i0 = 0
            prev_hmul = None
            prev_sg = None
            for isl, w in enumerate(ISL):
                p.fence_all("sync")
                wl = p.dma("sync", wg_sl[:].rearrange("p (k t) -> p k t", k=8)[:, :, 0:w],
                           c.dram["wgqT"][:, i0:i0 + w].rearrange("(k p) t -> p k t", p=128))
                wl = p.dma("sync", wu_sl[:].rearrange("p (k t) -> p k t", k=8)[:, :, 0:w],
                           c.dram["wuqT"][:, i0:i0 + w].rearrange("(k p) t -> p k t", p=128))
                p.wait("tensor", wl)
                for tt in range(NTT):
                    r0 = tt * 128
                    p.fence_all("gpsimd")
                    xl = p.dma("gpsimd", xqt[:].rearrange("p (k t) -> p k t", k=8),
                               c.dram["xqT"][b, :, r0:r0 + 128]
                               .rearrange("(k p) t -> p k t", p=128))
                    p.wait("tensor", xl)
                    p.wait("tensor", prev_hmul)
                    p.wait("tensor", prev_sg)
                    mmg = mmu = None
                    for kb in range(8):
                        mmg = p.do("tensor", lambda t, kb=kb, w=w: t.matmul(
                            ps_g[:, 0:w], xqt[:, kb * 128:(kb + 1) * 128],
                            wg_sl[:, kb * 512: kb * 512 + w],
                            start=(kb == 0), stop=(kb == 7)),
                            inc="st" if kb == 7 else None)
                    for kb in range(8):
                        mmu = p.do("tensor", lambda t, kb=kb, w=w: t.matmul(
                            ps_u[:, 0:w], xqt[:, kb * 128:(kb + 1) * 128],
                            wu_sl[:, kb * 512: kb * 512 + w],
                            start=(kb == 0), stop=(kb == 7)),
                            inc="st" if kb == 7 else None)
                    p.wait("scalar", mmg)
                    sg = p.do("scalar", lambda s, w=w: s.activation(
                        out=gsil[:, 0:w], in_=ps_g[:, 0:w], func=AF.Silu), inc="ss")
                    prev_sg = sg
                    p.wait("vector", sg)
                    p.wait("vector", mmu)
                    hm = p.do("vector", lambda v, w=w: v.tensor_tensor(
                        out=hft[:, 0:w], in0=gsil[:, 0:w], in1=ps_u[:, 0:w],
                        op=ALU.mult), inc="sv")
                    prev_hmul = hm
                    p.wait("vector", hm)
                    qt_ = fp4_quant(p, c, hqb[:, 0:w], hft[:, 0:w], q_am, q_ri,
                                    q_v, q_t, q_u, q_w, q_m, w)
                    p.wait("tensor", qt_)
                    for j in range(w // 128):
                        ib = i0 // 128 + j
                        tr = p.do("tensor", lambda t, j=j: t.transpose(
                            ps_tp[:], hqb[:, j * 128:(j + 1) * 128], p._ident[:]), inc="st")
                        p.wait("vector", tr)
                        cv = p.do("vector", lambda v, ib=ib, r0=r0: v.tensor_copy(
                            out=hqT[:, ib * 2048 + r0: ib * 2048 + r0 + 128], in_=ps_tp[:]),
                            inc="sv")
                        p.wait("tensor", cv)
                i0 += w
            p.close()
            c.barrier()
            p = Ph(c, name=_phname())
            p._ident = ident
            wd_sl = p.sb("f0_wd", [128, 22 * 512], BF16)
            gev = p.sb("f0_gev", [128, 512], F32)
            ps_d = p.ps("f0_ps_d", [128, 512], F32)
            for half in range(2):
                p.fence_all("sync")
                wl = p.dma("sync", wd_sl[:].rearrange("p (k t) -> p k t", k=22),
                           c.dram["wdqT"][:, half * 512:(half + 1) * 512]
                           .rearrange("(k p) t -> p k t", p=128))
                p.wait("tensor", wl)
                for tt in range(NTT):
                    r0 = tt * 128
                    mm = None
                    for ib in range(22):
                        mm = p.do("tensor", lambda t, ib=ib, r0=r0: t.matmul(
                            ps_d[:], hqT[:, ib * 2048 + r0: ib * 2048 + r0 + 128],
                            wd_sl[:, ib * 512:(ib + 1) * 512],
                            start=(ib == 0), stop=(ib == 21)),
                            inc="st" if ib == 21 else None)
                    p.wait("vector", mm)
                    cv = p.do("vector", lambda v: v.tensor_copy(out=gev[:], in_=ps_d[:]),
                              inc="sv")
                    p.wait("sync", cv)
                    sd = p.dma("sync", c.dram["g_buf"][b, r0:r0 + 128,
                                                       half * 512:(half + 1) * 512], gev[:])
                    p.wait("vector", sd)
                    p.wait("tensor", cv)
            p.close()
            c.barrier()

def ph_l1_attn(c, ident):
    nc = c.nc
    DI = 2048
    for b in range(B_PER_CORE):
        for dr in ("f", "b"):
            rev = (dr == "b")
            _l1_inproj(c, ident, b, dr, rev)
            c.barrier()
            _l1_conv(c, ident, b, dr)
            c.barrier()
            _l1_xproj_delta(c, ident, b, dr)
            c.barrier()
            _l1_scan(c, ident, b, dr, rev)
            c.barrier()
            _l1_outproj(c, ident, b, dr)
            c.barrier()
        _l1_gate_proj(c, ident, b)
        c.barrier()


def _l1_inproj(c, ident, b, dr, rev):
    with Ph(c, name=_phname()) as p:
        p._ident = ident
        win = p.sb("i1_win", [128, 8 * 4096], BF16)
        wld = p.sb("i1_wld", [128, 4096], F32)
        xnc = p.sb("i1_xnc", [128, 8 * 512], BF16)
        ev = p.sb("i1_ev", [128, 512], BF16)
        ev2 = p.sb("i1_ev2", [128, 512], BF16)
        ps = p.ps("i1_ps", [128, 512], F32)
        wname = f"L1_attn_inproj_{dr}"
        for kb in range(8):
            ld = p.dma("sync", wld[:], c.dram[wname][kb * 128:(kb + 1) * 128, :])
            p.wait("vector", ld)
            cv = p.do("vector", lambda v, kb=kb: v.tensor_copy(
                out=win[:, kb * 4096:(kb + 1) * 4096], in_=wld[:]), inc="sv")
            p.wait("sync", cv)
        prev_ev = None
        for ts4 in range(4):
            t0 = ts4 * 512
            p.fence_all("gpsimd")
            xl = p.dma("gpsimd", xnc[:].rearrange("p (k t) -> p k t", k=8),
                       c.dram["xnT"][b, :, t0:t0 + 512].rearrange("(k p) t -> p k t", p=128))
            p.wait("tensor", xl)
            for mf in range(32):
                p.wait("tensor", prev_ev)
                mm = None
                for kb in range(8):
                    mm = p.do("tensor", lambda t, kb=kb, mf=mf: t.matmul(
                        ps[:], win[:, kb * 4096 + mf * 128: kb * 4096 + mf * 128 + 128],
                        xnc[:, kb * 512:(kb + 1) * 512],
                        start=(kb == 0), stop=(kb == 7)),
                        inc="st" if kb == 7 else None)
                tdst = (3 - ts4) * 512 if rev else t0
                if mf < 16:
                    p.wait("vector", mm)
                    if rev:
                        cv = p.do("vector", lambda v: v.tensor_copy(
                            out=ev[:], in_=ps[:, ::-1]), inc="sv")
                    else:
                        cv = p.do("vector", lambda v: v.tensor_copy(
                            out=ev[:], in_=ps[:]), inc="sv")
                    p.wait("sync", cv)
                    sd = p.dma("sync", c.dram["l1_uT"][mf * 128:(mf + 1) * 128,
                                                       tdst:tdst + 512], ev[:])
                    p.wait("vector", sd)
                    prev_ev = cv
                else:
                    mz = mf - 16
                    p.wait("scalar", mm)
                    sg = p.do("scalar", lambda s: s.activation(
                        out=ev2[:], in_=ps[:], func=AF.Silu), inc="ss")
                    p.wait("vector", sg)
                    if rev:
                        cv = p.do("vector", lambda v: v.tensor_copy(
                            out=ev[:], in_=ev2[:, ::-1]), inc="sv")
                    else:
                        cv = p.do("vector", lambda v: v.tensor_copy(
                            out=ev[:], in_=ev2[:]), inc="sv")
                    p.wait("sync", cv)
                    sd = p.dma("sync", c.dram["l1_zsT"][mz * 128:(mz + 1) * 128,
                                                        tdst:tdst + 512], ev[:])
                    p.wait("vector", sd)
                    prev_ev = cv


def _l1_conv(c, ident, b, dr):
    with Ph(c, name=_phname()) as p:
        p._ident = ident
        upad = p.sb("c1_upad", [128, 2051], BF16)
        cw = p.sb("c1_cw", [128, 4], F32)
        cb = p.sb("c1_cb", [128, 1], F32)
        mk = [p.sb(f"c1_mk{k}", [128, T], F32) for k in range(4)]
        acc = p.sb("c1_acc", [128, T], F32)
        ucb = p.sb("c1_ucb", [128, T], BF16)
        for dt in range(16):
            p.fence_all("sync")
            p.fence_all("scalar")
            p.do("vector", lambda v: v.memset(upad[:, 0:3], 0.0))
            l1 = p.dma("sync", upad[:, 3:2051],
                       c.dram["l1_uT"][dt * 128:(dt + 1) * 128, :])
            l2 = p.dma("sync", cw[:], c.dram[f"L1_attn_conv_w_{dr}"][dt * 128:(dt + 1) * 128, :])
            l3 = p.dma("sync", cb[:],
                       c.dram[f"L1_attn_conv_b_{dr}"][0:1, dt * 128:(dt + 1) * 128]
                       .rearrange("a b -> b a"), allow_slow_non_contiguous=True)
            p.wait("scalar", l1)
            p.wait("scalar", l2)
            p.wait("vector", l3)
            aa = []
            for k in range(4):
                aa.append(p.do("scalar", lambda s, k=k: s.activation(
                    out=mk[k][:], in_=upad[:, k:k + 2048], func=AF.Identity,
                    scale=cw[:, k:k + 1]), inc="ss"))
            for k in range(4):
                p.wait("vector", aa[k])
            p.do("vector", lambda v: v.tensor_tensor(
                out=acc[:], in0=mk[0][:], in1=cb[:, 0:1].to_broadcast([128, T]), op=ALU.add))
            p.do("vector", lambda v: v.tensor_tensor(out=acc[:], in0=acc[:], in1=mk[1][:],
                                                     op=ALU.add))
            p.do("vector", lambda v: v.tensor_tensor(out=acc[:], in0=acc[:], in1=mk[2][:],
                                                     op=ALU.add))
            s4 = p.do("vector", lambda v: v.tensor_tensor(
                out=acc[:], in0=acc[:], in1=mk[3][:], op=ALU.add), inc="sv")
            p.wait("scalar", s4)
            sg = p.do("scalar", lambda s: s.activation(out=ucb[:], in_=acc[:], func=AF.Silu),
                      inc="ss")
            p.wait("sync", sg)
            sd = p.dma("sync", c.dram["l1_ucT"][dt * 128:(dt + 1) * 128, :], ucb[:])
            p.wait("scalar", sd)
            p.wait("vector", sd)


def _l1_xproj_delta(c, ident, b, dr):
    with Ph(c, name=_phname()) as p:
        p._ident = ident
        xpj = p.sb("x1_xpj", [128, 16 * 96], BF16)
        wld = p.sb("x1_wld", [128, 96], F32)
        ucc = p.sb("x1_ucc", [128, 16 * 512], BF16)
        dtT = p.sb("x1_dtT", [64, 4 * 512], BF16)
        bcv = p.sb("x1_bcv", [128, 4 * 512], F32)
        dtw = p.sb("x1_dtw", [64, 2048], BF16)
        dtwl = p.sb("x1_dtwl", [64, 2048], F32)
        dtb = p.sb("x1_dtb", [128, 1], F32)
        sp0 = p.sb("x1_sp0", [128, 512], F32)
        spa = p.sb("x1_spa", [128, 512], F32)
        spe = p.sb("x1_spe", [128, 512], F32)
        spr = p.sb("x1_spr", [128, 512], F32)
        ps = p.ps("x1_ps", [128, 512], F32)
        for kb in range(16):
            ld = p.dma("sync", wld[:], c.dram[f"L1_attn_xproj_{dr}"][kb * 128:(kb + 1) * 128, :])
            p.wait("vector", ld)
            cv = p.do("vector", lambda v, kb=kb: v.tensor_copy(
                out=xpj[:, kb * 96:(kb + 1) * 96], in_=wld[:]), inc="sv")
            p.wait("sync", cv)
        ld = p.dma("sync", dtwl[:], c.dram[f"L1_attn_dt_w_{dr}"][:, :])
        p.wait("vector", ld)
        cv = p.do("vector", lambda v: v.tensor_copy(out=dtw[:], in_=dtwl[:]), inc="sv")
        p.wait("tensor", cv)
        prev_e1 = None
        prev_e2 = None
        for ts4 in range(4):
            t0 = ts4 * 512
            p.fence_all("gpsimd")
            ul = p.dma("gpsimd", ucc[:].rearrange("p (k t) -> p k t", k=16),
                       c.dram["l1_ucT"][:, t0:t0 + 512].rearrange("(k p) t -> p k t", p=128))
            p.wait("tensor", ul)
            mm = None
            for kb in range(16):
                mm = p.do("tensor", lambda t, kb=kb: t.matmul(
                    ps[0:96, :], xpj[:, kb * 96:(kb + 1) * 96],
                    ucc[:, kb * 512:(kb + 1) * 512],
                    start=(kb == 0), stop=(kb == 15)),
                    inc="st" if kb == 15 else None)
            p.wait("vector", mm)
            e1 = p.do("vector", lambda v, ts4=ts4: v.tensor_copy(
                out=dtT[:, ts4 * 512:(ts4 + 1) * 512], in_=ps[0:64, :]), inc="sv")
            e2 = p.do("vector", lambda v, ts4=ts4: v.tensor_copy(
                out=bcv[64:96, ts4 * 512:(ts4 + 1) * 512], in_=ps[64:96, :]), inc="sv")
            prev_e1, prev_e2 = e1, e2
            p.wait("tensor", e1)
            p.wait("tensor", e2)
        p.wait("sync", prev_e2)
        sd = p.dma("sync", c.dram["l1_bc"][:, :], bcv[64:96, :])
        p.wait("vector", sd)
        prev_del = None
        prev_m1 = None
        for dblk in range(16):
            p.wait("gpsimd", prev_m1)
            lb = p.dma("gpsimd", dtb[:],
                       c.dram[f"L1_attn_dt_b_{dr}"][0:1, dblk * 128:(dblk + 1) * 128]
                       .rearrange("a b -> b a"), allow_slow_non_contiguous=True)
            p.wait("vector", lb)
            for ts4 in range(4):
                p.wait("tensor", prev_del)
                mm = p.do("tensor", lambda t, dblk=dblk, ts4=ts4: t.matmul(
                    ps[:], dtw[:, dblk * 128:(dblk + 1) * 128],
                    dtT[:, ts4 * 512:(ts4 + 1) * 512],
                    start=True, stop=True), inc="st")
                p.wait("vector", mm)
                m1 = p.do("vector", lambda v: v.tensor_tensor(
                    out=sp0[:], in0=ps[:], in1=dtb[:, 0:1].to_broadcast([128, 512]),
                    op=ALU.add), inc="sv")
                prev_m1 = m1
                p.wait("scalar", m1)
                m2 = p.do("scalar", lambda s: s.activation(
                    out=spa[:], in_=sp0[:], func=AF.Abs), inc="ss")
                m3 = p.do("scalar", lambda s: s.activation(
                    out=spe[:], in_=spa[:], func=AF.Exp, scale=-1.0), inc="ss")
                m4 = p.do("scalar", lambda s: s.activation(
                    out=spe[:], in_=spe[:], func=AF.Ln, bias=1.0), inc="ss")
                p.wait("vector", m4)
                m5 = p.do("vector", lambda v: v.tensor_scalar(
                    out=spr[:], in0=sp0[:], scalar1=0.0, scalar2=None,
                    op0=ALU.max))
                m6 = p.do("vector", lambda v: v.tensor_tensor(
                    out=sp0[:], in0=spr[:], in1=spe[:], op=ALU.add), inc="sv")
                prev_del = m6
                p.wait("sync", m6)
                sd = p.dma("sync", c.dram["l1_dT"][dblk * 128:(dblk + 1) * 128,
                                                   ts4 * 512:(ts4 + 1) * 512], sp0[:])
                p.wait("vector", sd)


def _l1_scan(c, ident, b, dr, rev):
    DI = 2048
    with Ph(c, name=_phname()) as p:
        p._ident = ident
        a_all = p.sb("s1_a", [128, 16 * 16], F32)
        dp_all = p.sb("s1_dp", [128, 16], F32)
        alog = p.sb("s1_alog", [128, 16], F32)
        ones_r = p.sb("s1_ones", [1, 128], F32)
        br = [p.sb(f"s1_br{i}", [1, T], F32) for i in range(2)]
        cr = [p.sb(f"s1_cr{i}", [1, T], F32) for i in range(2)]
        bb = [p.sb(f"s1_bb{i}", [128, T], BF16) for i in range(2)]
        cbb = [p.sb(f"s1_cb{i}", [128, T], BF16) for i in range(2)]
        dlt = p.sb("s1_dlt", [128, 4 * T], F32)
        duq = p.sb("s1_du", [128, 4 * T], BF16)
        yac = p.sb("s1_yac", [128, 4 * T], F32)
        ucb = p.sb("s1_ucb", [128, T], BF16)
        da = [p.sb(f"s1_da{i}", [128, T], F32) for i in range(3)]
        xx = [p.sb(f"s1_xx{i}", [128, T], BF16) for i in range(3)]
        hh = [p.sb(f"s1_hh{i}", [128, T], BF16) for i in range(3)]
        hc = [p.sb(f"s1_hc{i}", [128, T], BF16) for i in range(3)]
        ucdp = p.sb("s1_ucdp", [128, T], F32)
        zs = p.sb("s1_zs", [128, T], BF16)
        ps_bc = [p.ps(f"s1_ps_bc{i}", [128, 512], F32) for i in range(2)]
        p.do("vector", lambda v: v.memset(ones_r[:], 1.0))
        for dt in range(16):
            la = p.dma("sync", alog[:], c.dram[f"L1_attn_Alog_{dr}"][dt * 128:(dt + 1) * 128, :])
            p.wait("scalar", la)
            ea = p.do("scalar", lambda s: s.activation(out=alog[:], in_=alog[:], func=AF.Exp),
                      inc="ss")
            p.wait("vector", ea)
            na = p.do("vector", lambda v, dt=dt: v.tensor_scalar(
                out=a_all[:, dt * 16:(dt + 1) * 16], in0=alog[:], scalar1=-1.0,
                scalar2=None, op0=ALU.mult), inc="sv")
            p.wait("sync", na)
        ldp = p.dma("sync", dp_all[:, 0:16],
                    c.dram[f"L1_attn_Dp_{dr}"][0:1, :].rearrange("a (dt p) -> (a p) dt", p=128),
                    allow_slow_non_contiguous=True)
        p.wait("scalar", ldp)
        p.wait("scalar", na)
        prev_q_ya = None

        for qtr in range(4):
            q0 = qtr * 4
            p.fence_all("sync")
            dl = None
            for i in range(4):
                dl = p.dma("sync", dlt[:, i * T:(i + 1) * T],
                           c.dram["l1_dT"][(q0 + i) * 128:(q0 + i + 1) * 128, :])
            p.wait("vector", dl)
            p.wait("gpsimd", dl)
            p.wait("scalar", dl)
            duk = None
            for i in range(4):
                ul = p.dma("gpsimd", ucb[:], c.dram["l1_ucT"][(q0 + i) * 128:(q0 + i + 1) * 128, :])
                p.wait("vector", ul)
                duk = p.do("vector", lambda v, i=i: v.tensor_tensor(
                    out=duq[:, i * T:(i + 1) * T], in0=dlt[:, i * T:(i + 1) * T],
                    in1=ucb[:], op=ALU.mult), inc="sv")
                p.wait("gpsimd", duk)
            p.wait("vector", prev_q_ya)
            mz = p.do("vector", lambda v: v.memset(yac[:], 0.0), inc="sv")
            p.wait("scalar", mz)
            p.wait("tensor", mz)
            p.wait("gpsimd", mz)

            NITER = 16 * 4
            tok_hs = [None] * NITER
            tok_ad = [None] * NITER
            tok_hcm = [None] * NITER
            tok_ya = [None] * NITER
            tok_bev = [None] * 16
            tok_lbc = [None] * 16

            def emit_bcast(n):
                sl_ = n % 2
                if n >= 2 and tok_ya[(n - 2) * 4 + 3] is not None:
                    p.wait("sync", tok_ya[(n - 2) * 4 + 3])
                    p.wait("tensor", tok_ya[(n - 2) * 4 + 3])
                lb = p.dma("sync", br[sl_][:], c.dram["l1_bc"][n:n + 1, :])
                lc = p.dma("sync", cr[sl_][:], c.dram["l1_bc"][16 + n:17 + n, :])
                p.wait("tensor", lc)
                if n >= 1:
                    p.wait("tensor", tok_bev[n - 1])
                bev = None
                evs = [None] * 8
                for j in range(8):
                    row, dst = (br, bb) if j < 4 else (cr, cbb)
                    sl2 = j % 4
                    bk = j % 2
                    if j >= 2:
                        p.wait("tensor", evs[j - 2])
                    bmm = p.do("tensor", lambda t, row=row, sl2=sl2, sl_=sl_, bk=bk: t.matmul(
                        ps_bc[bk][:], ones_r[0:1, :], row[sl_][0:1, sl2 * 512:(sl2 + 1) * 512],
                        start=True, stop=True), inc="st")
                    p.wait("vector", bmm)
                    bev = p.do("vector", lambda v, dst=dst, sl2=sl2, sl_=sl_, bk=bk: v.tensor_copy(
                        out=dst[sl_][:, sl2 * 512:(sl2 + 1) * 512], in_=ps_bc[bk][:]), inc="sv")
                    evs[j] = bev
                tok_bev[n] = bev

            emit_bcast(0)
            for n in range(16):
                sl_ = n % 2
                if n + 1 < 16:
                    emit_bcast(n + 1)
                for i in range(4):
                    k = n * 4 + i
                    kb3 = k % 3
                    dt = q0 + i
                    if k >= 3:
                        p.wait("scalar", tok_hs[k - 3])
                    ad = p.do("scalar", lambda s, i=i, dt=dt, n=n, kb3=kb3: s.activation(
                        out=da[kb3][:], in_=dlt[:, i * T:(i + 1) * T], func=AF.Exp,
                        scale=a_all[:, dt * 16 + n: dt * 16 + n + 1]), inc="ss")
                    tok_ad[k] = ad
                    p.wait("vector", tok_bev[n])
                    xm = p.do("vector", lambda v, i=i, kb3=kb3, sl_=sl_: v.tensor_tensor(
                        out=xx[kb3][:], in0=duq[:, i * T:(i + 1) * T], in1=bb[sl_][:],
                        op=ALU.mult), inc="sv")
                    p.wait("vector", ad)
                    hs = p.do("vector", lambda v, kb3=kb3: v.tensor_tensor_scan(
                        out=hh[kb3][:], data0=da[kb3][:], data1=xx[kb3][:], initial=0.0,
                        op0=ALU.mult, op1=ALU.add), inc="sv")
                    tok_hs[k] = hs
                    if k >= 3:
                        p.wait("vector", tok_ya[k - 3])
                    hcm = p.do("vector", lambda v, kb3=kb3, sl_=sl_: v.tensor_tensor(
                        out=hc[kb3][:], in0=hh[kb3][:], in1=cbb[sl_][:], op=ALU.mult),
                        inc="sv")
                    tok_hcm[k] = hcm
                    p.wait("gpsimd", hcm)
                    ya = p.do("gpsimd", lambda g, i=i, kb3=kb3: g.tensor_tensor(
                        out=yac[:, i * T:(i + 1) * T], in0=yac[:, i * T:(i + 1) * T],
                        in1=hc[kb3][:], op=ALU.add), inc="sg")
                    tok_ya[k] = ya
            for i in range(4):
                dt = q0 + i
                p.fence_all("gpsimd")
                ul = p.dma("gpsimd", ucb[:], c.dram["l1_ucT"][dt * 128:(dt + 1) * 128, :])
                zl = p.dma("gpsimd", zs[:], c.dram["l1_zsT"][dt * 128:(dt + 1) * 128, :])
                p.wait("scalar", ul)
                p.wait("scalar", tok_hs[63])
                ud = p.do("scalar", lambda s, dt=dt: s.activation(
                    out=ucdp[:], in_=ucb[:], func=AF.Identity,
                    scale=dp_all[:, dt:dt + 1]), inc="ss")
                p.wait("vector", ud)
                p.wait("vector", tok_ya[63])
                yf = p.do("vector", lambda v, i=i: v.tensor_tensor(
                    out=ucdp[:], in0=yac[:, i * T:(i + 1) * T], in1=ucdp[:],
                    op=ALU.add), inc="sv")
                p.wait("vector", zl)
                p.wait("vector", yf)
                if rev:
                    yg = p.do("vector", lambda v: v.tensor_tensor(
                        out=hh[0][:], in0=ucdp[:], in1=zs[:], op=ALU.mult), inc="sv")
                    p.wait("vector", yg)
                    yg = p.do("vector", lambda v: v.tensor_copy(
                        out=hc[0][:], in_=hh[0][:, ::-1]), inc="sv")
                else:
                    yg = p.do("vector", lambda v: v.tensor_tensor(
                        out=hc[0][:], in0=ucdp[:], in1=zs[:], op=ALU.mult), inc="sv")
                p.wait("sync", yg)
                sd = p.dma("sync", c.dram["l1_ygT"][dt * 128:(dt + 1) * 128, :], hc[0][:])
                p.wait("vector", sd)
                p.wait("gpsimd", sd)
            prev_q_ya = tok_ya[63]


def _l1_outproj(c, ident, b, dr):
    with Ph(c, name=_phname()) as p:
        p._ident = ident
        opw = p.sb("o1_opw", [128, 16 * 1024], BF16)
        wld = p.sb("o1_wld", [128, 1024], F32)
        ygc = p.sb("o1_ygc", [128, 16 * 512], BF16)
        ev = p.sb("o1_ev", [128, 512], BF16)
        ps = p.ps("o1_ps", [128, 512], F32)
        wname = f"L1_attn_outproj_{dr}"
        dstname = "l1_yfT" if dr == "f" else "l1_ybT"
        for kb in range(16):
            ld = p.dma("sync", wld[:], c.dram[wname][kb * 128:(kb + 1) * 128, :])
            p.wait("vector", ld)
            cv = p.do("vector", lambda v, kb=kb: v.tensor_copy(
                out=opw[:, kb * 1024:(kb + 1) * 1024], in_=wld[:]), inc="sv")
            p.wait("sync", cv)
        prev_ev = None
        for ts4 in range(4):
            t0 = ts4 * 512
            p.fence_all("gpsimd")
            yl = p.dma("gpsimd", ygc[:].rearrange("p (k t) -> p k t", k=16),
                       c.dram["l1_ygT"][:, t0:t0 + 512].rearrange("(k p) t -> p k t", p=128))
            p.wait("tensor", yl)
            for mf in range(8):
                p.wait("tensor", prev_ev)
                mm = None
                for kb in range(16):
                    mm = p.do("tensor", lambda t, kb=kb, mf=mf: t.matmul(
                        ps[:], opw[:, kb * 1024 + mf * 128: kb * 1024 + mf * 128 + 128],
                        ygc[:, kb * 512:(kb + 1) * 512],
                        start=(kb == 0), stop=(kb == 15)),
                        inc="st" if kb == 15 else None)
                p.wait("vector", mm)
                cv = p.do("vector", lambda v: v.tensor_copy(out=ev[:], in_=ps[:]), inc="sv")
                prev_ev = cv
                p.wait("sync", cv)
                sd = p.dma("sync", c.dram[dstname][mf * 128:(mf + 1) * 128, t0:t0 + 512], ev[:])
                p.wait("vector", sd)


def _l1_gate_proj(c, ident, b):
    with Ph(c, name=_phname()) as p:
        p._ident = ident
        gw = p.sb("g1_gw", [128, 16 * 1024], BF16)
        pw = p.sb("g1_pw", [128, 8 * 1024], BF16)
        wld = p.sb("g1_wld", [128, 1024], F32)
        yc = p.sb("g1_yc", [128, 16 * 512], BF16)
        gb_c = p.sb("g1_gb", [128, 8], F32)
        pb_r = p.sb("g1_pb", [1, 1024], F32)
        ones_r = p.sb("g1_ones", [1, 128], F32)
        gsig = p.sb("g1_gsig", [128, 512], BF16)
        ycmb = p.sb("g1_ycmb", [128, 8 * 512], BF16)
        sev = p.sb("g1_sev", [128, 512], F32)
        tmp = p.sb("g1_tmp", [128, 512], F32)
        ps = p.ps("g1_ps", [128, 512], F32)
        ps2 = p.ps("g1_ps2", [128, 512], F32)
        p.do("vector", lambda v: v.memset(ones_r[:], 1.0))
        for kb in range(16):
            ld = p.dma("sync", wld[:], c.dram["L1_attn_gate_w"][kb * 128:(kb + 1) * 128, :])
            p.wait("vector", ld)
            cv = p.do("vector", lambda v, kb=kb: v.tensor_copy(
                out=gw[:, kb * 1024:(kb + 1) * 1024], in_=wld[:]), inc="sv")
            p.wait("sync", cv)
        for kb in range(8):
            ld = p.dma("sync", wld[:], c.dram["L1_attn_proj_w"][kb * 128:(kb + 1) * 128, :])
            p.wait("vector", ld)
            cv = p.do("vector", lambda v, kb=kb: v.tensor_copy(
                out=pw[:, kb * 1024:(kb + 1) * 1024], in_=wld[:]), inc="sv")
            p.wait("sync", cv)
        l1 = p.dma("sync", gb_c[:, 0:8],
                   c.dram["L1_attn_gate_b"][0:1, :].rearrange("a (m p) -> (a p) m", p=128),
                   allow_slow_non_contiguous=True)
        l2 = p.dma("sync", pb_r[:], c.dram["L1_attn_proj_b"][:, :])
        p.wait("scalar", l1)
        p.wait("tensor", l2)
        prev_pj = None
        for ts4 in range(4):
            t0 = ts4 * 512
            p.fence_all("gpsimd")
            yl = p.dma("gpsimd", yc[:, 0:8 * 512].rearrange("p (k t) -> p k t", k=8),
                       c.dram["l1_yfT"][:, t0:t0 + 512].rearrange("(k p) t -> p k t", p=128))
            yl = p.dma("gpsimd", yc[:, 8 * 512:16 * 512].rearrange("p (k t) -> p k t", k=8),
                       c.dram["l1_ybT"][:, t0:t0 + 512].rearrange("(k p) t -> p k t", p=128))
            p.wait("tensor", yl)
            for mf in range(8):
                p.wait("tensor", prev_pj)
                mm = None
                for kb in range(16):
                    mm = p.do("tensor", lambda t, kb=kb, mf=mf: t.matmul(
                        ps[:], gw[:, kb * 1024 + mf * 128: kb * 1024 + mf * 128 + 128],
                        yc[:, kb * 512:(kb + 1) * 512],
                        start=(kb == 0), stop=(kb == 15)),
                        inc="st" if kb == 15 else None)
                p.wait("scalar", mm)
                sg = p.do("scalar", lambda s, mf=mf: s.activation(
                    out=gsig[:], in_=ps[:], func=AF.Sigmoid,
                    bias=gb_c[:, mf:mf + 1], scale=1.0), inc="ss")
                p.wait("vector", sg)
                d1 = p.do("vector", lambda v, mf=mf: v.tensor_tensor(
                    out=tmp[:], in0=yc[:, mf * 512:(mf + 1) * 512],
                    in1=yc[:, (8 + mf) * 512:(9 + mf) * 512], op=ALU.subtract), inc="sv")
                p.wait("vector", d1)
                d2 = p.do("vector", lambda v: v.tensor_tensor(
                    out=tmp[:], in0=tmp[:], in1=gsig[:], op=ALU.mult))
                d3 = p.do("vector", lambda v, mf=mf: v.tensor_tensor(
                    out=ycmb[:, mf * 512:(mf + 1) * 512], in0=tmp[:],
                    in1=yc[:, (8 + mf) * 512:(9 + mf) * 512], op=ALU.add), inc="sv")
                prev_pj = d3
                p.wait("tensor", d3)
            for tq in range(4):
                tok0 = t0 + tq * 128
                for half in range(2):
                    mm = None
                    for kb in range(8):
                        mm = p.do("tensor", lambda t, kb=kb, tq=tq, half=half: t.matmul(
                            ps2[:], ycmb[:, kb * 512 + tq * 128: kb * 512 + tq * 128 + 128],
                            pw[:, kb * 1024 + half * 512: kb * 1024 + (half + 1) * 512],
                            start=(kb == 0), stop=False), inc=None)
                    mm = p.do("tensor", lambda t, half=half: t.matmul(
                        ps2[:], ones_r[0:1, :], pb_r[0:1, half * 512:(half + 1) * 512],
                        start=False, stop=True), inc="st")
                    p.wait("vector", mm)
                    cv = p.do("vector", lambda v: v.tensor_copy(out=sev[:], in_=ps2[:]),
                              inc="sv")
                    p.wait("sync", cv)
                    sd = p.dma("sync", c.dram["s_buf"][b, tok0:tok0 + 128,
                                                       half * 512:(half + 1) * 512], sev[:])
                    p.wait("vector", sd)
                    p.wait("tensor", cv)
                prev_pj = cv

def ph_l1_ffn(c, ident):
    _l1_router(c, ident)
    c.barrier()
    for e in range(-1, 8):
        _l1_expert(c, ident, e)
        c.barrier()


def _l1_router(c, ident):
    nc = c.nc
    with Ph(c, name=_phname()) as p:
        p._ident = ident
        xt = p.sb("r1_x", [128, D], BF16)
        xm = p.sb("r1_xm", [128, D], F32)
        sq = p.sb("r1_sq", [128, D], F32)
        x2 = p.sb("r1_x2", [128, D], BF16)
        x2T = p.sb("r1_x2T", [128, 8 * 128], BF16)
        gamb = p.sb("r1_gamb", [128, D], F32)
        betb = p.sb("r1_betb", [128, D], F32)
        grow = p.sb("r1_grow", [1, D], F32)
        gwt = p.sb("r1_gwt", [128, 8 * 8], BF16)
        gwl = p.sb("r1_gwl", [128, 8], F32)
        st = p.sb("r1_st", [128, 8], F32)
        pr = p.sb("r1_pr", [128, 8], F32)
        mx8 = p.sb("r1_mx8", [128, 8], F32)
        msk = p.sb("r1_msk", [128, 8], U8)
        wv = p.sb("r1_wv", [128, 8], F32)
        ps_b = p.ps("r1_ps_b", [128, 512], F32)
        ps_t = p.ps("r1_ps_t", [128, 128], BF16)
        ps_l = p.ps("r1_ps_l", [128, 8], F32)
        ones_r = p.sb("r1_ones", [1, 128], F32)
        p.do("vector", lambda v: v.memset(ones_r[:], 1.0))
        last_bc_mm = None
        for nm, dst in (("L1_ffn_ln_scale", gamb), ("L1_ffn_ln_bias", betb)):
            p.wait("sync", last_bc_mm)
            ld = p.dma("sync", grow[:], c.dram[nm][:, :])
            p.wait("tensor", ld)
            for sl in range(2):
                mm = p.do("tensor", lambda t, sl=sl: t.matmul(
                    ps_b[:], ones_r[0:1, :], grow[0:1, sl * 512:(sl + 1) * 512],
                    start=True, stop=True), inc="st")
                last_bc_mm = mm
                p.wait("vector", mm)
                cv = p.do("vector", lambda v, sl=sl, dst=dst: v.tensor_copy(
                    out=dst[:, sl * 512:(sl + 1) * 512], in_=ps_b[:]), inc="sv")
                p.wait("tensor", cv)
        gwl8 = p.sb("r1_gwl8", [128, 64], F32)
        ld = p.dma("sync", gwl8[:].rearrange("p (k e) -> p k e", k=8),
                   c.dram["L1_ffn_gate_wT"][:, :].rearrange("(k p) e -> p k e", p=128))
        p.wait("vector", ld)
        cv = p.do("vector", lambda v: v.tensor_copy(out=gwt[:], in_=gwl8[:]), inc="sv")
        p.wait("sync", cv)
        for b in range(B_PER_CORE):
            for tt in range(NTT):
                r0 = tt * 128
                p.fence_all("sync")
                ld = p.dma("sync", xt[:], c.dram["xn_tok"][b, r0:r0 + 128, :])
                p.wait("vector", ld)
                rs = p.do("vector", lambda v: v.reduce_sum(st[:, 0:1], xt[:], axis=AX.X),
                          inc="sv")
                p.wait("vector", rs)
                nm_ = p.do("vector", lambda v: v.tensor_scalar(
                    out=st[:, 1:2], in0=st[:, 0:1], scalar1=1.0 / D, scalar2=None,
                    op0=ALU.mult), inc="sv")
                p.wait("vector", nm_)
                xms = p.do("vector", lambda v: v.tensor_tensor(
                    out=xm[:], in0=xt[:], in1=st[:, 1:2].to_broadcast([128, D]),
                    op=ALU.subtract), inc="sv")
                p.wait("scalar", xms)
                a2 = p.do("scalar", lambda s: s.activation(out=sq[:], in_=xm[:], func=AF.Square,
                                                           accum_out=st[:, 2:3]), inc="ss")
                p.wait("vector", a2)
                v1 = p.do("vector", lambda v: v.tensor_scalar(
                    out=st[:, 3:4], in0=st[:, 2:3], scalar1=1.0 / D, scalar2=1e-5,
                    op0=ALU.mult, op1=ALU.add), inc="sv")
                p.wait("scalar", v1)
                a3 = p.do("scalar", lambda s: s.activation(out=st[:, 4:5], in_=st[:, 3:4],
                                                           func=AF.Sqrt), inc="ss")
                p.wait("vector", a3)
                rc = p.do("vector", lambda v: v.reciprocal(st[:, 5:6], st[:, 4:5]), inc="sv")
                p.wait("vector", rc)
                n1 = p.do("vector", lambda v: v.tensor_tensor(
                    out=xm[:], in0=xm[:], in1=st[:, 5:6].to_broadcast([128, D]),
                    op=ALU.mult), inc="sv")
                p.wait("vector", n1)
                n2 = p.do("vector", lambda v: v.tensor_tensor(
                    out=xm[:], in0=xm[:], in1=gamb[:], op=ALU.mult))
                n3 = p.do("vector", lambda v: v.tensor_tensor(
                    out=x2[:], in0=xm[:], in1=betb[:], op=ALU.add), inc="sv")
                p.wait("tensor", n3)
                cvl = None
                for k in range(8):
                    tr = p.do("tensor", lambda t, k=k: t.transpose(
                        ps_t[:], x2[:, k * 128:(k + 1) * 128], p._ident[:]), inc="st")
                    p.wait("vector", tr)
                    cvl = p.do("vector", lambda v, k=k: v.tensor_copy(
                        out=x2T[:, k * 128:(k + 1) * 128], in_=ps_t[:]), inc="sv")
                    p.wait("tensor", cvl)
                mm = None
                for kb in range(8):
                    mm = p.do("tensor", lambda t, kb=kb: t.matmul(
                        ps_l[:], x2T[:, kb * 128:(kb + 1) * 128], gwt[:, kb * 8:(kb + 1) * 8],
                        start=(kb == 0), stop=(kb == 7)), inc="st" if kb == 7 else None)
                p.wait("vector", mm)
                if getattr(c, "_dbg_l1r", False):
                    dl_ = p.do("vector", lambda v: v.tensor_copy(out=wv[:], in_=ps_l[:]),
                               inc="sv")
                    p.wait("sync", dl_)
                    dd = p.dma("sync", c.dram["l1_dbg_lg"][b, r0:r0 + 128, :], wv[:])
                    p.wait("vector", dd)
                r1 = p.do("vector", lambda v: v.reduce_max(st[:, 6:7], ps_l[:], axis=AX.X,
                                                           negate=True), inc="sv")
                p.wait("vector", r1)
                sb_ = p.do("vector", lambda v: v.tensor_tensor(
                    out=pr[:], in0=ps_l[:], in1=st[:, 6:7].to_broadcast([128, 8]),
                    op=ALU.add), inc="sv")
                p.wait("scalar", sb_)
                ex = p.do("scalar", lambda s: s.activation(out=pr[:], in_=pr[:], func=AF.Exp,
                                                           accum_out=st[:, 7:8]), inc="ss")
                p.wait("vector", ex)
                rc2 = p.do("vector", lambda v: v.reciprocal(st[:, 0:1], st[:, 7:8]), inc="sv")
                p.wait("vector", rc2)
                pn = p.do("vector", lambda v: v.tensor_tensor(
                    out=pr[:], in0=pr[:], in1=st[:, 0:1].to_broadcast([128, 8]),
                    op=ALU.mult), inc="sv")
                p.wait("vector", pn)
                m8 = p.do("vector", lambda v: v.max(out=mx8[:], in_=pr[:]), inc="sv")
                p.wait("vector", m8)
                w1 = p.do("vector", lambda v: v.tensor_tensor(
                    out=st[:, 1:2], in0=mx8[:, 0:1], in1=mx8[:, 1:2], op=ALU.add), inc="sv")
                p.wait("vector", w1)
                w2 = p.do("vector", lambda v: v.tensor_scalar(
                    out=st[:, 2:3], in0=st[:, 1:2], scalar1=1e-5, scalar2=None,
                    op0=ALU.add), inc="sv")
                p.wait("vector", w2)
                w3 = p.do("vector", lambda v: v.reciprocal(st[:, 3:4], st[:, 2:3]), inc="sv")
                p.wait("vector", w3)
                c1 = p.do("vector", lambda v: v.tensor_tensor(
                    out=msk[:], in0=pr[:], in1=mx8[:, 1:2].to_broadcast([128, 8]),
                    op=ALU.is_ge), inc="sv")
                p.wait("vector", c1)
                c2 = p.do("vector", lambda v: v.tensor_tensor(
                    out=wv[:], in0=pr[:], in1=st[:, 3:4].to_broadcast([128, 8]),
                    op=ALU.mult), inc="sv")
                p.wait("vector", c2)
                zz = p.do("vector", lambda v: v.memset(st[:, 4:5], 0.0), inc="sv")
                p.wait("vector", zz)
                c4 = p.do("vector", lambda v: v.select(wv[:], msk[:], wv[:],
                                                       st[:, 4:5].to_broadcast([128, 8])),
                          inc="sv")
                p.wait("sync", c4)
                sd = p.dma("sync", c.dram["l1_moew"][b, r0:r0 + 128, :], wv[:])
                p.wait("vector", sd)
                if getattr(c, "_dbg_l1r", False):
                    d1 = p.dma("sync", c.dram["l1_dbg_pr"][b, r0:r0 + 128, :], pr[:])
                    d2 = p.dma("sync", c.dram["l1_dbg_mx"][b, r0:r0 + 128, :], mx8[:])
                    p.wait("vector", d1)
                    p.wait("vector", d2)


def _l1_expert(c, ident, e):
    nc = c.nc
    with Ph(c, name=_phname()) as p:
        p._ident = ident
        wg_e = p.sb("e1_wg", [128, 8 * 2048], BF16)
        wu_e = p.sb("e1_wu", [128, 8 * 2048], BF16)
        wd_e = p.sb("e1_wd", [128, 16 * 1024], BF16)
        wld = p.sb("e1_wld", [128, 2048], F32)
        xnc = p.sb("e1_xnc", [128, 8 * 512], BF16)
        itT = p.sb("e1_itT", [128, 16 * 512], BF16)
        gsil = p.sb("e1_gsil", [128, 512], F32)
        wcol = p.sb("e1_wcol", [128, 1], F32)
        acct = p.sb("e1_acct", [128, 512], F32)
        wout = p.sb("e1_wout", [128, 512], F32)
        ps_g = p.ps("e1_ps_g", [128, 512], F32)
        ps_u = p.ps("e1_ps_u", [128, 512], F32)
        ps_d = p.ps("e1_ps_d", [128, 512], F32)
        if e < 0:
            srcs = [("L1_ffn_sWg", wg_e, 2048, None), ("L1_ffn_sWu", wu_e, 2048, None),
                    ("L1_ffn_sWd", wd_e, 1024, None)]
        else:
            srcs = [("L1_ffn_Wg", wg_e, 2048, e), ("L1_ffn_Wu", wu_e, 2048, e),
                    ("L1_ffn_Wd", wd_e, 1024, e)]
        for nm, dst, ncol, ei in srcs:
            nkb = 8 if ncol == 2048 else 16
            for kb in range(nkb):
                if ei is None:
                    src = c.dram[nm][kb * 128:(kb + 1) * 128, :]
                else:
                    src = c.dram[nm][ei, kb * 128:(kb + 1) * 128, :]
                ld = p.dma("sync", wld[:, 0:ncol], src)
                p.wait("vector", ld)
                cv = p.do("vector", lambda v, kb=kb, dst=dst, ncol=ncol: v.tensor_copy(
                    out=dst[:, kb * ncol:(kb + 1) * ncol], in_=wld[:, 0:ncol]), inc="sv")
                p.wait("sync", cv)
        prev_sd = None
        for b in range(B_PER_CORE):
            prev_it = None
            prev_ev = None
            for ts4 in range(4):
                t0 = ts4 * 512
                p.fence_all("gpsimd")
                xl = p.dma("gpsimd", xnc[:].rearrange("p (k t) -> p k t", k=8),
                           c.dram["xnT"][b, :, t0:t0 + 512].rearrange("(k p) t -> p k t", p=128))
                p.wait("tensor", xl)
                for ib in range(16):
                    p.wait("tensor", prev_it)
                    mmg = None
                    for kb in range(8):
                        mmg = p.do("tensor", lambda t, kb=kb, ib=ib: t.matmul(
                            ps_g[:], wg_e[:, kb * 2048 + ib * 128: kb * 2048 + ib * 128 + 128],
                            xnc[:, kb * 512:(kb + 1) * 512],
                            start=(kb == 0), stop=(kb == 7)), inc="st" if kb == 7 else None)
                    mmu = None
                    for kb in range(8):
                        mmu = p.do("tensor", lambda t, kb=kb, ib=ib: t.matmul(
                            ps_u[:], wu_e[:, kb * 2048 + ib * 128: kb * 2048 + ib * 128 + 128],
                            xnc[:, kb * 512:(kb + 1) * 512],
                            start=(kb == 0), stop=(kb == 7)), inc="st" if kb == 7 else None)
                    p.wait("scalar", mmg)
                    sg = p.do("scalar", lambda s: s.activation(
                        out=gsil[:], in_=ps_g[:], func=AF.Silu), inc="ss")
                    p.wait("vector", sg)
                    p.wait("vector", mmu)
                    it = p.do("vector", lambda v, ib=ib: v.tensor_tensor(
                        out=itT[:, ib * 512:(ib + 1) * 512], in0=gsil[:], in1=ps_u[:],
                        op=ALU.mult), inc="sv")
                    prev_it = it
                p.wait("tensor", prev_it)
                for tq in range(4):
                    tok0 = t0 + tq * 128
                    lw = p.dma("gpsimd", wcol[:],
                               c.dram["l1_moew"][b, tok0:tok0 + 128, e:e + 1],
                               allow_slow_non_contiguous=True) if e >= 0 else None
                    if e >= 0:
                        p.wait("scalar", lw)
                    for half in range(2):
                        mm = None
                        for kb in range(16):
                            mm = p.do("tensor", lambda t, kb=kb, tq=tq, half=half: t.matmul(
                                ps_d[:], itT[:, kb * 512 + tq * 128: kb * 512 + tq * 128 + 128],
                                wd_e[:, kb * 1024 + half * 512: kb * 1024 + (half + 1) * 512],
                                start=(kb == 0), stop=(kb == 15)),
                                inc="st" if kb == 15 else None)
                        if e < 0:
                            p.wait("vector", mm)
                            cv = p.do("vector", lambda v: v.tensor_copy(
                                out=wout[:], in_=ps_d[:]), inc="sv")
                            p.wait("sync", cv)
                            sd = p.dma("sync", c.dram["g_buf"][b, tok0:tok0 + 128,
                                                              half * 512:(half + 1) * 512],
                                       wout[:])
                            p.wait("vector", sd)
                            p.wait("tensor", cv)
                            prev_ev = cv
                        else:
                            p.wait("scalar", mm)
                            wo_ = p.do("scalar", lambda s: s.activation(
                                out=wout[:], in_=ps_d[:], func=AF.Identity,
                                scale=wcol[:, 0:1]), inc="ss")
                            p.wait("sync", prev_sd)
                            la = p.dma("sync", acct[:],
                                       c.dram["g_buf"][b, tok0:tok0 + 128,
                                                       half * 512:(half + 1) * 512])
                            p.wait("vector", la)
                            p.wait("vector", wo_)
                            ad = p.do("vector", lambda v: v.tensor_tensor(
                                out=acct[:], in0=acct[:], in1=wout[:], op=ALU.add), inc="sv")
                            p.wait("sync", ad)
                            sd = p.dma("sync", c.dram["g_buf"][b, tok0:tok0 + 128,
                                                              half * 512:(half + 1) * 512],
                                       acct[:])
                            prev_sd = sd
                            p.wait("vector", sd)
                            p.wait("scalar", sd)
                            p.wait("tensor", wo_)
                            prev_ev = wo_
                    if e >= 0:
                        p.wait("gpsimd", prev_ev)

def ph_l2_attn(c, ident):
    nc = c.nc
    with ExitStack() as outer:
        oe = outer.enter_context
        c._l2_qt = oe(nc.sbuf_tensor("l2_qt", [128, 8 * 2048], BF16))
        c._l2_kt = oe(nc.sbuf_tensor("l2_kt", [128, 8 * 2048], BF16))
        c._l2_ktok = oe(nc.sbuf_tensor("l2_ktok", [128, 16 * 1024], BF16))
        c._l2_vt = oe(nc.sbuf_tensor("l2_vt", [128, 16 * 1024], BF16))
        for b in range(B_PER_CORE):
            _l2_proj(c, ident, b)
            c.barrier()
            _l2_linattn(c, ident, b)
            c.barrier()
            _l2_out(c, ident, b)
            c.barrier()


def _elu1_evac(p, c, ps, out_bf, t1, t2, width=512):
    e1 = p.do("vector", lambda v: v.tensor_scalar(
        out=t1[:, 0:width], in0=ps, scalar1=0.0, scalar2=None, op0=ALU.min), inc="sv")
    e2 = p.do("vector", lambda v: v.tensor_scalar(
        out=t2[:, 0:width], in0=ps, scalar1=0.0, scalar2=None, op0=ALU.max), inc="sv")
    p.wait("scalar", e1)
    e3 = p.do("scalar", lambda s: s.activation(out=t1[:, 0:width], in_=t1[:, 0:width],
                                               func=AF.Exp), inc="ss")
    p.wait("vector", e3)
    e4 = p.do("vector", lambda v: v.tensor_tensor(out=out_bf, in0=t2[:, 0:width],
                                                  in1=t1[:, 0:width], op=ALU.add), inc="sv")
    return e4


def _l2_proj(c, ident, b):
    nc = c.nc
    with Ph(c, name=_phname()) as p:
        p._ident = ident
        qt = c._l2_qt; kt = c._l2_kt; ktok = c._l2_ktok; vt = c._l2_vt
        wsb = p.sb("p2_w", [128, 8 * 1024], BF16)
        wld = p.sb("p2_wld", [128, 1024], F32)
        xnc = p.sb("p2_xnc", [128, 8 * 512], BF16)
        xnt = p.sb("p2_xnt", [128, 8 * 128], BF16)
        t1 = p.sb("p2_t1", [128, 512], F32)
        t2 = p.sb("p2_t2", [128, 512], F32)
        ps = p.ps("p2_ps", [128, 512], F32)

        def load_w(name):
            for kb in range(8):
                ld = p.dma("sync", wld[:], c.dram[name][kb * 128:(kb + 1) * 128, :])
                p.wait("vector", ld)
                cv = p.do("vector", lambda v, kb=kb: v.tensor_copy(
                    out=wsb[:, kb * 1024:(kb + 1) * 1024], in_=wld[:]), inc="sv")
                p.wait("sync", cv)
            return cv

        for wname, dst, feat in (("L2_attn_Wq", qt, True), ("L2_attn_Wk", kt, True)):
            lw = load_w(wname)
            p.wait("tensor", lw)
            prev_e = None
            for ts4 in range(4):
                t0 = ts4 * 512
                p.fence_all("gpsimd")
                xl = p.dma("gpsimd", xnc[:].rearrange("p (k t) -> p k t", k=8),
                           c.dram["xnT"][b, :, t0:t0 + 512].rearrange("(k p) t -> p k t", p=128))
                p.wait("tensor", xl)
                for mf in range(8):
                    p.wait("tensor", prev_e)
                    mm = None
                    for kb in range(8):
                        mm = p.do("tensor", lambda t, kb=kb, mf=mf: t.matmul(
                            ps[:], wsb[:, kb * 1024 + mf * 128: kb * 1024 + mf * 128 + 128],
                            xnc[:, kb * 512:(kb + 1) * 512],
                            start=(kb == 0), stop=(kb == 7)),
                            inc="st" if kb == 7 else None)
                    p.wait("vector", mm)
                    ev = _elu1_evac(p, c, ps[:], dst[:, mf * 2048 + t0: mf * 2048 + t0 + 512],
                                    t1, t2)
                    prev_e = ev
        for wname, dst, do_elu in (("L2_attn_Wk", ktok, True), ("L2_attn_Wv", vt, False)):
            lw = load_w(wname)
            p.wait("tensor", lw)
            prev_e = None
            for tt in range(NTT):
                r0 = tt * 128
                p.fence_all("gpsimd")
                xl = p.dma("gpsimd", xnt[:].rearrange("p (k t) -> p k t", k=8),
                           c.dram["xnT"][b, :, r0:r0 + 128].rearrange("(k p) t -> p k t", p=128))
                p.wait("tensor", xl)
                for half in range(2):
                    p.wait("tensor", prev_e)
                    mm = None
                    for kb in range(8):
                        mm = p.do("tensor", lambda t, kb=kb, half=half: t.matmul(
                            ps[:], xnt[:, kb * 128:(kb + 1) * 128],
                            wsb[:, kb * 1024 + half * 512: kb * 1024 + (half + 1) * 512],
                            start=(kb == 0), stop=(kb == 7)),
                            inc="st" if kb == 7 else None)
                    p.wait("vector", mm)
                    if do_elu:
                        ev = _elu1_evac(p, c, ps[:],
                                        dst[:, tt * 1024 + half * 512: tt * 1024 + (half + 1) * 512],
                                        t1, t2)
                    else:
                        ev = p.do("vector", lambda v, tt=tt, half=half, dst=dst: v.tensor_copy(
                            out=dst[:, tt * 1024 + half * 512: tt * 1024 + (half + 1) * 512],
                            in_=ps[:]), inc="sv")
                    prev_e = ev


def _l2_linattn(c, ident, b):
    nc = c.nc
    NH, DH = 16, 64
    with Ph(c, name=_phname()) as p:
        p._ident = ident
        qt = c._l2_qt; kt = c._l2_kt; ktok = c._l2_ktok; vt = c._l2_vt
        trimask = p.sb("a2_tri", [128, 128], BF16)
        ones_c = p.sb("a2_ones", [128, 1], BF16)
        am = p.sb("a2_am", [128, 128], BF16)
        amT = p.sb("a2_amT", [128, 128], BF16)
        ssb = p.sb("a2_s", [128, 64], BF16)
        zsb = p.sb("a2_z", [128, 1], BF16)
        den = p.sb("a2_den", [128, 1], F32)
        ao = p.sb("a2_ao", [128, 64], BF16)
        aopad = p.sb("a2_aopad", [128, 128], BF16)
        ps_a = p.ps("a2_ps_a", [128, 128], F32)
        ps_t = p.ps("a2_ps_t", [128, 128], BF16)
        ps_o = p.ps("a2_ps_o", [128, 64], F32)
        ps_d = p.ps("a2_ps_d", [128, 1], F32)
        ps_S = p.ps("a2_ps_S", [128, 64], F32)
        ps_z = p.ps("a2_ps_z", [128, 1], F32)
        p.do("gpsimd", lambda g: g.memset(trimask[:], 1.0))
        gm = p.do("gpsimd", lambda g: g.affine_select(
            out=trimask[:], in_=trimask[:], compare_op=ALU.is_ge, fill=0.0,
            base=0, pattern=[[-1, 128]], channel_multiplier=1), inc="sg")
        p.do("vector", lambda v: v.memset(ones_c[:], 1.0))
        p.do("vector", lambda v: v.memset(aopad[:], 0.0))
        mzv = p.do("vector", lambda v: v.memset(den[:], 0.0), inc="sv")
        p.wait("vector", gm)
        p.wait("tensor", mzv)
        for h in range(NH):
            pb = (h % 2) * 64
            jb = h // 2
            prev_upd = None
            prev_cv2 = None
            for ch in range(NTT):
                t0 = ch * 128
                if ch > 0:
                    p.wait("vector", prev_upd)
                    sc1 = p.do("vector", lambda v, pb=pb: v.tensor_copy(
                        out=ssb[pb:pb + 64, :], in_=ps_S[pb:pb + 64, :]), inc="sv")
                    sc2 = p.do("vector", lambda v, pb=pb: v.tensor_copy(
                        out=zsb[pb:pb + 64, :], in_=ps_z[pb:pb + 64, :]), inc="sv")
                    p.wait("tensor", sc2)
                mm = p.do("tensor", lambda t, pb=pb, jb=jb, t0=t0: t.matmul(
                    ps_a[:], qt[pb:pb + 64, jb * 2048 + t0: jb * 2048 + t0 + 128],
                    kt[pb:pb + 64, jb * 2048 + t0: jb * 2048 + t0 + 128],
                    start=True, stop=True), inc="st")
                p.wait("vector", mm)
                msk = p.do("vector", lambda v: v.tensor_tensor(
                    out=am[:], in0=ps_a[:], in1=trimask[:], op=ALU.mult), inc="sv")
                p.wait("tensor", msk)
                tr = p.do("tensor", lambda t: t.transpose(ps_t[:], am[:], p._ident[:]),
                          inc="st")
                p.wait("vector", tr)
                cv = p.do("vector", lambda v: v.tensor_copy(out=amT[:], in_=ps_t[:]), inc="sv")
                p.wait("tensor", cv)
                mo = p.do("tensor", lambda t, h=h, ch=ch, t0=t0: t.matmul(
                    ps_o[:], amT[:], vt[:, ch * 1024 + h * 64: ch * 1024 + h * 64 + 64],
                    start=True, stop=(ch == 0)), inc="st")
                md = p.do("tensor", lambda t, ch=ch: t.matmul(
                    ps_d[:], amT[:], ones_c[:], start=True, stop=(ch == 0)), inc="st")
                if ch > 0:
                    mo = p.do("tensor", lambda t, pb=pb, jb=jb, t0=t0: t.matmul(
                        ps_o[:], qt[pb:pb + 64, jb * 2048 + t0: jb * 2048 + t0 + 128],
                        ssb[pb:pb + 64, :], start=False, stop=True), inc="st")
                    md = p.do("tensor", lambda t, pb=pb, jb=jb, t0=t0: t.matmul(
                        ps_d[:], qt[pb:pb + 64, jb * 2048 + t0: jb * 2048 + t0 + 128],
                        zsb[pb:pb + 64, :], start=False, stop=True), inc="st")
                tp = None if pb == 0 else (0, pb)
                upd1 = p.do("tensor", lambda t, h=h, ch=ch, pb=pb, tp=tp: t.matmul(
                    ps_S[pb:pb + 64, :], ktok[:, ch * 1024 + h * 64: ch * 1024 + h * 64 + 64],
                    vt[:, ch * 1024 + h * 64: ch * 1024 + h * 64 + 64],
                    start=(ch == 0), stop=(ch == NTT - 1), tile_position=tp), inc="st")
                upd2 = p.do("tensor", lambda t, h=h, ch=ch, pb=pb, tp=tp: t.matmul(
                    ps_z[pb:pb + 64, :], ktok[:, ch * 1024 + h * 64: ch * 1024 + h * 64 + 64],
                    ones_c[:], start=(ch == 0), stop=(ch == NTT - 1), tile_position=tp),
                    inc="st")
                prev_upd = upd2
                p.wait("vector", upd2)
                d1 = p.do("vector", lambda v: v.tensor_scalar(
                    out=den[:], in0=ps_d[:], scalar1=1e-6, scalar2=None, op0=ALU.add),
                    inc="sv")
                p.wait("vector", d1)
                d2 = p.do("vector", lambda v: v.reciprocal(den[:], den[:]), inc="sv")
                p.wait("vector", d2)
                ev = p.do("vector", lambda v: v.tensor_tensor(
                    out=ao[:], in0=ps_o[:], in1=den[:, 0:1].to_broadcast([128, 64]),
                    op=ALU.mult), inc="sv")
                p.wait("vector", ev)
                cp2 = p.do("vector", lambda v, pb=pb: v.tensor_copy(
                    out=aopad[:, pb:pb + 64], in_=ao[:]), inc="sv")
                zp = p.do("vector", lambda v, pb=pb: v.memset(
                    aopad[:, (64 if pb == 0 else 0):(128 if pb == 0 else 64)], 0.0), inc="sv")
                p.wait("tensor", cp2)
                p.wait("tensor", zp)
                tr2 = p.do("tensor", lambda t: t.transpose(ps_t[:], aopad[:], p._ident[:]),
                           inc="st")
                p.wait("vector", tr2)
                cv2 = p.do("vector", lambda v, pb=pb: v.tensor_copy(
                    out=am[pb:pb + 64, :], in_=ps_t[pb:pb + 64, :]), inc="sv")
                p.wait("sync", cv2)
                sd = p.dma("sync", c.dram["l2_aoT"][jb * 128 + pb:jb * 128 + pb + 64,
                                                    t0:t0 + 128], am[pb:pb + 64, :])
                p.wait("vector", sd)
                p.wait("tensor", sd)
                prev_cv2 = cv2


def _l2_out(c, ident, b):
    with Ph(c, name=_phname()) as p:
        p._ident = ident
        wo = p.sb("w2_wo", [128, 8 * 512], BF16)
        wld = p.sb("w2_wld", [128, 512], F32)
        aot = p.sb("w2_aot", [128, 8 * 128], BF16)
        sev = p.sb("w2_sev", [128, 512], F32)
        ps = p.ps("w2_ps", [128, 512], F32)
        for half in range(2):
            p.fence_all("sync")
            for kb in range(8):
                ld = p.dma("sync", wld[:], c.dram["L2_attn_Wo"][kb * 128:(kb + 1) * 128,
                                                                half * 512:(half + 1) * 512])
                p.wait("vector", ld)
                cv = p.do("vector", lambda v, kb=kb: v.tensor_copy(
                    out=wo[:, kb * 512:(kb + 1) * 512], in_=wld[:]), inc="sv")
                p.wait("sync", cv)
            p.wait("tensor", cv)
            for tt in range(NTT):
                r0 = tt * 128
                p.fence_all("gpsimd")
                al = None
                for kb in range(8):
                    al = p.dma("gpsimd", aot[:, kb * 128:(kb + 1) * 128],
                               c.dram["l2_aoT"][kb * 128:(kb + 1) * 128, r0:r0 + 128])
                p.wait("tensor", al)
                mm = None
                for kb in range(8):
                    mm = p.do("tensor", lambda t, kb=kb: t.matmul(
                        ps[:], aot[:, kb * 128:(kb + 1) * 128], wo[:, kb * 512:(kb + 1) * 512],
                        start=(kb == 0), stop=(kb == 7)), inc="st" if kb == 7 else None)
                p.wait("vector", mm)
                cv = p.do("vector", lambda v: v.tensor_copy(out=sev[:], in_=ps[:]), inc="sv")
                p.wait("sync", cv)
                sd = p.dma("sync", c.dram["s_buf"][b, r0:r0 + 128,
                                                   half * 512:(half + 1) * 512], sev[:])
                p.wait("vector", sd)
                p.wait("tensor", cv)

def ph_l2_ffn(c, ident):
    _l2_dequant(c, ident)
    c.barrier()
    for b in range(B_PER_CORE):
        _l2_x2(c, ident, b)
        c.barrier()
        _l2_fc1(c, ident, b)
        c.barrier()
        _l2_fc2(c, ident, b)
        c.barrier()


def _l2_dequant(c, ident):
    nc = c.nc
    with Ph(c, name=_phname()) as p:
        p._ident = ident
        qi = p.sb("d2_qi", [128, 4096], I32)
        qf = p.sb("d2_qf", [128, 4096], F32)
        scb = p.sb("d2_scb", [128, 4096], F32)
        srow = p.sb("d2_srow", [1, 4096], F32)
        ones_r = p.sb("d2_ones", [1, 128], F32)
        ob = p.sb("d2_ob", [128, 4096], BF16)
        ps = p.ps("d2_ps", [128, 512], F32)
        p.do("vector", lambda v: v.memset(ones_r[:], 1.0))

        def dq(qname, sname, dname, nrows, ncols):
            for kb in range(nrows // 128):
                p.fence_all("sync")
                l1 = p.dma("sync", qi[:, 0:ncols], c.dram[qname][kb * 128:(kb + 1) * 128, :])
                l2 = p.dma("sync", srow[:, 0:ncols], c.dram[sname][kb:kb + 1, :])
                p.wait("tensor", l2)
                bcl = None
                for sl in range(ncols // 512):
                    bm = p.do("tensor", lambda t, sl=sl: t.matmul(
                        ps[:], ones_r[0:1, :], srow[0:1, sl * 512:(sl + 1) * 512],
                        start=True, stop=True), inc="st")
                    p.wait("vector", bm)
                    bcl = p.do("vector", lambda v, sl=sl: v.tensor_copy(
                        out=scb[:, sl * 512:(sl + 1) * 512], in_=ps[:]), inc="sv")
                    p.wait("tensor", bcl)
                p.wait("vector", l1)
                c1 = p.do("vector", lambda v: v.tensor_copy(out=qf[:, 0:ncols],
                                                            in_=qi[:, 0:ncols]), inc="sv")
                p.wait("vector", c1)
                c2 = p.do("vector", lambda v: v.tensor_scalar(
                    out=qf[:, 0:ncols], in0=qf[:, 0:ncols], scalar1=-128.0, scalar2=None,
                    op0=ALU.add), inc="sv")
                p.wait("vector", c2)
                p.wait("vector", bcl)
                c3 = p.do("vector", lambda v: v.tensor_tensor(
                    out=ob[:, 0:ncols], in0=qf[:, 0:ncols], in1=scb[:, 0:ncols],
                    op=ALU.mult), inc="sv")
                p.wait("sync", c3)
                sd = p.dma("sync", c.dram[dname][kb * 128:(kb + 1) * 128, :], ob[:, 0:ncols])
                p.wait("vector", sd)

        dq("L2_ffn_fc1_q", "L2_ffn_fc1_scales", "l2_w1", 1024, 4096)
        dq("L2_ffn_fc2_q", "L2_ffn_fc2_scales", "l2_w2", 4096, 1024)


def _l2_x2(c, ident, b):
    nc = c.nc
    with Ph(c, name=_phname()) as p:
        p._ident = ident
        xt = p.sb("x2_x", [128, D], BF16)
        xm = p.sb("x2_xm", [128, D], F32)
        sq = p.sb("x2_sq", [128, D], F32)
        x2 = p.sb("x2_x2", [128, D], BF16)
        gamb = p.sb("x2_gamb", [128, D], F32)
        betb = p.sb("x2_betb", [128, D], F32)
        grow = p.sb("x2_grow", [1, D], F32)
        st = p.sb("x2_st", [128, 8], F32)
        tb = p.sb("x2_tb", [128, 8 * 128], BF16)
        ones_r = p.sb("x2_ones", [1, 128], F32)
        ps_b = p.ps("x2_ps_b", [128, 512], F32)
        ps_t = p.ps("x2_ps_t", [128, 128], BF16)
        p.do("vector", lambda v: v.memset(ones_r[:], 1.0))
        last_mm = None
        for nm, dst in (("L2_ffn_ln_gamma", gamb), ("L2_ffn_ln_beta", betb)):
            p.wait("sync", last_mm)
            ld = p.dma("sync", grow[:], c.dram[nm][:, :])
            p.wait("tensor", ld)
            for sl in range(2):
                mm = p.do("tensor", lambda t, sl=sl: t.matmul(
                    ps_b[:], ones_r[0:1, :], grow[0:1, sl * 512:(sl + 1) * 512],
                    start=True, stop=True), inc="st")
                last_mm = mm
                p.wait("vector", mm)
                cv = p.do("vector", lambda v, sl=sl, dst=dst: v.tensor_copy(
                    out=dst[:, sl * 512:(sl + 1) * 512], in_=ps_b[:]), inc="sv")
                p.wait("tensor", cv)
        for tt in range(NTT):
            r0 = tt * 128
            p.fence_all("sync")
            ld = p.dma("sync", xt[:], c.dram["xn_tok"][b, r0:r0 + 128, :])
            p.wait("vector", ld)
            rs = p.do("vector", lambda v: v.reduce_sum(st[:, 0:1], xt[:], axis=AX.X), inc="sv")
            p.wait("vector", rs)
            nm_ = p.do("vector", lambda v: v.tensor_scalar(
                out=st[:, 1:2], in0=st[:, 0:1], scalar1=1.0 / D, scalar2=None,
                op0=ALU.mult), inc="sv")
            p.wait("vector", nm_)
            x1_ = p.do("vector", lambda v: v.tensor_tensor(
                out=xm[:], in0=xt[:], in1=st[:, 1:2].to_broadcast([128, D]),
                op=ALU.subtract), inc="sv")
            p.wait("scalar", x1_)
            a2 = p.do("scalar", lambda s: s.activation(out=sq[:], in_=xm[:], func=AF.Square,
                                                       accum_out=st[:, 2:3]), inc="ss")
            p.wait("vector", a2)
            v1 = p.do("vector", lambda v: v.tensor_scalar(
                out=st[:, 3:4], in0=st[:, 2:3], scalar1=1.0 / D, scalar2=1e-5,
                op0=ALU.mult, op1=ALU.add), inc="sv")
            p.wait("scalar", v1)
            a3 = p.do("scalar", lambda s: s.activation(out=st[:, 4:5], in_=st[:, 3:4],
                                                       func=AF.Sqrt), inc="ss")
            p.wait("vector", a3)
            rc = p.do("vector", lambda v: v.reciprocal(st[:, 5:6], st[:, 4:5]), inc="sv")
            p.wait("vector", rc)
            n1 = p.do("vector", lambda v: v.tensor_tensor(
                out=xm[:], in0=xm[:], in1=st[:, 5:6].to_broadcast([128, D]), op=ALU.mult),
                inc="sv")
            p.wait("vector", n1)
            n2 = p.do("vector", lambda v: v.tensor_tensor(
                out=xm[:], in0=xm[:], in1=gamb[:], op=ALU.mult))
            n3 = p.do("vector", lambda v: v.tensor_tensor(
                out=x2[:], in0=xm[:], in1=betb[:], op=ALU.add), inc="sv")
            p.wait("tensor", n3)
            cv = None
            for k in range(8):
                tr = p.do("tensor", lambda t, k=k: t.transpose(
                    ps_t[:], x2[:, k * 128:(k + 1) * 128], p._ident[:]), inc="st")
                p.wait("vector", tr)
                cv = p.do("vector", lambda v, k=k: v.tensor_copy(
                    out=tb[:, k * 128:(k + 1) * 128], in_=ps_t[:]), inc="sv")
                p.wait("tensor", cv)
            p.wait("sync", cv)
            sd = p.dma("sync", c.dram["l2_x2T"][:, r0:r0 + 128].rearrange("(k p) t -> p k t", p=128),
                       tb[:].rearrange("p (k t) -> p k t", k=8))
            p.wait("vector", sd)


def _l2_fc1(c, ident, b):
    nc = c.nc
    with Ph(c, name=_phname()) as p:
        p._ident = ident
        w1c = p.sb("f2_w1c", [128, 8 * 128], BF16)
        x2c = p.sb("f2_x2c", [128, 8 * 512], BF16)
        b1c = p.sb("f2_b1c", [128, 1], F32)
        hev = p.sb("f2_hev", [128, 512], BF16)
        ps = p.ps("f2_ps", [128, 512], F32)
        for nb in range(32):
            p.fence_all("sync")
            wl = p.dma("sync", w1c[:].rearrange("p (k t) -> p k t", k=8),
                       c.dram["l2_w1"][:, nb * 128:(nb + 1) * 128]
                       .rearrange("(k p) t -> p k t", p=128))
            bl = p.dma("sync", b1c[:],
                       c.dram["L2_ffn_fc1_bias"][0:1, nb * 128:(nb + 1) * 128]
                       .rearrange("a b -> b a"), allow_slow_non_contiguous=True)
            p.wait("tensor", wl)
            p.wait("scalar", bl)
            prev_ev = None
            for ts4 in range(4):
                t0 = ts4 * 512
                p.fence_all("gpsimd")
                xl = p.dma("gpsimd", x2c[:].rearrange("p (k t) -> p k t", k=8),
                           c.dram["l2_x2T"][:, t0:t0 + 512].rearrange("(k p) t -> p k t", p=128))
                p.wait("tensor", xl)
                p.wait("tensor", prev_ev)
                mm = None
                for kb in range(8):
                    mm = p.do("tensor", lambda t, kb=kb: t.matmul(
                        ps[:], w1c[:, kb * 128:(kb + 1) * 128], x2c[:, kb * 512:(kb + 1) * 512],
                        start=(kb == 0), stop=(kb == 7)), inc="st" if kb == 7 else None)
                p.wait("scalar", mm)
                ev = p.do("scalar", lambda s: s.activation(
                    out=hev[:], in_=ps[:], func=AF.Silu, bias=b1c[:, 0:1], scale=1.0),
                    inc="ss")
                prev_ev = ev
                p.wait("sync", ev)
                sd = p.dma("sync", c.dram["l2_hT"][nb * 128:(nb + 1) * 128, t0:t0 + 512],
                           hev[:])
                p.wait("scalar", sd)


def _l2_fc2(c, ident, b):
    nc = c.nc
    with Ph(c, name=_phname()) as p:
        p._ident = ident
        w2c = p.sb("g2_w2c", [128, 32 * 512], BF16)
        wld = p.sb("g2_wld", [128, 512], BF16)
        htc = p.sb("g2_htc", [128, 32 * 128], BF16)
        b2r = p.sb("g2_b2r", [1, D], F32)
        xnt_ = p.sb("g2_xn", [128, 512], BF16)
        gev = p.sb("g2_gev", [128, 512], F32)
        ones_r = p.sb("g2_ones", [1, 128], F32)
        ps = p.ps("g2_ps", [128, 512], F32)
        p.do("vector", lambda v: v.memset(ones_r[:], 1.0))
        lb = p.dma("sync", b2r[:], c.dram["L2_ffn_fc2_bias"][:, :])
        p.wait("tensor", lb)
        for half in range(2):
            p.fence_all("sync")
            wl = p.dma("sync", w2c[:].rearrange("p (k t) -> p k t", k=32),
                       c.dram["l2_w2"][:, half * 512:(half + 1) * 512]
                       .rearrange("(k p) t -> p k t", p=128))
            p.wait("tensor", wl)
            for tt in range(NTT):
                r0 = tt * 128
                p.fence_all("gpsimd")
                hl = p.dma("gpsimd", htc[:].rearrange("p (k t) -> p k t", k=32),
                           c.dram["l2_hT"][:, r0:r0 + 128].rearrange("(k p) t -> p k t", p=128))
                p.wait("tensor", hl)
                mm = None
                for kb in range(32):
                    mm = p.do("tensor", lambda t, kb=kb, half=half: t.matmul(
                        ps[:], htc[:, kb * 128:(kb + 1) * 128], w2c[:, kb * 512:(kb + 1) * 512],
                        start=(kb == 0), stop=False), inc=None)
                mm = p.do("tensor", lambda t, half=half: t.matmul(
                    ps[:], ones_r[0:1, :], b2r[0:1, half * 512:(half + 1) * 512],
                    start=False, stop=True), inc="st")
                xl = p.dma("sync", xnt_[:], c.dram["xn_tok"][b, r0:r0 + 128,
                                                             half * 512:(half + 1) * 512])
                p.wait("vector", mm)
                p.wait("vector", xl)
                ev = p.do("vector", lambda v: v.scalar_tensor_tensor(
                    out=gev[:], in0=ps[:], scalar=0.5, in1=xnt_[:],
                    op0=ALU.mult, op1=ALU.add), inc="sv")
                p.wait("sync", ev)
                sd = p.dma("sync", c.dram["g_buf"][b, r0:r0 + 128,
                                                   half * 512:(half + 1) * 512], gev[:])
                p.wait("vector", sd)
                p.wait("tensor", ev)

def ph_l3_attn(c, ident):
    nc = c.nc
    NH, DH = 8, 128
    with ExitStack() as outer:
        oe = outer.enter_context
        qt_sb = oe(nc.sbuf_tensor("a3_qt", [128, 8 * 2048], BF16))
        kt_sb = oe(nc.sbuf_tensor("a3_kt", [128, 8 * 2048], BF16))
        v_sb = oe(nc.sbuf_tensor("a3_v", [128, 16 * 1024], BF16))
        for b in range(B_PER_CORE):
            p = Ph(c, name=_phname())
            p._ident = ident
            wq_sb = p.sb("a3_wqkv", [128, 8 * 3072], BF16)
            wld = p.sb("a3_wld", [128, 1536], F32)
            bqr = p.sb("a3_bq", [1, 3072], F32)
            ones_r = p.sb("a3_ones", [1, 128], F32)
            xnt = p.sb("a3_xnt", [128, 8 * 128], BF16)
            evb = p.sb("a3_evb", [128, 512], BF16)
            ps_qkv = [p.ps(f"a3_ps{i}", [128, 512], F32) for i in range(6)]
            ps_tp = p.ps("a3_ps_tp", [128, 128], BF16)
            p.do("vector", lambda v: v.memset(ones_r[:], 1.0))
            for kb in range(8):
                for ch in range(2):
                    ld = p.dma("sync", wld[:], c.dram["L3_attn_w_qkvT"][kb * 128:(kb + 1) * 128,
                                                                        ch * 1536:(ch + 1) * 1536])
                    p.wait("vector", ld)
                    cv = p.do("vector", lambda v, kb=kb, ch=ch: v.tensor_copy(
                        out=wq_sb[:, kb * 3072 + ch * 1536: kb * 3072 + (ch + 1) * 1536],
                        in_=wld[:]), inc="sv")
                    p.wait("sync", cv)
            lb = p.dma("sync", bqr[:], c.dram["L3_attn_b_qkv"][:, :])
            p.wait("tensor", lb)
            p.wait("tensor", cv)
            prev_ev = None
            for tt in range(NTT):
                r0 = tt * 128
                p.fence_all("gpsimd")
                xl = p.dma("gpsimd", xnt[:].rearrange("p (k t) -> p k t", k=8),
                           c.dram["xnT"][b, :, r0:r0 + 128].rearrange("(k p) t -> p k t", p=128))
                p.wait("tensor", xl)
                p.wait("tensor", prev_ev)
                mm = None
                for sl in range(6):
                    for kb in range(8):
                        mm = p.do("tensor", lambda t, kb=kb, sl=sl: t.matmul(
                            ps_qkv[sl][:], xnt[:, kb * 128:(kb + 1) * 128],
                            wq_sb[:, kb * 3072 + sl * 512: kb * 3072 + (sl + 1) * 512],
                            start=(kb == 0), stop=False), inc=None)
                    mm = p.do("tensor", lambda t, sl=sl: t.matmul(
                        ps_qkv[sl][:], ones_r[0:1, :], bqr[0:1, sl * 512:(sl + 1) * 512],
                        start=False, stop=True), inc="st")
                p.wait("vector", mm)
                lastv = None
                for sl in range(4):
                    dst = qt_sb if sl < 2 else kt_sb
                    cvb = p.do("vector", lambda v, sl=sl: v.tensor_copy(
                        out=evb[:], in_=ps_qkv[sl][:]), inc="sv")
                    p.wait("tensor", cvb)
                    for j in range(4):
                        h = (sl % 2) * 4 + j
                        tr = p.do("tensor", lambda t, j=j: t.transpose(
                            ps_tp[:], evb[:, j * 128:(j + 1) * 128], p._ident[:]), inc="st")
                        p.wait("vector", tr)
                        cv2 = p.do("vector", lambda v, h=h, r0=r0, dst=dst: v.tensor_copy(
                            out=dst[:, h * 2048 + r0: h * 2048 + r0 + 128], in_=ps_tp[:]),
                            inc="sv")
                        p.wait("tensor", cv2)
                    lastv = cv2
                for sl in range(2):
                    lastv = p.do("vector", lambda v, sl=sl, tt=tt: v.tensor_copy(
                        out=v_sb[:, tt * 1024 + sl * 512: tt * 1024 + (sl + 1) * 512],
                        in_=ps_qkv[4 + sl][:]), inc="sv")
                prev_ev = lastv
            p.close()
            c.barrier()
            p = Ph(c, name=_phname())
            p._ident = ident
            mskL = p.sb("a3_mskL", [128, 128], F32)
            mskR = p.sb("a3_mskR", [128, 128], F32)
            probs = p.sb("a3_probs", [128, 384], BF16)
            probsT = p.sb("a3_probsT", [128, 128], BF16)
            ao = p.sb("a3_ao", [128, 128], BF16)
            stat = p.sb("a3_stat", [128, 4], F32)
            ps_sc = p.ps("a3_ps_sc", [128, 384], F32)
            ps_tp = p.ps("a3_ps_tp2", [128, 128], BF16)
            ps_ao = p.ps("a3_ps_ao", [128, 128], F32)
            p.do("gpsimd", lambda g: g.memset(mskL[:], 0.0))
            g1 = p.do("gpsimd", lambda g: g.affine_select(
                out=mskL[:], in_=mskL[:], compare_op=ALU.is_ge, fill=-1e30,
                base=-1, pattern=[[1, 128]], channel_multiplier=-1), inc="sg")
            p.do("gpsimd", lambda g: g.memset(mskR[:], 0.0))
            g2 = p.do("gpsimd", lambda g: g.affine_select(
                out=mskR[:], in_=mskR[:], compare_op=ALU.is_ge, fill=-1e30,
                base=-1, pattern=[[-1, 128]], channel_multiplier=1), inc="sg")
            p.wait("vector", g2)
            prev_ex = None
            prev_tr = None
            prev_cv2 = None
            for h in range(NH):
                for qi in range(NTT):
                    q0 = qi * 128
                    k_lo = max(0, qi - 1)
                    k_hi = min(NTT - 1, qi + 1)
                    nk = (k_hi - k_lo + 1) * 128
                    p.wait("tensor", prev_ex)
                    mm = p.do("tensor", lambda t, h=h, q0=q0, k_lo=k_lo, nk=nk: t.matmul(
                        ps_sc[:, 0:nk],
                        qt_sb[:, h * 2048 + q0: h * 2048 + q0 + 128],
                        kt_sb[:, h * 2048 + k_lo * 128: h * 2048 + k_lo * 128 + nk],
                        start=True, stop=True), inc="st")
                    p.wait("vector", mm)
                    mtok = mm
                    if qi - 1 >= 0:
                        mtok = p.do("vector", lambda v: v.tensor_tensor(
                            out=ps_sc[:, 0:128], in0=ps_sc[:, 0:128], in1=mskL[:],
                            op=ALU.add), inc="sv")
                    if qi + 1 <= NTT - 1:
                        off = (k_hi - k_lo) * 128
                        mtok = p.do("vector", lambda v, off=off: v.tensor_tensor(
                            out=ps_sc[:, off:off + 128], in0=ps_sc[:, off:off + 128],
                            in1=mskR[:], op=ALU.add), inc="sv")
                    p.wait("vector", mtok)
                    r1 = p.do("vector", lambda v, nk=nk: v.reduce_max(
                        stat[:, 0:1], ps_sc[:, 0:nk], axis=AX.X, negate=True), inc="sv")
                    p.wait("vector", r1)
                    r2 = p.do("vector", lambda v: v.tensor_scalar(
                        out=stat[:, 1:2], in0=stat[:, 0:1], scalar1=1.0 / 32.0, scalar2=None,
                        op0=ALU.mult), inc="sv")
                    p.wait("scalar", r2)
                    p.wait("scalar", prev_tr)
                    ex = p.do("scalar", lambda s, nk=nk: s.activation(
                        out=probs[:, 0:nk], in_=ps_sc[:, 0:nk], func=AF.Exp,
                        bias=stat[:, 1:2], scale=1.0 / 32.0, accum_out=stat[:, 2:3]),
                        inc="ss")
                    prev_ex = ex
                    p.wait("vector", ex)
                    r3 = p.do("vector", lambda v: v.reciprocal(stat[:, 3:4], stat[:, 2:3]),
                              inc="sv")
                    p.wait("tensor", ex)
                    amm = None
                    prev_amm = None
                    for j in range(k_lo, k_hi + 1):
                        jj = j - k_lo
                        p.wait("tensor", prev_cv2)
                        tr = p.do("tensor", lambda t, jj=jj: t.transpose(
                            ps_tp[:], probs[:, jj * 128:(jj + 1) * 128], p._ident[:]),
                            inc="st")
                        prev_tr = tr
                        prev_cv2 = None
                        p.wait("vector", tr)
                        p.wait("vector", prev_amm)
                        cv = p.do("vector", lambda v: v.tensor_copy(out=probsT[:], in_=ps_tp[:]),
                                  inc="sv")
                        p.wait("tensor", cv)
                        amm = p.do("tensor", lambda t, j=j, jj=jj, h=h, k_lo=k_lo, k_hi=k_hi:
                                   t.matmul(
                            ps_ao[:], probsT[:],
                            v_sb[:, j * 1024 + (h % 8) * 128: j * 1024 + (h % 8) * 128 + 128],
                            start=(j == k_lo), stop=(j == k_hi)), inc="st")
                        prev_amm = amm
                    p.wait("vector", amm)
                    p.wait("vector", r3)
                    ev = p.do("vector", lambda v: v.tensor_tensor(
                        out=ao[:], in0=ps_ao[:], in1=stat[:, 3:4].to_broadcast([128, 128]),
                        op=ALU.mult), inc="sv")
                    p.wait("tensor", ev)
                    tr2 = p.do("tensor", lambda t: t.transpose(ps_tp[:], ao[:], p._ident[:]),
                               inc="st")
                    p.wait("vector", tr2)
                    cv2 = p.do("vector", lambda v: v.tensor_copy(out=probsT[:], in_=ps_tp[:]),
                               inc="sv")
                    prev_cv2 = cv2
                    p.wait("sync", cv2)
                    sd = p.dma("sync", c.dram["l3_aoT"][h * 128:(h + 1) * 128, q0:q0 + 128],
                               probsT[:])
                    p.wait("vector", sd)
                    p.wait("tensor", sd)
            p.close()
            c.barrier()
            p = Ph(c, name=_phname())
            p._ident = ident
            wo_sb = p.sb("a3_wo", [128, 8 * 512], BF16)
            wld = p.sb("a3_wld2", [128, 512], F32)
            aot = p.sb("a3_aot", [128, 8 * 128], BF16)
            bpr = p.sb("a3_bp", [1, 1024], F32)
            ones_r = p.sb("a3_ones2", [1, 128], F32)
            sev = p.sb("a3_sev", [128, 512], F32)
            ps = p.ps("a3_ps_o", [128, 512], F32)
            p.do("vector", lambda v: v.memset(ones_r[:], 1.0))
            lb = p.dma("sync", bpr[:], c.dram["L3_attn_b_proj"][:, :])
            p.wait("tensor", lb)
            for half in range(2):
                p.fence_all("sync")
                wld8 = wld
                ld = p.dma("sync", wld8[:, 0:512], c.dram["L3_attn_w_projT"][0:128,
                           half * 512:(half + 1) * 512])
                cv = None
                for kb in range(8):
                    ld = p.dma("sync", wld[:, 0:512],
                               c.dram["L3_attn_w_projT"][kb * 128:(kb + 1) * 128,
                                                         half * 512:(half + 1) * 512])
                    p.wait("vector", ld)
                    cv = p.do("vector", lambda v, kb=kb: v.tensor_copy(
                        out=wo_sb[:, kb * 512:(kb + 1) * 512], in_=wld[:, 0:512]), inc="sv")
                    p.wait("sync", cv)
                p.wait("tensor", cv)
                for tt in range(NTT):
                    r0 = tt * 128
                    p.fence_all("gpsimd")
                    al = p.dma("gpsimd", aot[:].rearrange("p (k t) -> p k t", k=8),
                               c.dram["l3_aoT"][:, r0:r0 + 128]
                               .rearrange("(k p) t -> p k t", p=128))
                    p.wait("tensor", al)
                    mm = None
                    for kb in range(8):
                        mm = p.do("tensor", lambda t, kb=kb: t.matmul(
                            ps[:], aot[:, kb * 128:(kb + 1) * 128],
                            wo_sb[:, kb * 512:(kb + 1) * 512],
                            start=(kb == 0), stop=False), inc=None)
                    mm = p.do("tensor", lambda t, half=half: t.matmul(
                        ps[:], ones_r[0:1, :], bpr[0:1, half * 512:(half + 1) * 512],
                        start=False, stop=True), inc="st")
                    p.wait("vector", mm)
                    cve = p.do("vector", lambda v: v.tensor_copy(out=sev[:], in_=ps[:]),
                               inc="sv")
                    p.wait("sync", cve)
                    sd = p.dma("sync", c.dram["s_buf"][b, r0:r0 + 128,
                                                       half * 512:(half + 1) * 512], sev[:])
                    p.wait("vector", sd)
                    p.wait("tensor", cve)
            p.close()
            c.barrier()

def ph_l3_ffn(c, ident):
    _l3_router(c, ident)
    c.barrier()
    _l3_threshold(c, ident)
    c.barrier()
    with Ph(c, name=_phname()) as p:
        zt = p.sb("z3_z", [128, D], F32)
        mz = p.do("vector", lambda v: v.memset(zt[:], 0.0), inc="sv")
        p.wait("sync", mz)
        for tt in range(NTT):
            p.dma("sync", c.dram["l3_zero"][tt * 128:(tt + 1) * 128, :], zt[:])
    c.barrier()
    bs = [_L3FFN_B] if _L3FFN_B is not None else list(range(B_PER_CORE))
    for b in bs:
        for e in range(8):
            _l3_gather(c, ident, b, e)
            c.barrier()
            _l3_expert(c, ident, b, e)
            c.barrier()
        _l3_accum(c, ident, b)
        c.barrier()


def _l3_router(c, ident):
    nc = c.nc
    with Ph(c, name=_phname()) as p:
        p._ident = ident
        gwt = p.sb("r3_gwt", [128, 8 * 8], BF16)
        gwl = p.sb("r3_gwl", [128, 8], F32)
        xnt = p.sb("r3_xnt", [128, 8 * 128], BF16)
        pr = p.sb("r3_pr", [128, 128], F32)
        idf = p.sb("r3_idf", [128, 128], F32)
        st = p.sb("r3_st", [128, 4], F32)
        prT = p.sb("r3_prT", [16, 128], F32)
        ps_l = p.ps("r3_ps_l", [128, 8], F32)
        ps_T = p.ps("r3_ps_T", [128, 128], F32)
        p.do("gpsimd", lambda g: g.memset(idf[:], 0.0))
        gi = p.do("gpsimd", lambda g: g.affine_select(
            out=idf[:], in_=idf[:], compare_op=ALU.not_equal, fill=1.0,
            base=0, pattern=[[-1, 128]], channel_multiplier=1), inc="sg")
        p.wait("tensor", gi)
        mzp = p.do("vector", lambda v: v.memset(pr[:], 0.0), inc="sv")
        gwl8 = p.sb("r3_gwl8", [128, 64], F32)
        ld = p.dma("sync", gwl8[:].rearrange("p (k e) -> p k e", k=8),
                   c.dram["L3_ffn_WgT"][:, :].rearrange("(k p) e -> p k e", p=128))
        p.wait("vector", ld)
        cv = p.do("vector", lambda v: v.tensor_copy(out=gwt[:], in_=gwl8[:]), inc="sv")
        p.wait("tensor", cv)
        for b in range(B_PER_CORE):
            for tt in range(NTT):
                r0 = tt * 128
                p.fence_all("gpsimd")
                xl = p.dma("gpsimd", xnt[:].rearrange("p (k t) -> p k t", k=8),
                           c.dram["xnT"][b, :, r0:r0 + 128].rearrange("(k p) t -> p k t", p=128))
                p.wait("tensor", xl)
                mm = None
                for kb in range(8):
                    mm = p.do("tensor", lambda t, kb=kb: t.matmul(
                        ps_l[:], xnt[:, kb * 128:(kb + 1) * 128], gwt[:, kb * 8:(kb + 1) * 8],
                        start=(kb == 0), stop=(kb == 7)), inc="st" if kb == 7 else None)
                p.wait("vector", mm)
                r1 = p.do("vector", lambda v: v.reduce_max(st[:, 0:1], ps_l[:], axis=AX.X,
                                                           negate=True), inc="sv")
                p.wait("vector", r1)
                sb_ = p.do("vector", lambda v: v.tensor_tensor(
                    out=pr[:, 0:8], in0=ps_l[:], in1=st[:, 0:1].to_broadcast([128, 8]),
                    op=ALU.add), inc="sv")
                p.wait("scalar", sb_)
                ex = p.do("scalar", lambda s: s.activation(out=pr[:, 0:8], in_=pr[:, 0:8],
                                                           func=AF.Exp,
                                                           accum_out=st[:, 1:2]), inc="ss")
                p.wait("vector", ex)
                rc = p.do("vector", lambda v: v.reciprocal(st[:, 2:3], st[:, 1:2]), inc="sv")
                p.wait("vector", rc)
                pn = p.do("vector", lambda v: v.tensor_tensor(
                    out=pr[:, 0:8], in0=pr[:, 0:8], in1=st[:, 2:3].to_broadcast([128, 8]),
                    op=ALU.mult), inc="sv")
                p.wait("sync", pn)
                sd1 = p.dma("sync", c.dram["l3_p"][b, r0:r0 + 128, :], pr[:, 0:8])
                p.wait("tensor", pn)
                tmm = p.do("tensor", lambda t: t.matmul(
                    ps_T[:], pr[:], idf[:], start=True, stop=True), inc="st")
                p.wait("vector", tmm)
                cvt = p.do("vector", lambda v: v.tensor_copy(out=prT[0:16, :],
                                                             in_=ps_T[0:16, :]), inc="sv")
                p.wait("sync", cvt)
                sd2 = p.dma("sync", c.dram["l3_pT"][8 * b:8 * b + 8, r0:r0 + 128],
                            prT[0:8, :])
                p.wait("vector", sd2)
                p.wait("vector", sd1)
                p.wait("tensor", sd2)


def _l3_threshold(c, ident):
    nc = c.nc
    NIT = 45
    with Ph(c, name=_phname()) as p:
        p._ident = ident
        pT = p.sb("t3_pT", [16, T], F32)
        lo = p.sb("t3_lo", [16, 1], F32)
        hi = p.sb("t3_hi", [16, 1], F32)
        mid = p.sb("t3_mid", [16, 1], F32)
        cnt = p.sb("t3_cnt", [16, 1], F32)
        prd = p.sb("t3_prd", [16, 1], U8)
        gt = p.sb("t3_gt", [16, T], F32)
        rnk = p.sb("t3_rnk", [16, T], F32)
        slot = p.sb("t3_slot", [16, T], F32)
        big = p.sb("t3_big", [16, 1], F32)
        msku = p.sb("t3_msku", [16, T], U8)
        on1 = p.sb("t3_on1", [16, T], F32)
        idf16 = p.sb("t3_id16", [16, 16], F32)
        ps_T = p.ps("t3_ps", [128, 16], F32)
        ld = p.dma("sync", pT[:], c.dram["l3_pT"][:, :])
        p.do("gpsimd", lambda g: g.memset(idf16[:], 0.0))
        gi = p.do("gpsimd", lambda g: g.affine_select(
            out=idf16[:], in_=idf16[:], compare_op=ALU.not_equal, fill=1.0,
            base=0, pattern=[[-1, 16]], channel_multiplier=1), inc="sg")
        mz1 = p.do("vector", lambda v: v.memset(lo[:], 0.0), inc="sv")
        mz2 = p.do("vector", lambda v: v.memset(hi[:], 1.0), inc="sv")
        mz3 = p.do("vector", lambda v: v.memset(big[:], 512.0), inc="sv")
        p.do("vector", lambda v: v.memset(on1[:], 1.0))
        p.wait("vector", ld)
        p.wait("vector", gi)
        for it in range(NIT):
            m1 = p.do("vector", lambda v: v.tensor_tensor(out=mid[:], in0=lo[:], in1=hi[:],
                                                          op=ALU.add), inc="sv")
            p.wait("vector", m1)
            m2 = p.do("vector", lambda v: v.tensor_scalar(out=mid[:], in0=mid[:],
                                                          scalar1=0.5, scalar2=None,
                                                          op0=ALU.mult), inc="sv")
            p.wait("vector", m2)
            m3 = p.do("vector", lambda v: v.tensor_tensor(
                out=gt[:], in0=pT[:], in1=mid[:, 0:1].to_broadcast([16, T]),
                op=ALU.is_gt), inc="sv")
            p.wait("vector", m3)
            m4 = p.do("vector", lambda v: v.reduce_sum(cnt[:], gt[:], axis=AX.X), inc="sv")
            p.wait("vector", m4)
            m5 = p.do("vector", lambda v: v.tensor_scalar(
                out=prd[:], in0=cnt[:], scalar1=512.0, scalar2=None, op0=ALU.is_ge),
                inc="sv")
            p.wait("vector", m5)
            m6 = p.do("vector", lambda v: v.select(lo[:], prd[:], mid[:], lo[:]), inc="sv")
            m7 = p.do("vector", lambda v: v.select(hi[:], prd[:], hi[:], mid[:]), inc="sv")
            p.wait("vector", m7)
        f1 = p.do("vector", lambda v: v.tensor_tensor(
            out=gt[:], in0=pT[:], in1=lo[:, 0:1].to_broadcast([16, T]), op=ALU.is_gt),
            inc="sv")
        p.wait("vector", f1)
        f1b = p.do("vector", lambda v: v.tensor_copy(out=msku[:], in_=gt[:]), inc="sv")
        f2 = p.do("vector", lambda v: v.tensor_tensor_scan(
            out=rnk[:], data0=on1[:], data1=gt[:], initial=0.0,
            op0=ALU.mult, op1=ALU.add), inc="sv")
        p.wait("vector", f2)
        f3 = p.do("vector", lambda v: v.tensor_scalar(
            out=rnk[:], in0=rnk[:], scalar1=-1.0, scalar2=None, op0=ALU.add), inc="sv")
        p.wait("vector", f3)
        p.wait("vector", f1b)
        f4 = p.do("vector", lambda v: v.select(slot[:], msku[:], rnk[:],
                                               big[:, 0:1].to_broadcast([16, T])), inc="sv")
        p.wait("vector", f4)
        f5 = p.do("vector", lambda v: v.tensor_scalar(
            out=slot[:], in0=slot[:], scalar1=512.0, scalar2=None, op0=ALU.min), inc="sv")
        p.wait("tensor", f5)
        sT = p.sb("t3_sT", [128, 16], F32)
        prev_cv = None
        for chk in range(NTT):
            t0 = chk * 128
            p.wait("tensor", prev_cv)
            mm = p.do("tensor", lambda t, t0=t0: t.matmul(
                ps_T[:], slot[:, t0:t0 + 128], idf16[:], start=True, stop=True), inc="st")
            p.wait("vector", mm)
            cv = p.do("vector", lambda v: v.tensor_copy(out=sT[:], in_=ps_T[:]), inc="sv")
            prev_cv = cv
            p.wait("sync", cv)
            sd = p.dma("sync", c.dram["l3_slot"][t0:t0 + 128, :], sT[:])
            p.wait("vector", sd)


def _l3_gather(c, ident, b, e):
    nc = c.nc
    from concourse.bass_core_rust import IndirectOffsetOnAxis
    with Ph(c, name=_phname()) as p:
        p._ident = ident
        slt = p.sb("g3_slt", [128, 16], F32)
        prt = p.sb("g3_prt", [128, 8], F32)
        slu = p.sb("g3_slu", [128, 1], U32)
        xnt_ = p.sb("g3_xn", [128, D], BF16)
        gcol = p.sb("g3_g", [128, 1], F32)
        tid = p.sb("g3_tid", [128, 1], U32)
        col = 8 * b + e
        for tt in range(NTT):
            r0 = tt * 128
            p.fence_all("sync")
            p.fence_all("gpsimd")
            l1 = p.dma("sync", slt[:], c.dram["l3_slot"][r0:r0 + 128, :])
            l2 = p.dma("sync", xnt_[:], c.dram["xn_tok"][b, r0:r0 + 128, :])
            l3 = p.dma("sync", prt[:], c.dram["l3_p"][b, r0:r0 + 128, :])
            p.wait("vector", l1)
            p.wait("vector", l3)
            cu = p.do("vector", lambda v: v.tensor_copy(out=slu[:], in_=slt[:, col:col + 1]),
                      inc="sv")
            p.wait("vector", cu)
            cg = p.do("vector", lambda v: v.tensor_copy(out=gcol[:], in_=prt[:, e:e + 1]),
                      inc="sv")
            ti = p.do("gpsimd", lambda g, r0=r0, e=e: g.iota(
                tid[:], pattern=[[0, 1]], base=r0 + e * T, channel_multiplier=1), inc="sg")
            p.wait("gpsimd", cu)
            p.wait("gpsimd", cg)
            p.wait("gpsimd", l2)
            s1 = p.do("gpsimd", lambda g: g.indirect_dma_start(
                c.dram["l3_Xg"][:, :], IndirectOffsetOnAxis(slu[:, 0:1], 0),
                xnt_[:], None), inc="dg", n=16)
            s2 = p.do("gpsimd", lambda g: g.indirect_dma_start(
                c.dram["l3_G"][:, :], IndirectOffsetOnAxis(slu[:, 0:1], 0),
                gcol[:], None), inc="dg", n=16)
            p.wait("gpsimd", ti)
            s3 = p.do("gpsimd", lambda g: g.indirect_dma_start(
                c.dram["l3_tokid"][:, :], IndirectOffsetOnAxis(slu[:, 0:1], 0),
                tid[:], None), inc="dg", n=16)
            p.wait("sync", s3)
            p.wait("vector", s3)


def _l3_expert(c, ident, b, e):
    nc = c.nc
    from concourse.bass_core_rust import IndirectOffsetOnAxis
    GC1 = 0.035677408136300523
    GC0 = 0.7978845608028654
    with Ph(c, name=_phname()) as p:
        p._ident = ident
        w1 = p.sb("e3_w1", [128, 8 * 4096], BF16)
        w2h = p.sb("e3_w2", [128, 32 * 1024], BF16)
        wld = p.sb("e3_wld", [128, 2048], F32)
        xg = p.sb("e3_xg", [128, D], BF16)
        xgT = p.sb("e3_xgT", [128, 8 * 512], BF16)
        h1T = p.sb("e3_h1T", [128, 32 * 512], BF16)
        b1c = p.sb("e3_b1c", [128, 1], F32)
        b2r = p.sb("e3_b2r", [1, D], F32)
        gc = p.sb("e3_gc", [128, 1], F32)
        tidu = p.sb("e3_tid", [128, 1], U32)
        xb = p.sb("e3_xb", [128, 512], F32)
        sq = p.sb("e3_sq", [128, 512], F32)
        mm_ = p.sb("e3_mm", [128, 512], F32)
        wout = p.sb("e3_wout", [128, D], F32)
        ones_r = p.sb("e3_ones", [1, 128], F32)
        ps = p.ps("e3_ps", [128, 512], F32)
        ps_t = p.ps("e3_ps_t", [128, 128], BF16)
        ps_d = p.ps("e3_ps_d", [128, 512], F32)
        p.do("vector", lambda v: v.memset(ones_r[:], 1.0))
        for kb in range(8):
            for ch in range(2):
                ld = p.dma("sync", wld[:], c.dram["L3_ffn_W1"][e, kb * 128:(kb + 1) * 128,
                                                               ch * 2048:(ch + 1) * 2048])
                p.wait("vector", ld)
                cv = p.do("vector", lambda v, kb=kb, ch=ch: v.tensor_copy(
                    out=w1[:, kb * 4096 + ch * 2048: kb * 4096 + (ch + 1) * 2048],
                    in_=wld[:]), inc="sv")
                p.wait("sync", cv)
        lb2 = p.dma("sync", b2r[:], c.dram["L3_ffn_b2"][e:e + 1, :])
        p.wait("tensor", cv)
        prev_mm = None
        for tq in range(4):
            r0 = tq * 128
            p.fence_all("gpsimd")
            xl = p.dma("gpsimd", xg[:], c.dram["l3_Xg"][r0:r0 + 128, :])
            p.wait("tensor", xl)
            for k in range(8):
                tr = p.do("tensor", lambda t, k=k: t.transpose(
                    ps_t[:], xg[:, k * 128:(k + 1) * 128], p._ident[:]), inc="st")
                p.wait("vector", tr)
                cv2 = p.do("vector", lambda v, k=k, r0=r0: v.tensor_copy(
                    out=xgT[:, k * 512 + r0: k * 512 + r0 + 128], in_=ps_t[:]), inc="sv")
                p.wait("tensor", cv2)
        p.wait("tensor", cv2)
        prev_h = None
        for nb in range(32):
            lb1 = p.dma("gpsimd", b1c[:],
                        c.dram["L3_ffn_b1"][e:e + 1, nb * 128:(nb + 1) * 128]
                        .rearrange("a b -> b a"), allow_slow_non_contiguous=True)
            p.wait("vector", lb1)
            p.wait("tensor", prev_h)
            mm = None
            for kb in range(8):
                mm = p.do("tensor", lambda t, kb=kb, nb=nb: t.matmul(
                    ps[:], w1[:, kb * 4096 + nb * 128: kb * 4096 + nb * 128 + 128],
                    xgT[:, kb * 512:(kb + 1) * 512],
                    start=(kb == 0), stop=(kb == 7)), inc="st" if kb == 7 else None)
            p.wait("vector", mm)
            h1 = p.do("vector", lambda v: v.tensor_tensor(
                out=xb[:], in0=ps[:], in1=b1c[:, 0:1].to_broadcast([128, 512]),
                op=ALU.add), inc="sv")
            p.wait("scalar", h1)
            h2 = p.do("scalar", lambda s: s.activation(out=sq[:], in_=xb[:], func=AF.Square),
                      inc="ss")
            p.wait("vector", h2)
            h3 = p.do("vector", lambda v: v.tensor_scalar(
                out=mm_[:], in0=sq[:], scalar1=GC1, scalar2=GC0, op0=ALU.mult, op1=ALU.add),
                inc="sv")
            p.wait("vector", h3)
            h4 = p.do("vector", lambda v: v.tensor_tensor(out=mm_[:], in0=mm_[:], in1=xb[:],
                                                          op=ALU.mult), inc="sv")
            p.wait("scalar", h4)
            h5 = p.do("scalar", lambda s: s.activation(out=mm_[:], in_=mm_[:], func=AF.Tanh),
                      inc="ss")
            p.wait("vector", h5)
            h6 = p.do("vector", lambda v: v.tensor_scalar(
                out=mm_[:], in0=mm_[:], scalar1=0.5, scalar2=0.5, op0=ALU.mult, op1=ALU.add),
                inc="sv")
            p.wait("vector", h6)
            h7 = p.do("vector", lambda v, nb=nb: v.tensor_tensor(
                out=h1T[:, nb * 512:(nb + 1) * 512], in0=mm_[:], in1=xb[:], op=ALU.mult),
                inc="sv")
            prev_h = h7
            p.wait("gpsimd", h1)
        p.wait("tensor", prev_h)
        zf = None
        for tt in range(NTT):
            zf = p.dma("sync", c.dram["l3_bufs"][e, tt * 128:(tt + 1) * 128, :],
                       c.dram["l3_zero"][tt * 128:(tt + 1) * 128, :])
        for kb in range(32):
            ld = p.dma("sync", wld[:, 0:1024], c.dram["L3_ffn_W2"][e, kb * 128:(kb + 1) * 128, :])
            p.wait("vector", ld)
            cw = p.do("vector", lambda v, kb=kb: v.tensor_copy(
                out=w2h[:, kb * 1024:(kb + 1) * 1024], in_=wld[:, 0:1024]), inc="sv")
            p.wait("sync", cw)
        p.wait("tensor", cw)
        p.wait("gpsimd", zf)
        p.wait("scalar", lb2)
        for tq in range(4):
            r0 = tq * 128
            p.fence_all("gpsimd")
            lg = p.dma("gpsimd", gc[:], c.dram["l3_G"][r0:r0 + 128, :])
            lt = p.dma("gpsimd", tidu[:], c.dram["l3_tokid"][r0:r0 + 128, :])
            for half in range(2):
                mm = None
                for kb in range(32):
                    mm = p.do("tensor", lambda t, kb=kb, r0=r0, half=half: t.matmul(
                        ps_d[:], h1T[:, kb * 512 + r0: kb * 512 + r0 + 128],
                        w2h[:, kb * 1024 + half * 512: kb * 1024 + (half + 1) * 512],
                        start=(kb == 0), stop=False), inc=None)
                mm = p.do("tensor", lambda t, half=half: t.matmul(
                    ps_d[:], ones_r[0:1, :], b2r[0:1, half * 512:(half + 1) * 512],
                    start=False, stop=True), inc="st")
                p.wait("scalar", mm)
                p.wait("scalar", lg)
                wo_ = p.do("scalar", lambda s, half=half: s.activation(
                    out=wout[:, half * 512:(half + 1) * 512], in_=ps_d[:], func=AF.Identity,
                    scale=gc[:, 0:1]), inc="ss")
                p.wait("tensor", wo_)
            p.wait("gpsimd", wo_)
            p.wait("gpsimd", lt)
            from concourse.bass_core_rust import IndirectOffsetOnAxis as IOA
            sc_ = p.do("gpsimd", lambda g: g.indirect_dma_start(
                c.dram["l3_bufs"].rearrange("e t d -> (e t) d"),
                    IOA(tidu[:, 0:1], 0), wout[:], None), inc="dg", n=16)
            p.wait("scalar", sc_)
            p.wait("gpsimd", sc_)


def _l3_accum(c, ident, b):
    nc = c.nc
    with Ph(c, name=_phname()) as p:
        acc8 = [p.sb(f"c3_acc8_{i}", [128, 8 * D], F32) for i in range(2)]
        toks = [None, None]
        prev = [None, None]
        for tt in range(NTT + 1):
            if tt < NTT:
                r0 = tt * 128
                bk = tt % 2
                buf = acc8[bk]
                if toks[bk] is not None:
                    p.wait("sync", toks[bk])
                    p.wait("gpsimd", toks[bk])
                src = c.dram["l3_bufs"][:, r0:r0 + 128, :]
                l1 = p.dma("sync", buf[:, 0:4 * D].rearrange("p (e d) -> p e d", e=4),
                           src[0:4].rearrange("e p d -> p e d"))
                l2 = p.dma("gpsimd", buf[:, 4 * D:8 * D].rearrange("p (e d) -> p e d", e=4),
                           src[4:8].rearrange("e p d -> p e d"))
                prev[bk] = (l1, l2)
            if tt > 0:
                r0p = (tt - 1) * 128
                bk = (tt - 1) % 2
                bufp = acc8[bk]
                pl1, pl2 = prev[bk]
                p.wait("vector", pl1)
                for e_ in range(1, 4):
                    p.do("vector", lambda v, e_=e_, bufp=bufp: v.tensor_add(
                        bufp[:, 0:D], bufp[:, 0:D], bufp[:, e_ * D:(e_ + 1) * D]))
                p.wait("gpsimd", pl2)
                gp = None
                for e_ in range(5, 8):
                    gp = p.do("gpsimd", lambda g, e_=e_, bufp=bufp: g.tensor_tensor(
                        out=bufp[:, 4 * D:5 * D], in0=bufp[:, 4 * D:5 * D],
                        in1=bufp[:, e_ * D:(e_ + 1) * D], op=ALU.add),
                        inc="sg" if e_ == 7 else None)
                p.wait("vector", gp)
                a = p.do("vector", lambda v, bufp=bufp: v.tensor_add(
                    bufp[:, 0:D], bufp[:, 0:D], bufp[:, 4 * D:5 * D]), inc="sv")
                p.wait("scalar", a)
                sd = p.do("scalar", lambda e, r0p=r0p, bufp=bufp: e.dma_start(
                    out=c.dram["g_buf"][b, r0p:r0p + 128, :], in_=bufp[:, 0:D]),
                    inc="dact", n=16)
                toks[bk] = sd


def ph_combine(c, l):
    with Ph(c, name=_phname()) as p:
        xt = p.sb("cb_x", [128, D], F32)
        st_ = p.sb("cb_s", [128, D], F32)
        gt = p.sb("cb_g", [128, D], F32)
        gs = p.sb("cb_gs", [128, D], F32)
        xsrc = c.dram["L0_attn_x"] if l == 0 else c.dram["xres"]
        for b in range(B_PER_CORE):
            for tt in range(NTT):
                r0 = tt * 128
                p.fence_all("sync")
                p.fence_all("gpsimd")
                l1 = p.dma("sync", xt[:], xsrc[b, r0:r0 + 128, :])
                l2 = p.dma("sync", st_[:], c.dram["s_buf"][b, r0:r0 + 128, :])
                l3 = p.dma("gpsimd", gt[:], c.dram["g_buf"][b, r0:r0 + 128, :])
                p.wait("scalar", l3)
                a1 = p.do("scalar", lambda s: s.activation(out=gs[:], in_=gt[:], func=AF.Silu),
                          inc="ss")
                p.wait("vector", l1)
                p.wait("vector", l2)
                p.wait("vector", a1)
                p.do("vector", lambda v: v.tensor_mul(gs[:], gs[:], st_[:]))
                cv = p.do("vector", lambda v: v.tensor_add(xt[:], xt[:], gs[:]), inc="sv")
                p.wait("sync", cv)
                p.dma("sync", c.dram["xres"][b, r0:r0 + 128, :], xt[:])
                if l == 3:
                    p.dma("sync", c.dram["y"][b, r0:r0 + 128, :], xt[:])



def _host_inputs(inputs, core):
    m = {}
    b0 = core * B_PER_CORE
    m["L0_attn_x"] = np.ascontiguousarray(inputs["L0_attn_x"][b0:b0 + B_PER_CORE])
    pass_through = [
        "L0_attn_w_qkv", "L0_attn_w_o",
        "L0_ffn_w_gate", "L0_ffn_w_up", "L0_ffn_w_down",
        "L1_ffn_Wg", "L1_ffn_Wu", "L1_ffn_Wd", "L1_ffn_sWg", "L1_ffn_sWu", "L1_ffn_sWd",
        "L2_attn_Wq", "L2_attn_Wk", "L2_attn_Wv", "L2_attn_Wo",
        "L2_ffn_fc1_q", "L2_ffn_fc1_scales", "L2_ffn_fc2_q", "L2_ffn_fc2_scales",
        "L1_attn_gate_w", "L1_attn_proj_w",
        "L3_ffn_W1", "L3_ffn_W2", "L3_ffn_b1", "L3_ffn_b2",
    ]
    for s in ("f", "b"):
        pass_through += [f"L1_attn_inproj_{s}", f"L1_attn_conv_w_{s}",
                         f"L1_attn_xproj_{s}", f"L1_attn_dt_w_{s}", f"L1_attn_Alog_{s}",
                         f"L1_attn_outproj_{s}"]
    for k in pass_through:
        m[k] = np.ascontiguousarray(inputs[k])
    row = ["L0_attn_log_scale", "L1_attn_gate_b", "L1_attn_proj_b",
           "L1_ffn_ln_scale", "L1_ffn_ln_bias", "L2_ffn_ln_gamma", "L2_ffn_ln_beta",
           "L2_ffn_fc1_bias", "L2_ffn_fc2_bias", "L3_attn_b_qkv", "L3_attn_b_proj"]
    for s in ("f", "b"):
        row += [f"L1_attn_conv_b_{s}", f"L1_attn_dt_b_{s}", f"L1_attn_Dp_{s}"]
    for k in row:
        m[k] = np.ascontiguousarray(inputs[k].reshape(1, -1))
    m["L1_ffn_gate_wT"] = np.ascontiguousarray(inputs["L1_ffn_gate_w"].T)
    m["L3_attn_w_qkvT"] = np.ascontiguousarray(inputs["L3_attn_w_qkv"].T)
    m["L3_attn_w_projT"] = np.ascontiguousarray(inputs["L3_attn_w_proj"].T)
    m["L3_ffn_WgT"] = np.ascontiguousarray(inputs["L3_ffn_Wg"].T)
    m["c_pos"] = np.arange(T, dtype=np.float32).reshape(1, T)
    gam = np.asarray(RIEMANN, dtype=np.float32)
    invfreq = (1.0 / (gam / gam[0])).astype(np.float32)
    m["c_invfreq"] = np.tile(invfreq, 16).reshape(1, 512)
    return m


def kernel(**inputs):
    nc = build_program()
    in_maps = [_host_inputs(inputs, core) for core in range(8)]
    res = run_bass_kernel_spmd(nc, in_maps, core_ids=list(range(8)))
    out = np.concatenate([res.results[i]["y"] for i in range(8)], axis=0)
    return out.astype(np.float32)
